# version 11
# baseline (speedup 1.0000x reference)
import numpy as np
import ml_dtypes
import concourse.bass as bass
import concourse.mybir as mybir
from concourse.bass_utils import run_bass_kernel_spmd

F32 = mybir.dt.float32
BF16 = mybir.dt.bfloat16
AF = mybir.ActivationFunctionType
ALU = mybir.AluOpType
AX = mybir.AxisListType
bf16 = ml_dtypes.bfloat16

B, S, D = 2, 8192, 2048
HD = 128
DFF = 8192
NT = S // 128
SCALE = HD ** -0.5
EPS = 1e-5

SG = 16
SGT = 4


class Sem:
    def __init__(self, handle, name):
        self.h = handle
        self.name = name
        self.v = 0

    def bump(self, n):
        self.v += n
        return self.v


class Em:

    def __init__(self, role, engine):
        self.role = role
        self.eng = engine
        self.waited = {}

    def _wait(self, s, v):
        if v <= 0 or self.waited.get(s.name, -1) >= v:
            return
        self.eng.wait_ge(s.h, v)
        self.waited[s.name] = v

    def ins(self, role, fn, waits=(), inc=None, inc_n=1):
        if role == self.role:
            for (s, v) in waits:
                self._wait(s, v)
            inst = fn(self.eng)
            if inc is not None:
                inst.then_inc(inc.h, inc_n)
        if inc is not None:
            return inc.bump(inc_n)
        return None

    def wait(self, role, s, v):
        if role == self.role:
            self._wait(s, v)

    def fence(self, role, s):
        if role == self.role:
            self._wait(s, s.v)


def build_program(debug=False):
    nc = bass.Bass("TRN2", target_bir_lowering=False, debug=False, num_devices=8)

    def inp(name, shape, dt=F32):
        return nc.dram_tensor(name, shape, dt, kind="ExternalInput").ap()

    x = inp("x", [S, D])
    xown = inp("xown", [2048, D])
    wq_g = inp("wq_g", [D, 512])
    wk_h = inp("wk_h", [D, HD])
    wv_h = inp("wv_h", [D, HD])
    wq_full = inp("wq_full", [D, D])
    wv_full = inp("wv_full", [D, 512])
    wg_n = inp("wg_n", [D, 1])
    wd_n = inp("wd_n", [512, 1])
    wo_r = inp("wo_r", [512, D])
    w1 = inp("w1", [DFF, D])
    w2 = inp("w2", [D, DFF])
    b1r = inp("b1r", [64, 128])
    b2row = inp("b2row", [1, D])

    out = nc.dram_tensor("out", [2048, D], F32, kind="ExternalOutput").ap()

    dbg_kind = "ExternalOutput" if debug else "Internal"

    def scr(name, shape, dt):
        return nc.dram_tensor(name, shape, dt, kind=dbg_kind).ap()

    lnS = scr("lnS", [S, D], BF16)
    attnP = nc.dram_tensor("attnP", [S, D], F32, kind="Internal").ap()
    attnR = nc.dram_tensor("attnR", [2048, D], F32, kind="Internal").ap()
    xmS = scr("xmS", [2048, D], F32)
    ln3S = scr("ln3S", [2048, D], BF16)
    w1qS = scr("w1qS", [DFF, D], BF16)
    w2qS = scr("w2qS", [D, DFF], BF16)
    w1qT = scr("w1qT", [D, DFF], BF16)
    w2qT = scr("w2qT", [DFF, D], BF16)
    wqbfS = nc.dram_tensor("wqbfS", [D, D], BF16, kind="Internal").ap()
    wvbfS = nc.dram_tensor("wvbfS", [D, 512], BF16, kind="Internal").ap()
    if debug:
        d_kT = nc.dram_tensor("d_kT", [128, S], BF16, kind="ExternalOutput").ap()
        d_v = nc.dram_tensor("d_v", [128, S], BF16, kind="ExternalOutput").ap()
        d_qT = nc.dram_tensor("d_qT", [128, 4 * S], BF16, kind="ExternalOutput").ap()
        d_mb = nc.dram_tensor("d_mb", [128, 64], F32, kind="ExternalOutput").ap()
        d_thr = nc.dram_tensor("d_thr", [128, 4], F32, kind="ExternalOutput").ap()
        d_abn = nc.dram_tensor("d_abn", [128, 32], mybir.dt.bfloat16, kind="ExternalOutput").ap()
        d_attnP = nc.dram_tensor("d_attnP", [S, D], F32, kind="ExternalOutput").ap()
        d_attnR = nc.dram_tensor("d_attnR", [2048, D], F32, kind="ExternalOutput").ap()

    identB = nc.inline_tensor(np.eye(128, dtype=bf16), name="identB")
    identF = nc.inline_tensor(np.eye(64, dtype=np.float32), name="identF")
    tq = np.arange(128)
    tri_blk = np.where(tq[None, :] >= tq[:, None], 0.0, -1e32).astype(bf16)
    triC = nc.inline_tensor(np.tile(tri_blk, (1, 4)), name="triC")

    def sb(name, shape, dt):
        return nc.alloc_sbuf_tensor(name, shape, dt).ap()

    def sb_at(name, shape, dt, base_of, byte_off=0):
        off = nc.lookup_mloc(base_of.tensor).addr + byte_off
        return nc.alloc_sbuf_tensor_at(name, shape, dt, offset=off).ap()

    identB_s = sb("identB_s", [128, 128], BF16)
    identF_s = sb("identF_s", [64, 64], F32)
    tri_s = sb("tri_s", [128, 512], BF16)
    ones_cb = sb("ones_cb", [128, 1], BF16)
    ones_rb = sb("ones_rb", [1, 128], BF16)
    ones_rf = sb("ones_rf", [1, 128], F32)
    wkh_s = sb("wkh_s", [128, 16 * 128], BF16)
    wvh_s = sb("wvh_s", [128, 16 * 128], BF16)
    wqg_s = sb("wqg_s", [128, 16 * 512], BF16)
    wgn_s = sb("wgn_s", [128, 16], BF16)
    wdn_s = sb("wdn_s", [128, 4], BF16)
    abn_s = sb("abn_s", [128, 32], BF16)
    kT_s = sb("kT_s", [128, S], BF16)
    v_s = sb("v_s", [128, S], BF16)
    qT_s = sb("qT_s", [128, 4 * S], BF16)
    stripes = sb("stripes", [128, 16 * 1024], BF16)
    w1b_s = sb("w1b_s", [128, 2 * 2048], BF16)
    mb_s = sb("mb_s", [128, 64], F32)
    gdc_s = sb("gdc_s", [128, 16], F32)
    xb_s = sb("xb_s", [128, 2 * 2048], F32)
    lnb_s = sb("lnb_s", [128, 2 * 2048], BF16)
    sqd_s = sb("sqd_s", [128, 2048], BF16)
    pT_s = sb("pT_s", [128, 4 * 512], BF16)
    onorm_s = sb("onorm_s", [128, 512], BF16)
    rb_s = sb("rb_s", [128, 512], F32)
    evac_s = sb("evac_s", [128, 512], F32)
    w2qT_s = sb("w2qT_s", [128, 2 * 1024], BF16)
    b1c_s = sb("b1c_s", [128, 64], F32)
    b1r_s = sb("b1r_s", [64, 128], F32)
    stat_s = sb("stat_s", [128, 24], F32)
    absacc_v = sb("absacc_v", [128, 2], F32)
    absacc_p = sb("absacc_p", [128, 2], F32)
    thr_s = sb("thr_s", [128, 4], F32)
    trow_s = sb("trow_s", [1, 4], F32)

    rem = nc.sbuf_bytes_remaining
    assert rem >= 0, f"SBUF over budget: {rem}"

    b2b_s = sb_at("b2b_s", [128, 2048], F32, wkh_s)
    ln3T_s = sb_at("ln3T_s", [128, 16 * 512], BF16, wqg_s)
    out2_s = sb_at("out2_s", [128, 8192], F32, kT_s)
    Hs_s = sb_at("Hs_s", [128, 64 * 512], BF16, qT_s)
    wo_s = sb_at("wo_s", [128, 4 * 2048], BF16, stripes)
    b2r_s = sb_at("b2r_s", [1, 2048], F32, stripes, byte_off=16384)
    qC_s = sb_at("qC_s", [128, 2048], BF16, w1b_s)
    rz_s = sb_at("rz_s", [1, 512], F32, evac_s)
    rzb_s = sb_at("rzb_s", [1, 512], BF16, onorm_s)
    absb_s = sb_at("absb_s", [128, 2048], BF16, pT_s)
    T4_s = sb_at("T4_s", [128, 2048], F32, stripes, byte_off=16384)
    T4b_s = sb_at("T4b_s", [128, 2048], BF16, stripes, byte_off=24576)
    assert nc.lookup_mloc(v_s.tensor).addr == nc.lookup_mloc(kT_s.tensor).addr + S * 2
    assert nc.lookup_mloc(wvh_s.tensor).addr == nc.lookup_mloc(wkh_s.tensor).addr + 16 * 128 * 2

    PS = [nc.alloc_psum_tensor(f"ps{i}", [128, 512], F32).ap() for i in range(7)]
    PS7 = nc.alloc_psum_tensor("ps7", [128, 512], F32).ap()

    qT_r = qT_s.rearrange("p (c u) -> p c u", c=4)

    with (
        nc.semaphore("LQ") as hLQ,
        nc.semaphore("LS") as hLS,
        nc.semaphore("LG") as hLG,
        nc.semaphore("CT") as hCT,
        nc.semaphore("CV") as hCV,
        nc.semaphore("CA") as hCA,
        nc.semaphore("CP") as hCP,
        nc.semaphore("CC0") as hCC0,
        nc.semaphore("CC1") as hCC1,
        nc.semaphore("CC2") as hCC2,
        nc.semaphore("CC3") as hCC3,
        nc.Block() as blk,
    ):
        handles = dict(LQ=hLQ, LS=hLS, LG=hLG, CT=hCT, CV=hCV, CA=hCA,
                       CP=hCP, CC0=hCC0, CC1=hCC1, CC2=hCC2, CC3=hCC3)

        def program(em):
            sm = {k: Sem(h, k) for k, h in handles.items()}
            LQ, LS, LG = sm["LQ"], sm["LS"], sm["LG"]
            CT, CV, CA, CP = sm["CT"], sm["CV"], sm["CA"], sm["CP"]
            CCg = [sm["CC0"], sm["CC1"], sm["CC2"], sm["CC3"]]

            st = {}

            em.ins('vector', lambda e: e.memset(ones_cb[:], 1.0), inc=CV)
            em.ins('vector', lambda e: e.memset(ones_rb[:], 1.0), inc=CV)
            em.ins('vector', lambda e: e.memset(ones_rf[:], 1.0), inc=CV)
            em.ins('vector', lambda e: e.memset(stat_s[:, 11:12], 1.0), inc=CV)
            v_ones = CV.v

            em.ins('sync', lambda e: e.dma_start(out=identB_s[:], in_=identB.ap()), inc=LQ, inc_n=16)
            em.ins('sync', lambda e: e.dma_start(out=identF_s[:], in_=identF.ap()), inc=LQ, inc_n=16)
            em.ins('sync', lambda e: e.dma_start(out=tri_s[:], in_=triC.ap()), inc=LQ, inc_n=16)
            em.ins('sync', lambda e: e.dma_start(out=b1r_s[:], in_=b1r), inc=LQ, inc_n=16)
            lq_consts = LQ.v

            em.ins('gpsimd', lambda e: e.dma_start(
                out=wkh_s.rearrange("p (dc c) -> p dc c", dc=16), in_=wk_h.rearrange("(dc p) c -> p dc c", p=128)), inc=LG, inc_n=16)
            em.ins('gpsimd', lambda e: e.dma_start(
                out=wvh_s.rearrange("p (dc c) -> p dc c", dc=16), in_=wv_h.rearrange("(dc p) c -> p dc c", p=128)), inc=LG, inc_n=16)
            em.ins('gpsimd', lambda e: e.dma_start(
                out=wqg_s.rearrange("p (dc c) -> p dc c", dc=16), in_=wq_g.rearrange("(dc p) c -> p dc c", p=128)), inc=LG, inc_n=16)
            em.ins('gpsimd', lambda e: e.dma_start(
                out=wgn_s.rearrange("p (cc o) -> p cc o", cc=16),
                in_=wg_n.rearrange("(cc p) o -> p cc o", p=128),
                allow_slow_non_contiguous=True), inc=LG, inc_n=16)
            em.ins('gpsimd', lambda e: e.dma_start(
                out=wdn_s.rearrange("p (cc o) -> p cc o", cc=4),
                in_=wd_n.rearrange("(cc p) o -> p cc o", p=128),
                allow_slow_non_contiguous=True), inc=LG, inc_n=16)
            em.ins('gpsimd', lambda e: e.dma_start(out=wqbfS, in_=wq_full),
                   inc=LG, inc_n=16)
            em.ins('gpsimd', lambda e: e.dma_start(out=wvbfS, in_=wv_full),
                   inc=LG, inc_n=16)
            lg_p0w = LG.v

            ctb1 = em.ins('tensor', lambda e: e.matmul(
                PS[6][0:128, 0:64], b1r_s[:], identF_s[:],
                start=True, stop=True),
                waits=[(LQ, lq_consts)], inc=CT)
            cv_b1 = em.ins('vector', lambda e: e.tensor_copy(
                b1c_s[:], PS[6][0:128, 0:64]), waits=[(CT, ctb1)], inc=CV)

            ct_abp = 0
            cv_abn_evac = 0
            for dh in range(2):
                lsx = 0
                for cc in range(16):
                    lsx = em.ins('scalar', lambda e: e.dma_start(
                        out=stripes[:, cc * 1024:(cc + 1) * 1024],
                        in_=wqbfS[dh * 1024:(dh + 1) * 1024, cc * 128:(cc + 1) * 128],
                        transpose=True),
                        waits=[(LG, lg_p0w), (CT, ct_abp)], inc=LS, inc_n=16)
                for d8 in range(8):
                    dc = dh * 8 + d8
                    for cc in range(16):
                        w5 = [(LS, lsx)]
                        if cc == 0:
                            w5.append((CV, cv_abn_evac))
                        em.ins('tensor', lambda e: e.matmul(
                            PS[5][:, 0:1],
                            stripes[:, cc * 1024 + d8 * 128:cc * 1024 + (d8 + 1) * 128],
                            wgn_s[:, cc:cc + 1], start=(cc == 0), stop=(cc == 15)),
                            waits=w5, inc=CT)
                    cv_abn_evac = em.ins('vector', lambda e: e.tensor_copy(
                        abn_s[:, dc * 2:dc * 2 + 1], PS[5][:, 0:1]),
                        waits=[(CT, CT.v)], inc=CV)
                ct_abp = CT.v
            lsx = 0
            for vc in range(4):
                lsx = em.ins('scalar', lambda e: e.dma_start(
                    out=stripes[:, vc * 2048:(vc + 1) * 2048],
                    in_=wvbfS[:, vc * 128:(vc + 1) * 128], transpose=True),
                    waits=[(LG, lg_p0w), (CT, ct_abp)], inc=LS, inc_n=16)
            for dc in range(16):
                for vc in range(4):
                    w5 = [(LS, lsx)]
                    if vc == 0:
                        w5.append((CV, cv_abn_evac))
                    em.ins('tensor', lambda e: e.matmul(
                        PS[5][:, 0:1],
                        stripes[:, vc * 2048 + dc * 128:vc * 2048 + (dc + 1) * 128],
                        wdn_s[:, vc:vc + 1], start=(vc == 0), stop=(vc == 3)),
                        waits=w5, inc=CT)
                cv_abn_evac = em.ins('vector', lambda e: e.tensor_copy(
                    abn_s[:, dc * 2 + 1:dc * 2 + 2], PS[5][:, 0:1]),
                    waits=[(CT, CT.v)], inc=CV)
            ct_abp = CT.v
            cv_ab = CV.v
            ct_ab_final = CT.v
            cv_abn_done = CV.v

            xb_free = [0, 0]
            sumsq_free = 0

            def emit_ln_tile(src_ap, dst_ap, xpar, lpar, load_waits, norm_waits):
                nonlocal sumsq_free
                lq1 = em.ins('sync', lambda e: e.dma_start(
                    out=xb_s[:, xpar * 2048:(xpar + 1) * 2048], in_=src_ap),
                    waits=list(load_waits) + [(CA, xb_free[xpar])], inc=LQ, inc_n=16)
                em.ins('vector', lambda e: e.reduce_sum(
                    stat_s[:, 3:4], xb_s[:, xpar * 2048:(xpar + 1) * 2048], axis=AX.X),
                    waits=[(LQ, lq1)], inc=CV)
                em.fence('vector', CV)
                em.ins('vector', lambda e: e.tensor_scalar(
                    stat_s[:, 0:1], stat_s[:, 3:4], 1.0 / 2048, None, ALU.mult), inc=CV)
                caq = em.ins('scalar', lambda e: e.activation(
                    sqd_s[:], xb_s[:, xpar * 2048:(xpar + 1) * 2048], AF.Square,
                    accum_out=stat_s[:, 4:5]),
                    waits=[(LQ, lq1), (CV, sumsq_free)], inc=CA)
                cv1 = em.ins('vector', lambda e: e.tensor_scalar(
                    stat_s[:, 6:7], stat_s[:, 4:5], 1.0 / 2048, EPS, ALU.mult, ALU.add),
                    waits=[(CA, caq)], inc=CV)
                sumsq_free = cv1
                em.ins('vector', lambda e: e.tensor_tensor(
                    stat_s[:, 7:8], stat_s[:, 0:1], stat_s[:, 0:1], ALU.mult), inc=CV)
                em.fence('vector', CV)
                cv3 = em.ins('vector', lambda e: e.tensor_tensor(
                    stat_s[:, 6:7], stat_s[:, 6:7], stat_s[:, 7:8], ALU.subtract), inc=CV)
                em.ins('scalar', lambda e: e.activation(
                    stat_s[:, 8:9], stat_s[:, 6:7], AF.Ln), waits=[(CV, cv3)], inc=CA)
                ca2 = em.ins('scalar', lambda e: e.activation(
                    stat_s[:, 1:2], stat_s[:, 8:9], AF.Exp, scale=-0.5), inc=CA)
                cv4 = em.ins('vector', lambda e: e.tensor_scalar(
                    stat_s[:, 2:3], stat_s[:, 0:1], stat_s[:, 1:2], -1.0,
                    ALU.mult, ALU.mult), waits=[(CA, ca2)], inc=CV)
                can = em.ins('scalar', lambda e: e.activation(
                    lnb_s[:, lpar * 2048:(lpar + 1) * 2048],
                    xb_s[:, xpar * 2048:(xpar + 1) * 2048], AF.Identity,
                    bias=stat_s[:, 2:3], scale=stat_s[:, 1:2]),
                    waits=[(CV, cv4)] + list(norm_waits), inc=CA)
                xb_free[xpar] = can
                lqs = em.ins('sync', lambda e: e.dma_start(
                    out=dst_ap, in_=lnb_s[:, lpar * 2048:(lpar + 1) * 2048]),
                    waits=[(CA, can)], inc=LQ, inc_n=16)
                return can, lqs

            abs_chunks = []
            for r in range(64):
                for c in range(2):
                    abs_chunks.append((w1[r * 128:(r + 1) * 128, c * 1024:(c + 1) * 1024], 0))
            for r in range(16):
                for c in range(8):
                    abs_chunks.append((w2[r * 128:(r + 1) * 128, c * 1024:(c + 1) * 1024], 1))
            abs_state = dict(i=0, sv=[False, False], sp=[False, False],
                             vfree=0, pfree=0, cvd=0, cpd=0)

            def emit_abs_chunk():
                a = abs_state
                if a['i'] >= len(abs_chunks):
                    return
                src, m = abs_chunks[a['i']]
                par = a['i'] % 2
                prevfree = [(CV, a['vfree'])] if par == 0 else [(CV, a['pfree'])]
                lg1 = em.ins('gpsimd', lambda e: e.dma_start(
                    out=absb_s[:, par * 1024:(par + 1) * 1024], in_=src),
                    waits=prevfree, inc=LG, inc_n=16)
                if par == 0:
                    em.ins('vector', lambda e: e.reduce_sum(
                        stat_s[:, 9:10], absb_s[:, 0:1024],
                        axis=AX.X, apply_absolute_value=True),
                        waits=[(LG, lg1)], inc=CV)
                    em.fence('vector', CV)
                    started = a['sv']
                    acc, scol = absacc_v, 9
                else:
                    em.ins('scalar', lambda e: e.activation(
                        sqd_s[:, 0:1024], absb_s[:, 1024:2048], AF.Abs,
                        accum_out=stat_s[:, 10:11]),
                        waits=[(LG, lg1), (CV, a['pfree'])], inc=CA)
                    em.wait('vector', CA, CA.v)
                    started = a['sp']
                    acc, scol = absacc_p, 10
                if not started[m]:
                    c2 = em.ins('vector', lambda e: e.tensor_copy(
                        acc[:, m:m + 1], stat_s[:, scol:scol + 1]), inc=CV)
                    started[m] = True
                else:
                    c2 = em.ins('vector', lambda e: e.tensor_tensor(
                        acc[:, m:m + 1], acc[:, m:m + 1], stat_s[:, scol:scol + 1],
                        ALU.add), inc=CV)
                if par == 0:
                    a['vfree'] = c2
                    a['cvd'] = c2
                else:
                    a['pfree'] = c2
                    a['cvd'] = c2
                a['i'] += 1

            lnb_lq = [0, 0]
            ca_mb_prev = 0
            cv_evK = 0
            cv_evV = 0
            cv_evG = cv_ab
            cv_evQ = 0
            ct_half = [ct_ab_final, ct_ab_final]
            lq_spill = {}
            for sg in range(SG):
                for tt in range(SGT):
                    t = sg * SGT + tt
                    par = t % 2
                    can, lqs = emit_ln_tile(
                        x[t * 128:(t + 1) * 128, :], lnS[t * 128:(t + 1) * 128, :],
                        par, par,
                        load_waits=[],
                        norm_waits=[(LQ, lnb_lq[par]),
                                    (CT, ct_ab_final if t < 2 else 0)])
                    lnb_lq[par] = lqs
                    for _ in range(4):
                        emit_abs_chunk()
                lq_spill[sg] = LQ.v

                half = sg % 2
                hb = half * 8192

                ls_str = 0
                for dc in range(16):
                    ls_str = em.ins('scalar', lambda e: e.dma_start(
                        out=stripes[:, hb + dc * 512:hb + (dc + 1) * 512],
                        in_=lnS[sg * 512:(sg + 1) * 512, dc * 128:(dc + 1) * 128],
                        transpose=True),
                        waits=[(LQ, lq_spill[sg]), (CT, ct_half[half])],
                        inc=LS, inc_n=16)

                def stripe(dc, lo, n):
                    return stripes[:, hb + dc * 512 + lo:hb + dc * 512 + lo + n]

                for dc in range(16):
                    em.ins('tensor', lambda e: e.matmul(
                        PS[0][:], wkh_s[:, dc * 128:(dc + 1) * 128], stripe(dc, 0, 512),
                        start=(dc == 0), stop=(dc == 15)),
                        waits=[(LS, ls_str), (CV, cv_evK), (LG, lg_p0w)], inc=CT)
                cv_evK = em.ins('vector', lambda e: e.tensor_copy(
                    kT_s[:, sg * 512:(sg + 1) * 512], PS[0][:]),
                    waits=[(CT, CT.v)], inc=CV)

                for tt in range(SGT):
                    for dc in range(16):
                        em.ins('tensor', lambda e: e.matmul(
                            PS[1][:, 0:128], stripe(dc, tt * 128, 128),
                            wvh_s[:, dc * 128:(dc + 1) * 128],
                            start=(dc == 0), stop=(dc == 15)),
                            waits=[(CV, cv_evV)], inc=CT)
                    cv_evV = em.ins('vector', lambda e: e.tensor_copy(
                        v_s[:, (sg * SGT + tt) * 128:(sg * SGT + tt + 1) * 128],
                        PS[1][:, 0:128]), waits=[(CT, CT.v)], inc=CV)

                for tt in range(SGT):
                    for dc in range(16):
                        em.ins('tensor', lambda e: e.matmul(
                            PS[3][:, 0:2], stripe(dc, tt * 128, 128),
                            abn_s[:, dc * 2:dc * 2 + 2],
                            start=(dc == 0), stop=(dc == 15)),
                            waits=[(CV, cv_evG)], inc=CT)
                    em.ins('vector', lambda e: e.tensor_copy(
                        gdc_s[:, tt:tt + 1], PS[3][:, 0:1]),
                        waits=[(CT, CT.v), (CA, ca_mb_prev)], inc=CV)
                    cv_evG = em.ins('vector', lambda e: e.tensor_copy(
                        gdc_s[:, 8 + tt:9 + tt], PS[3][:, 1:2]), inc=CV)

                ca_t = em.ins('scalar', lambda e: e.activation(
                    stat_s[:, 12:16], gdc_s[:, 0:4], AF.Tanh, scale=0.5),
                    waits=[(CV, cv_evG)], inc=CA)
                em.ins('vector', lambda e: e.tensor_scalar(
                    stat_s[:, 12:16], stat_s[:, 12:16], 0.5, 0.5, ALU.mult, ALU.add),
                    waits=[(CA, ca_t)], inc=CV)
                em.fence('vector', CV)
                cv6 = em.ins('vector', lambda e: e.tensor_tensor(
                    stat_s[:, 16:20], stat_s[:, 12:16], gdc_s[:, 8:12], ALU.mult),
                    waits=[(CA, ca_mb_prev)], inc=CV)
                ca_s = em.ins('scalar', lambda e: e.activation(
                    stat_s[:, 20:24], stat_s[:, 16:20], AF.Sign),
                    waits=[(CV, cv6)], inc=CA)
                ca_mb_prev = ca_s
                em.ins('vector', lambda e: e.tensor_scalar(
                    stat_s[:, 20:24], stat_s[:, 20:24], 1.0, 5e29, ALU.subtract, ALU.mult),
                    waits=[(CA, ca_s)], inc=CV)
                em.fence('vector', CV)
                em.ins('vector', lambda e: e.tensor_tensor(
                    mb_s[:, sg * 4:(sg + 1) * 4], stat_s[:, 16:20], stat_s[:, 20:24],
                    ALU.add), inc=CV)

                for cc in range(4):
                    for dc in range(16):
                        em.ins('tensor', lambda e: e.matmul(
                            PS[2][:], wqg_s[:, dc * 512 + cc * 128:dc * 512 + (cc + 1) * 128],
                            stripe(dc, 0, 512), start=(dc == 0), stop=(dc == 15)),
                            waits=[(CV, cv_evQ)], inc=CT)
                    cv_evQ = em.ins('vector', lambda e: e.tensor_copy(
                        qT_s[:, cc * S + sg * 512:cc * S + (sg + 1) * 512], PS[2][:]),
                        waits=[(CT, CT.v)], inc=CV)

                ct_half[half] = CT.v

            while abs_state['i'] < len(abs_chunks):
                emit_abs_chunk()

            cvt = em.ins('vector', lambda e: e.tensor_tensor(
                absacc_v[:, 0:2], absacc_v[:, 0:2], absacc_p[:, 0:2], ALU.add),
                waits=[(CV, abs_state['cvd'])], inc=CV)
            ctt1 = em.ins('tensor', lambda e: e.matmul(
                PS[5][0:1, 0:2], stat_s[:, 11:12], absacc_v[:, 0:2],
                start=True, stop=True), waits=[(CV, cvt)], inc=CT)
            em.ins('vector', lambda e: e.tensor_copy(trow_s[0:1, 0:2], PS[5][0:1, 0:2]),
                   waits=[(CT, ctt1)], inc=CV)
            em.fence('vector', CV)
            cvc = em.ins('vector', lambda e: e.tensor_scalar(
                trow_s[0:1, 2:4], trow_s[0:1, 0:2], 0.7 / (DFF * D), None, ALU.mult),
                inc=CV)
            em.ins('tensor', lambda e: e.matmul(
                PS[5][:, 2:3], ones_rf[0:1, :], trow_s[0:1, 2:3], start=True, stop=True),
                waits=[(CV, cvc)], inc=CT)
            ctb2 = em.ins('tensor', lambda e: e.matmul(
                PS[5][:, 3:4], ones_rf[0:1, :], trow_s[0:1, 3:4], start=True, stop=True),
                inc=CT)
            em.ins('vector', lambda e: e.tensor_copy(thr_s[:, 0:1], PS[5][:, 2:3]),
                   waits=[(CT, ctb2)], inc=CV)
            em.ins('vector', lambda e: e.tensor_scalar(
                thr_s[:, 1:2], PS[5][:, 2:3], -1.0, None, ALU.mult), inc=CV)
            em.ins('vector', lambda e: e.tensor_copy(thr_s[:, 2:3], PS[5][:, 3:4]), inc=CV)
            cv_thr = em.ins('vector', lambda e: e.tensor_scalar(
                thr_s[:, 3:4], PS[5][:, 3:4], -1.0, None, ALU.mult), inc=CV)

            lg_wo = em.ins('gpsimd', lambda e: e.dma_start(
                out=wo_s.rearrange("p (ch c) -> p ch c", ch=4), in_=wo_r.rearrange("(ch p) c -> p ch c", p=128)),
                waits=[(CT, CT.v)], inc=LG, inc_n=16)

            if debug:
                em.ins('sync', lambda e: e.dma_start(out=d_kT, in_=kT_s[:]),
                       waits=[(CV, CV.v)], inc=LQ, inc_n=16)
                em.ins('sync', lambda e: e.dma_start(out=d_v, in_=v_s[:]), inc=LQ, inc_n=16)
                em.ins('sync', lambda e: e.dma_start(out=d_qT, in_=qT_s[:]), inc=LQ, inc_n=16)
                em.ins('sync', lambda e: e.dma_start(out=d_mb, in_=mb_s[:]), inc=LQ, inc_n=16)
                em.ins('sync', lambda e: e.dma_start(out=d_thr, in_=thr_s[:]), inc=LQ, inc_n=16)
                em.ins('sync', lambda e: e.dma_start(out=d_abn, in_=abn_s[:]), inc=LQ, inc_n=16)

            qchunks = []
            for r in range(64):
                qchunks.append((w1[r * 128:(r + 1) * 128, :],
                                w1qS[r * 128:(r + 1) * 128, :], 0))
            for r in range(16):
                for c in range(4):
                    qchunks.append((w2[r * 128:(r + 1) * 128, c * 2048:(c + 1) * 2048],
                                    w2qS[r * 128:(r + 1) * 128, c * 2048:(c + 1) * 2048], 2))
            qs = dict(i=0, free=[(CV, 0), (CP, 0)], outlq=[0, 0], laststore=0,
                      abuf_lq=[0, 0], sq_ca=0, w1done=0, w2done=0)

            def emit_quant_chunk():
                if qs['i'] >= len(qchunks):
                    return
                src, dst, tsel = qchunks[qs['i']]
                par = qs['i'] % 2
                role = 'vector'
                CX = CV
                wf = xb_s[:, par * 2048:(par + 1) * 2048]
                abuf = lnb_s[:, par * 2048:(par + 1) * 2048]
                obuf = sqd_s[:] if par == 0 else qC_s[:]
                lq1 = em.ins('sync', lambda e: e.dma_start(out=wf, in_=src),
                             waits=[qs['free'][par]], inc=LQ, inc_n=16)
                em.ins(role, lambda e: e.tensor_scalar(
                    obuf, wf, thr_s[:, tsel + 1:tsel + 2], 1.0, ALU.is_ge, ALU.subtract),
                    waits=[(LQ, lq1), (CV, cv_thr), (LQ, qs['outlq'][par]),
                           (CA, qs['sq_ca'])], inc=CX)
                em.ins(role, lambda e: e.tensor_scalar(
                    abuf, wf, thr_s[:, tsel:tsel + 1], None, ALU.is_ge),
                    waits=[(LQ, qs['abuf_lq'][par])], inc=CX)
                c3 = em.ins(role, lambda e: e.tensor_tensor(
                    obuf, obuf, abuf, ALU.add), inc=CX)
                qs['free'][par] = (CX, c3)
                lqs = em.ins('sync', lambda e: e.dma_start(out=dst, in_=obuf),
                             waits=[(CX, c3)], inc=LQ, inc_n=16)
                qs['outlq'][par] = lqs
                qs['laststore'] = lqs
                if qs['i'] == 63:
                    qs['w1done'] = lqs
                if qs['i'] == len(qchunks) - 1:
                    qs['w2done'] = lqs
                qs['i'] += 1

            p25_state = dict(lq_xm_prev=0, cv_xm=0)
            p25_done = set()

            def emit_p25_tile(l):
                nonlocal sumsq_free
                p25_done.add(l)
                g = l // 4
                lqa = em.ins('sync', lambda e: e.dma_start(
                    out=xb_s[:, 0:2048], in_=attnR[l * 128:(l + 1) * 128, :]),
                    waits=[(CCg[g], 1), qs['free'][0],
                           (LQ, p25_state['lq_xm_prev'])],
                    inc=LQ, inc_n=16)
                lqb = em.ins('sync', lambda e: e.dma_start(
                    out=xb_s[:, 2048:4096], in_=xown[l * 128:(l + 1) * 128, :]),
                    waits=[qs['free'][1]], inc=LQ, inc_n=16)
                em.ins('vector', lambda e: e.reduce_sum(
                    stat_s[:, 3:4], xb_s[:, 0:2048], axis=AX.X),
                    waits=[(LQ, lqa)], inc=CV)
                em.fence('vector', CV)
                em.ins('vector', lambda e: e.tensor_scalar(
                    stat_s[:, 0:1], stat_s[:, 3:4], 1.0 / 2048, None, ALU.mult), inc=CV)
                caq = em.ins('scalar', lambda e: e.activation(
                    sqd_s[:], xb_s[:, 0:2048], AF.Square, accum_out=stat_s[:, 4:5]),
                    waits=[(LQ, lqa), (LQ, qs['laststore']), (CV, sumsq_free)], inc=CA)
                cv1 = em.ins('vector', lambda e: e.tensor_scalar(
                    stat_s[:, 6:7], stat_s[:, 4:5], 1.0 / 2048, EPS, ALU.mult, ALU.add),
                    waits=[(CA, caq)], inc=CV)
                sumsq_free = cv1
                em.ins('vector', lambda e: e.tensor_tensor(
                    stat_s[:, 7:8], stat_s[:, 0:1], stat_s[:, 0:1], ALU.mult), inc=CV)
                em.fence('vector', CV)
                cv3 = em.ins('vector', lambda e: e.tensor_tensor(
                    stat_s[:, 6:7], stat_s[:, 6:7], stat_s[:, 7:8], ALU.subtract), inc=CV)
                em.ins('scalar', lambda e: e.activation(
                    stat_s[:, 8:9], stat_s[:, 6:7], AF.Ln), waits=[(CV, cv3)], inc=CA)
                ca2 = em.ins('scalar', lambda e: e.activation(
                    stat_s[:, 1:2], stat_s[:, 8:9], AF.Exp, scale=-0.5), inc=CA)
                cv4 = em.ins('vector', lambda e: e.tensor_scalar(
                    stat_s[:, 2:3], stat_s[:, 0:1], stat_s[:, 1:2], -1.0,
                    ALU.mult, ALU.mult), waits=[(CA, ca2)], inc=CV)
                em.fence('vector', CV)
                em.ins('vector', lambda e: e.tensor_scalar(
                    xb_s[:, 0:2048], xb_s[:, 0:2048], stat_s[:, 1:2], stat_s[:, 2:3],
                    ALU.mult, ALU.add), waits=[(CV, cv4)], inc=CV)
                cv_xm = em.ins('vector', lambda e: e.tensor_tensor(
                    xb_s[:, 0:2048], xb_s[:, 0:2048], xb_s[:, 2048:4096], ALU.add),
                    waits=[(LQ, lqb)], inc=CV)
                p25_state['cv_xm'] = cv_xm
                p25_state['lq_xm_prev'] = em.ins('sync', lambda e: e.dma_start(
                    out=xmS[l * 128:(l + 1) * 128, :], in_=xb_s[:, 0:2048]),
                    waits=[(CV, cv_xm)], inc=LQ, inc_n=16)

            bounce_tiles = []
            for f0 in range(8):
                for d0 in range(16):
                    bounce_tiles.append((
                        w1qS[f0 * 1024:(f0 + 1) * 1024, d0 * 128:(d0 + 1) * 128],
                        w1qT[d0 * 128:(d0 + 1) * 128, f0 * 1024:(f0 + 1) * 1024], 0))
            for f0 in range(64):
                for m0 in range(2):
                    bounce_tiles.append((
                        w2qS[m0 * 1024:(m0 + 1) * 1024, f0 * 128:(f0 + 1) * 128],
                        w2qT[f0 * 128:(f0 + 1) * 128, m0 * 1024:(m0 + 1) * 1024], 1))
            bounce_state = dict(i=0, lq=0, plq=[0, 0])

            def emit_bounce_pair():
                bi = bounce_state['i']
                if bi >= len(bounce_tiles):
                    return
                bsrc, bdst, m = bounce_tiles[bi]
                par = bi % 2
                gate = qs['w1done'] if m == 0 else qs['w2done']
                lsb = em.ins('sync', lambda e: e.dma_start(
                    out=w2qT_s[:, par * 1024:(par + 1) * 1024], in_=bsrc,
                    transpose=True),
                    waits=[(LQ, gate), (LQ, bounce_state['plq'][par])],
                    inc=LQ, inc_n=16)
                bounce_state['plq'][par] = em.ins('sync', lambda e: e.dma_start(
                    out=bdst, in_=w2qT_s[:, par * 1024:(par + 1) * 1024]),
                    waits=[(LQ, lsb)], inc=LQ, inc_n=16)
                bounce_state['lq'] = bounce_state['plq'][par]
                bounce_state['i'] = bi + 1

            qs['abuf_lq'] = [lnb_lq[0], lnb_lq[1]]
            qs['sq_ca'] = CA.v
            cv_p1 = CV.v
            em.wait('tensor', CV, cv_p1)
            em.wait('scalar', CV, cv_p1)
            em.wait('scalar', CP, CP.v)

            PSA = [PS[0], PS[1], PS[5], PS[6]]
            psa_free = [0, 0, 0, 0]
            pt_free = [0, 0, 0, 0]
            cv_t4_prev = 0
            ct_t4z_prev = 0
            ct_p1_stripes = CT.v
            cv_onorm_prev = 0
            cv_evac_prev = 0
            cv_evac_pair = [0, 0]
            ca_rb_prev = 0
            ct_B_prev = 0
            ct_wo_prev = 0
            cv_zp_prev = 0
            lq_evac_prev = 0
            ca_last_exp = 0
            for t in range(NT):
                ct_pv_last = 0
                nk = t + 1
                ngrp = (nk + 3) // 4
                exp_of = {}
                exp_last_grp = {-1: ca_last_exp}
                for g in range(ngrp + 1):
                    if g < ngrp:
                        kts = range(4 * g, min(4 * g + 4, nk))
                        first = True
                        for kt in kts:
                            bank = kt % 4
                            wS = [(CA, exp_last_grp[g - 1])] if first else []
                            first = False
                            ctS = em.ins('tensor', lambda e: e.matmul(
                                PSA[bank][:], kT_s[:, kt * 128:(kt + 1) * 128],
                                qT_r[:, :, t * 128:(t + 1) * 128],
                                start=True, stop=True), waits=wS, inc=CT)
                            if kt == t:
                                cvtri = em.ins('vector', lambda e: e.tensor_tensor(
                                    PSA[bank][:], PSA[bank][:], tri_s[:], ALU.add),
                                    waits=[(CT, ctS)], inc=CV)
                                exp_of[kt] = ('cv', cvtri)
                            else:
                                exp_of[kt] = ('ct', ctS)
                    if g >= 1:
                        kts = range(4 * (g - 1), min(4 * (g - 1) + 4, nk))
                        first = True
                        for kt in kts:
                            bank = kt % 4
                            wpv = [(CA, exp_last_grp[g - 1])] if first else []
                            first = False
                            if kt == 0:
                                wpv.append((CV, cv_onorm_prev))
                            ctP = em.ins('tensor', lambda e: e.matmul(
                                PS[2][:], v_s[:, kt * 128:(kt + 1) * 128],
                                pT_s[:, bank * 512:(bank + 1) * 512],
                                start=(kt == 0), stop=(kt == t)), waits=wpv, inc=CT)
                            ct_pv_last = ctP
                            pt_free[bank] = ctP
                        g0 = 4 * (g - 1)
                        w = min(4, nk - g0)
                        if g == 1:
                            cv_t4 = em.ins('vector', lambda e: e.tensor_copy(
                                T4_s[:, 0:w * 512], pT_s[:, 0:w * 512]),
                                waits=[(CA, exp_last_grp[g - 1]), (CT, ct_t4z_prev),
                                       (CT, ct_p1_stripes)], inc=CV)
                            if w < 4:
                                cv_t4 = em.ins('vector', lambda e: e.memset(
                                    T4_s[:, w * 512:2048], 0.0), inc=CV)
                        else:
                            cv_t4 = em.ins('vector', lambda e: e.tensor_tensor(
                                T4_s[:, 0:w * 512], T4_s[:, 0:w * 512],
                                pT_s[:, 0:w * 512], ALU.add),
                                waits=[(CA, exp_last_grp[g - 1])], inc=CV)
                        cv_t4_prev = cv_t4
                    if g < ngrp:
                        kts = range(4 * g, min(4 * g + 4, nk))
                        for kt in kts:
                            bank = kt % 4
                            kind, val = exp_of[kt]
                            wexp = [(CV, val)] if kind == 'cv' else [(CT, val)]
                            caE = em.ins('scalar', lambda e: e.activation(
                                pT_s[:, bank * 512:(bank + 1) * 512], PSA[bank][:],
                                AF.Exp, bias=mb_s[:, kt:kt + 1], scale=SCALE),
                                waits=wexp + [(CT, pt_free[bank]),
                                              (CV, cv_t4_prev)], inc=CA)
                            ca_last_exp = caE
                        exp_last_grp[g] = ca_last_exp

                cv_t4b = em.ins('vector', lambda e: e.tensor_copy(
                    T4b_s[:], T4_s[:]), waits=[(CT, ct_t4z_prev)], inc=CV)
                nzm = min(4, nk)
                for j in range(nzm):
                    wz = [(CV, cv_t4b)]
                    if j == 0:
                        wz.append((CV, cv_zp_prev))
                    em.ins('tensor', lambda e: e.matmul(
                        PS[3][0:1, :], ones_cb[:], T4b_s[:, j * 512:(j + 1) * 512],
                        start=(j == 0), stop=(j == nzm - 1)), waits=wz, inc=CT)
                ct_t4z_prev = CT.v
                cv_zp = em.ins('vector', lambda e: e.tensor_scalar(
                    rz_s[0:1, :], PS[3][0:1, :], 1e-30, None, ALU.add),
                    waits=[(CT, ct_t4z_prev), (CA, ca_rb_prev), (LQ, lq_evac_prev)],
                    inc=CV)
                cv_zp_prev = cv_zp
                em.fence('vector', CV)
                em.ins('vector', lambda e: e.reciprocal(rz_s[0:1, :], rz_s[0:1, :]),
                       inc=CV)
                em.fence('vector', CV)
                cv_rzb = em.ins('vector', lambda e: e.tensor_copy(
                    rzb_s[0:1, :], rz_s[0:1, :]),
                    waits=[(CT, ct_B_prev), (CT, ct_wo_prev)], inc=CV)
                ctB = em.ins('tensor', lambda e: e.matmul(
                    PS[4][:], ones_rb[0:1, :], rzb_s[0:1, :], start=True, stop=True),
                    waits=[(CV, cv_rzb), (CA, ca_rb_prev), (CV, cv_evac_prev)], inc=CT)
                ct_B_prev = ctB
                ca_rb = em.ins('scalar', lambda e: e.copy(rb_s[:], PS[4][:]),
                               waits=[(CT, ctB), (CV, cv_onorm_prev)], inc=CA)
                ca_rb_prev = ca_rb
                cv_on = em.ins('vector', lambda e: e.tensor_tensor(
                    onorm_s[:], PS[2][:], rb_s[:], ALU.mult),
                    waits=[(CA, ca_rb), (CT, ct_pv_last)], inc=CV)
                cv_onorm_prev = cv_on

                WOB = [PS7, PS[4]]
                for pair in range(2):
                    for hl in range(4):
                        for oc2 in range(2):
                            oc = pair * 2 + oc2
                            w = [(CV, cv_on), (LG, lg_wo)]
                            if hl == 0:
                                w.append((CV, cv_evac_pair[1] if pair == 0
                                          else cv_evac_pair[0]))
                            em.ins('tensor', lambda e: e.matmul(
                                WOB[oc2][:], onorm_s[:, hl * 128:(hl + 1) * 128],
                                wo_s[:, hl * 2048 + oc * 512:hl * 2048 + (oc + 1) * 512],
                                start=(hl == 0), stop=(hl == 3)), waits=w, inc=CT)
                    ct_wo = CT.v
                    ct_wo_prev = ct_wo
                    for oc2 in range(2):
                        oc = pair * 2 + oc2
                        cvs = em.ins('vector', lambda e: e.tensor_copy(
                            evac_s[:], WOB[oc2][:]),
                            waits=[(CT, ct_wo), (LQ, lq_evac_prev)], inc=CV)
                        lq_evac_prev = em.ins('sync', lambda e: e.dma_start(
                            out=attnP[t * 128:(t + 1) * 128,
                                      oc * 512:(oc + 1) * 512],
                            in_=evac_s[:]), waits=[(CV, cvs)], inc=LQ, inc_n=16)
                    cv_evac_pair[pair] = cvs
                cv_evac_prev = cvs


                if t >= 4:
                    for _ in range(4):
                        emit_quant_chunk()
                if 24 <= t < 56:
                    for _ in range(8):
                        emit_bounce_pair()
                if t in (45, 49, 53):
                    for l in range(4 * ((t - 45) // 4), 4 * ((t - 45) // 4) + 4):
                        emit_p25_tile(l)

                if t % 16 == 15:
                    g = t // 16
                    em.ins('gpsimd', lambda e: e.collective_compute(
                        "ReduceScatter", ALU.add,
                        replica_groups=[[0, 1, 2, 3], [4, 5, 6, 7]],
                        ins=[attnP[g * 2048:(g + 1) * 2048, :].opt()],
                        outs=[attnR[g * 512:(g + 1) * 512, :].opt()]),
                        waits=[(LQ, LQ.v)], inc=CCg[g])

            lq_bounce_all = bounce_state['lq']
            for l in range(16):
                if l not in p25_done:
                    emit_p25_tile(l)
            lq_xm_all = p25_state['lq_xm_prev']

            ln3_lq = [0, 0]
            for l in range(16):
                par = l % 2
                can, lqs = emit_ln_tile(
                    xmS[l * 128:(l + 1) * 128, :], ln3S[l * 128:(l + 1) * 128, :],
                    par, par,
                    load_waits=[(LQ, lq_xm_all)],
                    norm_waits=[(LQ, ln3_lq[par]), qs['free'][par]])
                ln3_lq[par] = lqs
            lq_ln3_all = LQ.v

            lqb2 = em.ins('sync', lambda e: e.dma_start(out=b2r_s[0:1, :], in_=b2row),
                          waits=[(CT, CT.v)], inc=LQ, inc_n=16)
            cv_now = CV.v
            ca_now = CA.v
            for ocb in range(4):
                ctb = em.ins('tensor', lambda e: e.matmul(
                    PS[ocb][:], ones_rf[0:1, :], b2r_s[0:1, ocb * 512:(ocb + 1) * 512],
                    start=True, stop=True),
                    waits=[(LQ, lqb2), (CV, cv_now), (CA, ca_now)], inc=CT)
                em.ins('vector', lambda e: e.tensor_copy(
                    b2b_s[:, ocb * 512:(ocb + 1) * 512], PS[ocb][:]),
                    waits=[(CT, ctb)], inc=CV)
            cv_b2b = CV.v

            ct_f1_half = [CT.v, CT.v]
            ct_w2blk = [0, 0]
            ca_hs = [0, 0]
            cv_out2 = cv_b2b
            cv_final = 0
            lq_out_last = 0
            for sg2 in range(4):
                ls_ln3 = 0
                for dc in range(16):
                    ls_ln3 = em.ins('scalar', lambda e: e.dma_start(
                        out=ln3T_s[:, dc * 512:(dc + 1) * 512],
                        in_=ln3S[sg2 * 512:(sg2 + 1) * 512, dc * 128:(dc + 1) * 128],
                        transpose=True),
                        waits=[(LQ, lq_ln3_all), (CT, ct_f1_half[0])],
                        inc=LS, inc_n=16)

                w1qT_r = w1qT.rearrange("(dc p) f -> p dc f", p=128)
                for fbg in range(64):
                    wpar = fbg % 2
                    lss = em.ins('sync', lambda e: e.dma_start(
                        out=w1b_s[:, wpar * 2048:(wpar + 1) * 2048],
                        in_=w1qT_r[:, :, fbg * 128:(fbg + 1) * 128]),
                        waits=[(LQ, lq_bounce_all), (CT, ct_f1_half[wpar])],
                        inc=LQ, inc_n=16)
                    pb = fbg % 2
                    for dc in range(16):
                        w = [(LQ, lss), (LS, ls_ln3)]
                        if dc == 0:
                            w.append((CA, ca_hs[pb]))
                            if fbg < 2 and sg2 > 0:
                                w.append((CV, cv_out2))
                        em.ins('tensor', lambda e: e.matmul(
                            PS[pb][:],
                            w1b_s[:, wpar * 2048 + dc * 128:
                                  wpar * 2048 + (dc + 1) * 128],
                            ln3T_s[:, dc * 512:(dc + 1) * 512],
                            start=(dc == 0), stop=(dc == 15)), waits=w, inc=CT)
                    ct_f1_half[wpar] = CT.v
                    ca_hs[pb] = em.ins('scalar', lambda e: e.activation(
                        Hs_s[:, fbg * 512:(fbg + 1) * 512], PS[pb][:], AF.Gelu,
                        bias=b1c_s[:, fbg:fbg + 1]),
                        waits=[(CT, CT.v), (CV, cv_b1)], inc=CA)
                ca_f1_done = max(ca_hs)

                for dmh in range(2):
                    for fb in range(64):
                        par = fb % 2
                        lsw = em.ins('scalar', lambda e: e.dma_start(
                            out=w2qT_s[:, par * 1024:(par + 1) * 1024],
                            in_=w2qT[fb * 128:(fb + 1) * 128,
                                     dmh * 1024:(dmh + 1) * 1024]),
                            waits=[(LQ, lq_bounce_all), (CT, ct_w2blk[par])],
                            inc=LS, inc_n=16)
                        for ts in range(4):
                            for oc2 in range(2):
                                bi = ts * 2 + oc2
                                bank = PS[bi] if bi < 7 else PS7
                                w = [(LS, lsw), (CA, ca_f1_done)]
                                if fb == 0:
                                    w.append((CV, cv_out2))
                                em.ins('tensor', lambda e: e.matmul(
                                    bank[:],
                                    Hs_s[:, fb * 512 + ts * 128:fb * 512 + (ts + 1) * 128],
                                    w2qT_s[:, par * 1024 + oc2 * 512:
                                           par * 1024 + (oc2 + 1) * 512],
                                    start=(fb == 0), stop=(fb == 63)), waits=w, inc=CT)
                        ct_w2blk[par] = CT.v
                    ct_f2 = CT.v
                    for ts in range(4):
                        for oc2 in range(2):
                            bi = ts * 2 + oc2
                            bank = PS[bi] if bi < 7 else PS7
                            cv_out2 = em.ins('vector', lambda e: e.tensor_tensor(
                                out2_s[:, ts * 2048 + dmh * 1024 + oc2 * 512:
                                       ts * 2048 + dmh * 1024 + (oc2 + 1) * 512],
                                bank[:],
                                b2b_s[:, dmh * 1024 + oc2 * 512:
                                      dmh * 1024 + (oc2 + 1) * 512],
                                ALU.add), waits=[(CT, ct_f2), (CV, cv_b2b),
                                                 (CV, cv_final)], inc=CV)

                for ts in range(4):
                    l = sg2 * 4 + ts
                    base = ts * 2048
                    lqx = em.ins('sync', lambda e: e.dma_start(
                        out=xb_s[:, 0:2048], in_=xmS[l * 128:(l + 1) * 128, :]),
                        waits=[(LQ, lq_xm_all), (CA, xb_free[0]),
                               (CV, cv_final)], inc=LQ, inc_n=16)
                    xb_free[0] = 0
                    em.ins('vector', lambda e: e.reduce_sum(
                        stat_s[:, 3:4], out2_s[:, base:base + 2048], axis=AX.X),
                        waits=[(CV, cv_out2)], inc=CV)
                    em.fence('vector', CV)
                    em.ins('vector', lambda e: e.tensor_scalar(
                        stat_s[:, 0:1], stat_s[:, 3:4], 1.0 / 2048, None, ALU.mult),
                        inc=CV)
                    caq = em.ins('scalar', lambda e: e.activation(
                        sqd_s[:], out2_s[:, base:base + 2048], AF.Square,
                        accum_out=stat_s[:, 4:5]),
                        waits=[(CV, cv_out2), (CV, sumsq_free)], inc=CA)
                    cv1 = em.ins('vector', lambda e: e.tensor_scalar(
                        stat_s[:, 6:7], stat_s[:, 4:5], 1.0 / 2048, EPS,
                        ALU.mult, ALU.add), waits=[(CA, caq)], inc=CV)
                    sumsq_free = cv1
                    em.ins('vector', lambda e: e.tensor_tensor(
                        stat_s[:, 7:8], stat_s[:, 0:1], stat_s[:, 0:1], ALU.mult),
                        inc=CV)
                    em.fence('vector', CV)
                    cv3 = em.ins('vector', lambda e: e.tensor_tensor(
                        stat_s[:, 6:7], stat_s[:, 6:7], stat_s[:, 7:8], ALU.subtract),
                        inc=CV)
                    em.ins('scalar', lambda e: e.activation(
                        stat_s[:, 8:9], stat_s[:, 6:7], AF.Ln), waits=[(CV, cv3)],
                        inc=CA)
                    ca2 = em.ins('scalar', lambda e: e.activation(
                        stat_s[:, 1:2], stat_s[:, 8:9], AF.Exp, scale=-0.5), inc=CA)
                    cv4 = em.ins('vector', lambda e: e.tensor_scalar(
                        stat_s[:, 2:3], stat_s[:, 0:1], stat_s[:, 1:2], -1.0,
                        ALU.mult, ALU.mult), waits=[(CA, ca2)], inc=CV)
                    em.fence('vector', CV)
                    em.ins('vector', lambda e: e.tensor_scalar(
                        out2_s[:, base:base + 2048], out2_s[:, base:base + 2048],
                        stat_s[:, 1:2], stat_s[:, 2:3], ALU.mult, ALU.add),
                        waits=[(CV, cv4)], inc=CV)
                    cv_final = em.ins('vector', lambda e: e.tensor_tensor(
                        out2_s[:, base:base + 2048], out2_s[:, base:base + 2048],
                        xb_s[:, 0:2048], ALU.add), waits=[(LQ, lqx)], inc=CV)
                    lq_out_last = em.ins('sync', lambda e: e.dma_start(
                        out=out[l * 128:(l + 1) * 128, :],
                        in_=out2_s[:, base:base + 2048]),
                        waits=[(CV, cv_final)], inc=LQ, inc_n=16)

            if debug:
                em.ins('gpsimd', lambda e: e.dma_start(out=d_attnP, in_=attnP),
                       waits=[(LQ, lq_out_last)], inc=LG, inc_n=16)
                em.ins('gpsimd', lambda e: e.dma_start(out=d_attnR, in_=attnR),
                       inc=LG, inc_n=16)
            em.wait('sync', LQ, lq_out_last)
            em.wait('gpsimd', LG, LG.v)
            em.wait('scalar', LS, LS.v)

        @blk.sync
        def _(eng):
            program(Em('sync', eng))

        @blk.gpsimd
        def _(eng):
            program(Em('gpsimd', eng))

        @blk.vector
        def _(eng):
            program(Em('vector', eng))

        @blk.scalar
        def _(eng):
            program(Em('scalar', eng))

        @blk.tensor
        def _(eng):
            program(Em('tensor', eng))

    return nc


def make_in_maps(X, Wq, Wk, Wv, Wg, Wd, Wo, w1, b1, w2, b2):
    in_maps = []
    for c in range(8):
        b, n = c // 4, c % 4
        own_rows = np.concatenate(
            [np.arange(g * 2048 + n * 512, g * 2048 + (n + 1) * 512)
             for g in range(4)])
        in_maps.append({
            "x": np.ascontiguousarray(X[b]),
            "xown": np.ascontiguousarray(X[b][own_rows]),
            "wq_g": np.ascontiguousarray(Wq[:, n * 512:(n + 1) * 512]),
            "wk_h": np.ascontiguousarray(Wk[:, n * 128:(n + 1) * 128]),
            "wv_h": np.ascontiguousarray(Wv[:, n * 128:(n + 1) * 128]),
            "wq_full": Wq,
            "wv_full": Wv,
            "wg_n": np.ascontiguousarray(Wg[:, n:n + 1]),
            "wd_n": np.ascontiguousarray(Wd[:, n:n + 1]),
            "wo_r": np.ascontiguousarray(Wo[n * 512:(n + 1) * 512, :]),
            "w1": w1,
            "w2": w2,
            "b1r": np.ascontiguousarray(b1.reshape(64, 128)),
            "b2row": np.ascontiguousarray(b2.reshape(1, 2048)),
        })
    return in_maps


def assemble(results):
    OUT = np.empty((B, S, D), dtype=np.float32)
    for c in range(8):
        b, n = c // 4, c % 4
        o = results[c]["out"]
        for g in range(4):
            OUT[b, g * 2048 + n * 512:g * 2048 + (n + 1) * 512, :] = \
                o[g * 512:(g + 1) * 512, :]
    return OUT


def kernel(L0_attn_hidden_states, L0_attn_Wq, L0_attn_Wk, L0_attn_Wv,
           L0_attn_Wg, L0_attn_Wd, L0_attn_Wo, L0_ffn_w1, L0_ffn_b1,
           L0_ffn_w2, L0_ffn_b2):
    f = lambda a: np.ascontiguousarray(np.asarray(a, dtype=np.float32))
    X = f(L0_attn_hidden_states)
    in_maps = make_in_maps(X, f(L0_attn_Wq), f(L0_attn_Wk), f(L0_attn_Wv),
                           f(L0_attn_Wg), f(L0_attn_Wd), f(L0_attn_Wo),
                           f(L0_ffn_w1), f(L0_ffn_b1), f(L0_ffn_w2),
                           f(L0_ffn_b2))
    nc = build_program(debug=False)
    res = run_bass_kernel_spmd(nc, in_maps, core_ids=list(range(8)))
    return assemble(res.results)


# version 14
# speedup vs baseline: 1.1733x; 1.1733x over previous
import numpy as np
import ml_dtypes
import concourse.bass as bass
import concourse.mybir as mybir
from concourse.bass_utils import run_bass_kernel_spmd

F32 = mybir.dt.float32
BF16 = mybir.dt.bfloat16
AF = mybir.ActivationFunctionType
ALU = mybir.AluOpType
AX = mybir.AxisListType
bf16 = ml_dtypes.bfloat16

B, S, D = 2, 8192, 2048
HD = 128
DFF = 8192
NT = S // 128
SCALE = HD ** -0.5
EPS = 1e-5

SG = 16
SGT = 4


class Sem:
    def __init__(self, handle, name):
        self.h = handle
        self.name = name
        self.v = 0

    def bump(self, n):
        self.v += n
        return self.v


class Em:

    def __init__(self, role, engine):
        self.role = role
        self.eng = engine
        self.waited = {}

    def _wait(self, s, v):
        if v <= 0 or self.waited.get(s.name, -1) >= v:
            return
        self.eng.wait_ge(s.h, v)
        self.waited[s.name] = v

    def ins(self, role, fn, waits=(), inc=None, inc_n=1):
        if role == self.role:
            for (s, v) in waits:
                self._wait(s, v)
            inst = fn(self.eng)
            if inc is not None:
                inst.then_inc(inc.h, inc_n)
        if inc is not None:
            return inc.bump(inc_n)
        return None

    def wait(self, role, s, v):
        if role == self.role:
            self._wait(s, v)

    def fence(self, role, s):
        if role == self.role:
            self._wait(s, s.v)


def build_program(debug=False):
    nc = bass.Bass("TRN2", target_bir_lowering=False, debug=False, num_devices=8)

    def inp(name, shape, dt=F32):
        return nc.dram_tensor(name, shape, dt, kind="ExternalInput").ap()

    x = inp("x", [S, D])
    xown = inp("xown", [2048, D])
    wq_g = inp("wq_g", [D, 512])
    wk_h = inp("wk_h", [D, HD])
    wv_h = inp("wv_h", [D, HD])
    wq_full = inp("wq_full", [D, D])
    wv_full = inp("wv_full", [D, 512])
    wg_n = inp("wg_n", [D, 1])
    wd_n = inp("wd_n", [512, 1])
    wo_r = inp("wo_r", [512, D])
    w1 = inp("w1", [DFF, D])
    w2 = inp("w2", [D, DFF])
    b1r = inp("b1r", [64, 128])
    b2row = inp("b2row", [1, D])

    out = nc.dram_tensor("out", [2048, D], F32, kind="ExternalOutput").ap()

    dbg_kind = "ExternalOutput" if debug else "Internal"

    def scr(name, shape, dt):
        return nc.dram_tensor(name, shape, dt, kind=dbg_kind).ap()

    lnS = scr("lnS", [S, D], BF16)
    attnP = nc.dram_tensor("attnP", [S, D], F32, kind="Internal").ap()
    attnR = nc.dram_tensor("attnR", [2048, D], F32, kind="Internal").ap()
    xmS = scr("xmS", [2048, D], F32)
    ln3S = scr("ln3S", [2048, D], BF16)
    w1qS = scr("w1qS", [DFF, D], BF16)
    w2qS = scr("w2qS", [D, DFF], BF16)
    w1qT = scr("w1qT", [D, DFF], BF16)
    w2qT = scr("w2qT", [DFF, D], BF16)
    wqbfS = nc.dram_tensor("wqbfS", [D, D], BF16, kind="Internal").ap()
    wvbfS = nc.dram_tensor("wvbfS", [D, 512], BF16, kind="Internal").ap()
    if debug:
        d_kT = nc.dram_tensor("d_kT", [128, S], BF16, kind="ExternalOutput").ap()
        d_v = nc.dram_tensor("d_v", [128, S], BF16, kind="ExternalOutput").ap()
        d_qT = nc.dram_tensor("d_qT", [128, 4 * S], BF16, kind="ExternalOutput").ap()
        d_mb = nc.dram_tensor("d_mb", [128, 64], F32, kind="ExternalOutput").ap()
        d_thr = nc.dram_tensor("d_thr", [128, 4], F32, kind="ExternalOutput").ap()
        d_abn = nc.dram_tensor("d_abn", [128, 32], mybir.dt.bfloat16, kind="ExternalOutput").ap()
        d_attnP = nc.dram_tensor("d_attnP", [S, D], F32, kind="ExternalOutput").ap()
        d_attnR = nc.dram_tensor("d_attnR", [2048, D], F32, kind="ExternalOutput").ap()

    identB = nc.inline_tensor(np.eye(128, dtype=bf16), name="identB")
    identF = nc.inline_tensor(np.eye(64, dtype=np.float32), name="identF")
    tq = np.arange(128)
    tri_blk = np.where(tq[None, :] >= tq[:, None], 0.0, -1e32).astype(bf16)
    triC = nc.inline_tensor(np.tile(tri_blk, (1, 4)), name="triC")

    def sb(name, shape, dt):
        return nc.alloc_sbuf_tensor(name, shape, dt).ap()

    def sb_at(name, shape, dt, base_of, byte_off=0):
        off = nc.lookup_mloc(base_of.tensor).addr + byte_off
        return nc.alloc_sbuf_tensor_at(name, shape, dt, offset=off).ap()

    identB_s = sb("identB_s", [128, 128], BF16)
    identF_s = sb("identF_s", [64, 64], F32)
    tri_s = sb("tri_s", [128, 512], BF16)
    ones_cb = sb("ones_cb", [128, 1], BF16)
    ones_rb = sb("ones_rb", [1, 128], BF16)
    ones_rf = sb("ones_rf", [1, 128], F32)
    wkh_s = sb("wkh_s", [128, 16 * 128], BF16)
    wvh_s = sb("wvh_s", [128, 16 * 128], BF16)
    wqg_s = sb("wqg_s", [128, 16 * 512], BF16)
    wgn_s = sb("wgn_s", [128, 16], BF16)
    wdn_s = sb("wdn_s", [128, 4], BF16)
    abn_s = sb("abn_s", [128, 32], BF16)
    kT_s = sb("kT_s", [128, S], BF16)
    v_s = sb("v_s", [128, S], BF16)
    qT_s = sb("qT_s", [128, 4 * S], BF16)
    stripes = sb("stripes", [128, 16 * 1024], BF16)
    w1b_s = sb("w1b_s", [128, 2 * 2048], BF16)
    mb_s = sb("mb_s", [128, 64], F32)
    gdc_s = sb("gdc_s", [128, 16], F32)
    xb_s = sb("xb_s", [128, 2 * 2048], F32)
    lnb_s = sb("lnb_s", [128, 2 * 2048], BF16)
    sqd_s = sb("sqd_s", [128, 2048], BF16)
    pT_s = sb("pT_s", [128, 8 * 512], BF16)
    onorm_s = sb("onorm_s", [128, 512], BF16)
    rb_s = sb("rb_s", [128, 512], F32)
    w2qT_s = sb("w2qT_s", [128, 2 * 1024], BF16)
    b1c_s = sb("b1c_s", [128, 64], F32)
    b1r_s = sb("b1r_s", [64, 128], F32)
    stat_s = sb("stat_s", [128, 24], F32)
    absacc_v = sb("absacc_v", [128, 2], F32)
    absacc_p = sb("absacc_p", [128, 2], F32)
    thr_s = sb("thr_s", [128, 4], F32)
    trow_s = sb("trow_s", [1, 4], F32)

    rem = nc.sbuf_bytes_remaining
    assert rem >= 0, f"SBUF over budget: {rem}"

    b2b_s = sb_at("b2b_s", [128, 2048], F32, wkh_s)
    ln3T_s = sb_at("ln3T_s", [128, 16 * 512], BF16, wqg_s)
    out2_s = sb_at("out2_s", [128, 8192], F32, kT_s)
    Hs_s = sb_at("Hs_s", [128, 64 * 512], BF16, qT_s)
    wo_s = sb_at("wo_s", [128, 4 * 2048], BF16, stripes)
    b2r_s = sb_at("b2r_s", [1, 2048], F32, stripes, byte_off=16384)
    qC_s = sb_at("qC_s", [128, 2048], BF16, w1b_s)
    evac_s = sb_at("evac_s", [128, 512], F32, rb_s)
    rz_s = sb_at("rz_s", [1, 512], F32, rb_s)
    rzb_s = sb_at("rzb_s", [1, 512], BF16, onorm_s)
    absb_s = sb_at("absb_s", [128, 2048], BF16, pT_s)
    T4_s = sb_at("T4_s", [128, 2048], F32, stripes, byte_off=16384)
    T4b_s = sb_at("T4b_s", [128, 2048], BF16, stripes, byte_off=24576)
    assert nc.lookup_mloc(v_s.tensor).addr == nc.lookup_mloc(kT_s.tensor).addr + S * 2
    assert nc.lookup_mloc(wvh_s.tensor).addr == nc.lookup_mloc(wkh_s.tensor).addr + 16 * 128 * 2

    PS = [nc.alloc_psum_tensor(f"ps{i}", [128, 512], F32).ap() for i in range(7)]
    PS7 = nc.alloc_psum_tensor("ps7", [128, 512], F32).ap()

    qT_r = qT_s.rearrange("p (c u) -> p c u", c=4)

    with (
        nc.semaphore("LQ") as hLQ,
        nc.semaphore("LS") as hLS,
        nc.semaphore("LG") as hLG,
        nc.semaphore("CT") as hCT,
        nc.semaphore("CV") as hCV,
        nc.semaphore("CA") as hCA,
        nc.semaphore("CP") as hCP,
        nc.semaphore("CC0") as hCC0,
        nc.semaphore("CC1") as hCC1,
        nc.semaphore("CC2") as hCC2,
        nc.semaphore("CC3") as hCC3,
        nc.Block() as blk,
    ):
        handles = dict(LQ=hLQ, LS=hLS, LG=hLG, CT=hCT, CV=hCV, CA=hCA,
                       CP=hCP, CC0=hCC0, CC1=hCC1, CC2=hCC2, CC3=hCC3)

        def program(em):
            sm = {k: Sem(h, k) for k, h in handles.items()}
            LQ, LS, LG = sm["LQ"], sm["LS"], sm["LG"]
            CT, CV, CA, CP = sm["CT"], sm["CV"], sm["CA"], sm["CP"]
            CCg = [sm["CC0"], sm["CC1"], sm["CC2"], sm["CC3"]]

            st = {}

            em.ins('vector', lambda e: e.memset(ones_cb[:], 1.0), inc=CV)
            em.ins('vector', lambda e: e.memset(ones_rb[:], 1.0), inc=CV)
            em.ins('vector', lambda e: e.memset(ones_rf[:], 1.0), inc=CV)
            em.ins('vector', lambda e: e.memset(stat_s[:, 11:12], 1.0), inc=CV)
            v_ones = CV.v

            em.ins('sync', lambda e: e.dma_start(out=identB_s[:], in_=identB.ap()), inc=LQ, inc_n=16)
            em.ins('sync', lambda e: e.dma_start(out=identF_s[:], in_=identF.ap()), inc=LQ, inc_n=16)
            em.ins('sync', lambda e: e.dma_start(out=tri_s[:], in_=triC.ap()), inc=LQ, inc_n=16)
            em.ins('sync', lambda e: e.dma_start(out=b1r_s[:], in_=b1r), inc=LQ, inc_n=16)
            lq_consts = LQ.v

            em.ins('gpsimd', lambda e: e.dma_start(
                out=wkh_s.rearrange("p (dc c) -> p dc c", dc=16), in_=wk_h.rearrange("(dc p) c -> p dc c", p=128)), inc=LG, inc_n=16)
            em.ins('gpsimd', lambda e: e.dma_start(
                out=wvh_s.rearrange("p (dc c) -> p dc c", dc=16), in_=wv_h.rearrange("(dc p) c -> p dc c", p=128)), inc=LG, inc_n=16)
            em.ins('gpsimd', lambda e: e.dma_start(
                out=wqg_s.rearrange("p (dc c) -> p dc c", dc=16), in_=wq_g.rearrange("(dc p) c -> p dc c", p=128)), inc=LG, inc_n=16)
            em.ins('gpsimd', lambda e: e.dma_start(
                out=wgn_s.rearrange("p (cc o) -> p cc o", cc=16),
                in_=wg_n.rearrange("(cc p) o -> p cc o", p=128),
                allow_slow_non_contiguous=True), inc=LG, inc_n=16)
            em.ins('gpsimd', lambda e: e.dma_start(
                out=wdn_s.rearrange("p (cc o) -> p cc o", cc=4),
                in_=wd_n.rearrange("(cc p) o -> p cc o", p=128),
                allow_slow_non_contiguous=True), inc=LG, inc_n=16)
            em.ins('gpsimd', lambda e: e.dma_start(out=wqbfS, in_=wq_full),
                   inc=LG, inc_n=16)
            em.ins('gpsimd', lambda e: e.dma_start(out=wvbfS, in_=wv_full),
                   inc=LG, inc_n=16)
            lg_p0w = LG.v

            ctb1 = em.ins('tensor', lambda e: e.matmul(
                PS[6][0:128, 0:64], b1r_s[:], identF_s[:],
                start=True, stop=True),
                waits=[(LQ, lq_consts)], inc=CT)
            cv_b1 = em.ins('vector', lambda e: e.tensor_copy(
                b1c_s[:], PS[6][0:128, 0:64]), waits=[(CT, ctb1)], inc=CV)

            ct_abp = 0
            cv_abn_evac = 0
            for dh in range(2):
                lsx = 0
                for cc in range(16):
                    lsx = em.ins('scalar', lambda e: e.dma_start(
                        out=stripes[:, cc * 1024:(cc + 1) * 1024],
                        in_=wqbfS[dh * 1024:(dh + 1) * 1024, cc * 128:(cc + 1) * 128],
                        transpose=True),
                        waits=[(LG, lg_p0w), (CT, ct_abp)], inc=LS, inc_n=16)
                for d8 in range(8):
                    dc = dh * 8 + d8
                    for cc in range(16):
                        w5 = [(LS, lsx)]
                        if cc == 0:
                            w5.append((CV, cv_abn_evac))
                        em.ins('tensor', lambda e: e.matmul(
                            PS[5][:, 0:1],
                            stripes[:, cc * 1024 + d8 * 128:cc * 1024 + (d8 + 1) * 128],
                            wgn_s[:, cc:cc + 1], start=(cc == 0), stop=(cc == 15)),
                            waits=w5, inc=CT)
                    cv_abn_evac = em.ins('vector', lambda e: e.tensor_copy(
                        abn_s[:, dc * 2:dc * 2 + 1], PS[5][:, 0:1]),
                        waits=[(CT, CT.v)], inc=CV)
                ct_abp = CT.v
            lsx = 0
            for vc in range(4):
                lsx = em.ins('scalar', lambda e: e.dma_start(
                    out=stripes[:, vc * 2048:(vc + 1) * 2048],
                    in_=wvbfS[:, vc * 128:(vc + 1) * 128], transpose=True),
                    waits=[(LG, lg_p0w), (CT, ct_abp)], inc=LS, inc_n=16)
            for dc in range(16):
                for vc in range(4):
                    w5 = [(LS, lsx)]
                    if vc == 0:
                        w5.append((CV, cv_abn_evac))
                    em.ins('tensor', lambda e: e.matmul(
                        PS[5][:, 0:1],
                        stripes[:, vc * 2048 + dc * 128:vc * 2048 + (dc + 1) * 128],
                        wdn_s[:, vc:vc + 1], start=(vc == 0), stop=(vc == 3)),
                        waits=w5, inc=CT)
                cv_abn_evac = em.ins('vector', lambda e: e.tensor_copy(
                    abn_s[:, dc * 2 + 1:dc * 2 + 2], PS[5][:, 0:1]),
                    waits=[(CT, CT.v)], inc=CV)
            ct_abp = CT.v
            cv_ab = CV.v
            ct_ab_final = CT.v
            cv_abn_done = CV.v

            xb_free = [0, 0]
            sumsq_free = 0

            def emit_ln_tile(src_ap, dst_ap, xpar, lpar, load_waits, norm_waits):
                nonlocal sumsq_free
                lq1 = em.ins('sync', lambda e: e.dma_start(
                    out=xb_s[:, xpar * 2048:(xpar + 1) * 2048], in_=src_ap),
                    waits=list(load_waits) + [(CA, xb_free[xpar])], inc=LQ, inc_n=16)
                em.ins('vector', lambda e: e.reduce_sum(
                    stat_s[:, 3:4], xb_s[:, xpar * 2048:(xpar + 1) * 2048], axis=AX.X),
                    waits=[(LQ, lq1)], inc=CV)
                em.fence('vector', CV)
                em.ins('vector', lambda e: e.tensor_scalar(
                    stat_s[:, 0:1], stat_s[:, 3:4], 1.0 / 2048, None, ALU.mult), inc=CV)
                caq = em.ins('scalar', lambda e: e.activation(
                    sqd_s[:], xb_s[:, xpar * 2048:(xpar + 1) * 2048], AF.Square,
                    accum_out=stat_s[:, 4:5]),
                    waits=[(LQ, lq1), (CV, sumsq_free)], inc=CA)
                cv1 = em.ins('vector', lambda e: e.tensor_scalar(
                    stat_s[:, 6:7], stat_s[:, 4:5], 1.0 / 2048, EPS, ALU.mult, ALU.add),
                    waits=[(CA, caq)], inc=CV)
                sumsq_free = cv1
                em.ins('vector', lambda e: e.tensor_tensor(
                    stat_s[:, 7:8], stat_s[:, 0:1], stat_s[:, 0:1], ALU.mult), inc=CV)
                em.fence('vector', CV)
                cv3 = em.ins('vector', lambda e: e.tensor_tensor(
                    stat_s[:, 6:7], stat_s[:, 6:7], stat_s[:, 7:8], ALU.subtract), inc=CV)
                em.ins('scalar', lambda e: e.activation(
                    stat_s[:, 8:9], stat_s[:, 6:7], AF.Ln), waits=[(CV, cv3)], inc=CA)
                ca2 = em.ins('scalar', lambda e: e.activation(
                    stat_s[:, 1:2], stat_s[:, 8:9], AF.Exp, scale=-0.5), inc=CA)
                cv4 = em.ins('vector', lambda e: e.tensor_scalar(
                    stat_s[:, 2:3], stat_s[:, 0:1], stat_s[:, 1:2], -1.0,
                    ALU.mult, ALU.mult), waits=[(CA, ca2)], inc=CV)
                can = em.ins('scalar', lambda e: e.activation(
                    lnb_s[:, lpar * 2048:(lpar + 1) * 2048],
                    xb_s[:, xpar * 2048:(xpar + 1) * 2048], AF.Identity,
                    bias=stat_s[:, 2:3], scale=stat_s[:, 1:2]),
                    waits=[(CV, cv4)] + list(norm_waits), inc=CA)
                xb_free[xpar] = can
                lqs = em.ins('sync', lambda e: e.dma_start(
                    out=dst_ap, in_=lnb_s[:, lpar * 2048:(lpar + 1) * 2048]),
                    waits=[(CA, can)], inc=LQ, inc_n=16)
                return can, lqs

            abs_chunks = []
            for r in range(64):
                for c in range(2):
                    abs_chunks.append((w1[r * 128:(r + 1) * 128, c * 1024:(c + 1) * 1024], 0))
            for r in range(16):
                for c in range(8):
                    abs_chunks.append((w2[r * 128:(r + 1) * 128, c * 1024:(c + 1) * 1024], 1))
            abs_state = dict(i=0, sv=[False, False], sp=[False, False],
                             vfree=0, pfree=0, cvd=0, cpd=0)

            def emit_abs_chunk():
                a = abs_state
                if a['i'] >= len(abs_chunks):
                    return
                src, m = abs_chunks[a['i']]
                par = a['i'] % 2
                prevfree = [(CV, a['vfree'])] if par == 0 else [(CV, a['pfree'])]
                lg1 = em.ins('gpsimd', lambda e: e.dma_start(
                    out=absb_s[:, par * 1024:(par + 1) * 1024], in_=src),
                    waits=prevfree, inc=LG, inc_n=16)
                if par == 0:
                    em.ins('vector', lambda e: e.reduce_sum(
                        stat_s[:, 9:10], absb_s[:, 0:1024],
                        axis=AX.X, apply_absolute_value=True),
                        waits=[(LG, lg1)], inc=CV)
                    em.fence('vector', CV)
                    started = a['sv']
                    acc, scol = absacc_v, 9
                else:
                    em.ins('scalar', lambda e: e.activation(
                        sqd_s[:, 0:1024], absb_s[:, 1024:2048], AF.Abs,
                        accum_out=stat_s[:, 10:11]),
                        waits=[(LG, lg1), (CV, a['pfree'])], inc=CA)
                    em.wait('vector', CA, CA.v)
                    started = a['sp']
                    acc, scol = absacc_p, 10
                if not started[m]:
                    c2 = em.ins('vector', lambda e: e.tensor_copy(
                        acc[:, m:m + 1], stat_s[:, scol:scol + 1]), inc=CV)
                    started[m] = True
                else:
                    c2 = em.ins('vector', lambda e: e.tensor_tensor(
                        acc[:, m:m + 1], acc[:, m:m + 1], stat_s[:, scol:scol + 1],
                        ALU.add), inc=CV)
                if par == 0:
                    a['vfree'] = c2
                    a['cvd'] = c2
                else:
                    a['pfree'] = c2
                    a['cvd'] = c2
                a['i'] += 1

            lnb_lq = [0, 0]
            ca_mb_prev = 0
            cv_evK = 0
            cv_evV = 0
            cv_evG = cv_ab
            cv_evQ = 0
            ct_half = [ct_ab_final, ct_ab_final]
            lq_spill = {}
            for sg in range(SG):
                for tt in range(SGT):
                    t = sg * SGT + tt
                    par = t % 2
                    can, lqs = emit_ln_tile(
                        x[t * 128:(t + 1) * 128, :], lnS[t * 128:(t + 1) * 128, :],
                        par, par,
                        load_waits=[],
                        norm_waits=[(LQ, lnb_lq[par]),
                                    (CT, ct_ab_final if t < 2 else 0)])
                    lnb_lq[par] = lqs
                    for _ in range(4):
                        emit_abs_chunk()
                lq_spill[sg] = LQ.v

                half = sg % 2
                hb = half * 8192

                ls_str = 0
                for dc in range(16):
                    ls_str = em.ins('scalar', lambda e: e.dma_start(
                        out=stripes[:, hb + dc * 512:hb + (dc + 1) * 512],
                        in_=lnS[sg * 512:(sg + 1) * 512, dc * 128:(dc + 1) * 128],
                        transpose=True),
                        waits=[(LQ, lq_spill[sg]), (CT, ct_half[half])],
                        inc=LS, inc_n=16)

                def stripe(dc, lo, n):
                    return stripes[:, hb + dc * 512 + lo:hb + dc * 512 + lo + n]

                for dc in range(16):
                    em.ins('tensor', lambda e: e.matmul(
                        PS[0][:], wkh_s[:, dc * 128:(dc + 1) * 128], stripe(dc, 0, 512),
                        start=(dc == 0), stop=(dc == 15)),
                        waits=[(LS, ls_str), (CV, cv_evK), (LG, lg_p0w)], inc=CT)
                cv_evK = em.ins('vector', lambda e: e.tensor_copy(
                    kT_s[:, sg * 512:(sg + 1) * 512], PS[0][:]),
                    waits=[(CT, CT.v)], inc=CV)

                for tt in range(SGT):
                    for dc in range(16):
                        em.ins('tensor', lambda e: e.matmul(
                            PS[1][:, 0:128], stripe(dc, tt * 128, 128),
                            wvh_s[:, dc * 128:(dc + 1) * 128],
                            start=(dc == 0), stop=(dc == 15)),
                            waits=[(CV, cv_evV)], inc=CT)
                    cv_evV = em.ins('vector', lambda e: e.tensor_copy(
                        v_s[:, (sg * SGT + tt) * 128:(sg * SGT + tt + 1) * 128],
                        PS[1][:, 0:128]), waits=[(CT, CT.v)], inc=CV)

                for tt in range(SGT):
                    for dc in range(16):
                        em.ins('tensor', lambda e: e.matmul(
                            PS[3][:, 0:2], stripe(dc, tt * 128, 128),
                            abn_s[:, dc * 2:dc * 2 + 2],
                            start=(dc == 0), stop=(dc == 15)),
                            waits=[(CV, cv_evG)], inc=CT)
                    em.ins('vector', lambda e: e.tensor_copy(
                        gdc_s[:, tt:tt + 1], PS[3][:, 0:1]),
                        waits=[(CT, CT.v), (CA, ca_mb_prev)], inc=CV)
                    cv_evG = em.ins('vector', lambda e: e.tensor_copy(
                        gdc_s[:, 8 + tt:9 + tt], PS[3][:, 1:2]), inc=CV)

                ca_t = em.ins('scalar', lambda e: e.activation(
                    stat_s[:, 12:16], gdc_s[:, 0:4], AF.Tanh, scale=0.5),
                    waits=[(CV, cv_evG)], inc=CA)
                em.ins('vector', lambda e: e.tensor_scalar(
                    stat_s[:, 12:16], stat_s[:, 12:16], 0.5, 0.5, ALU.mult, ALU.add),
                    waits=[(CA, ca_t)], inc=CV)
                em.fence('vector', CV)
                cv6 = em.ins('vector', lambda e: e.tensor_tensor(
                    stat_s[:, 16:20], stat_s[:, 12:16], gdc_s[:, 8:12], ALU.mult),
                    waits=[(CA, ca_mb_prev)], inc=CV)
                ca_s = em.ins('scalar', lambda e: e.activation(
                    stat_s[:, 20:24], stat_s[:, 16:20], AF.Sign),
                    waits=[(CV, cv6)], inc=CA)
                ca_mb_prev = ca_s
                em.ins('vector', lambda e: e.tensor_scalar(
                    stat_s[:, 20:24], stat_s[:, 20:24], 1.0, 5e29, ALU.subtract, ALU.mult),
                    waits=[(CA, ca_s)], inc=CV)
                em.fence('vector', CV)
                em.ins('vector', lambda e: e.tensor_tensor(
                    mb_s[:, sg * 4:(sg + 1) * 4], stat_s[:, 16:20], stat_s[:, 20:24],
                    ALU.add), inc=CV)

                for cc in range(4):
                    for dc in range(16):
                        em.ins('tensor', lambda e: e.matmul(
                            PS[2][:], wqg_s[:, dc * 512 + cc * 128:dc * 512 + (cc + 1) * 128],
                            stripe(dc, 0, 512), start=(dc == 0), stop=(dc == 15)),
                            waits=[(CV, cv_evQ)], inc=CT)
                    cv_evQ = em.ins('vector', lambda e: e.tensor_copy(
                        qT_s[:, cc * S + sg * 512:cc * S + (sg + 1) * 512], PS[2][:]),
                        waits=[(CT, CT.v)], inc=CV)

                ct_half[half] = CT.v

            while abs_state['i'] < len(abs_chunks):
                emit_abs_chunk()

            cvt = em.ins('vector', lambda e: e.tensor_tensor(
                absacc_v[:, 0:2], absacc_v[:, 0:2], absacc_p[:, 0:2], ALU.add),
                waits=[(CV, abs_state['cvd'])], inc=CV)
            ctt1 = em.ins('tensor', lambda e: e.matmul(
                PS[5][0:1, 0:2], stat_s[:, 11:12], absacc_v[:, 0:2],
                start=True, stop=True), waits=[(CV, cvt)], inc=CT)
            em.ins('vector', lambda e: e.tensor_copy(trow_s[0:1, 0:2], PS[5][0:1, 0:2]),
                   waits=[(CT, ctt1)], inc=CV)
            em.fence('vector', CV)
            cvc = em.ins('vector', lambda e: e.tensor_scalar(
                trow_s[0:1, 2:4], trow_s[0:1, 0:2], 0.7 / (DFF * D), None, ALU.mult),
                inc=CV)
            em.ins('tensor', lambda e: e.matmul(
                PS[5][:, 2:3], ones_rf[0:1, :], trow_s[0:1, 2:3], start=True, stop=True),
                waits=[(CV, cvc)], inc=CT)
            ctb2 = em.ins('tensor', lambda e: e.matmul(
                PS[5][:, 3:4], ones_rf[0:1, :], trow_s[0:1, 3:4], start=True, stop=True),
                inc=CT)
            em.ins('vector', lambda e: e.tensor_copy(thr_s[:, 0:1], PS[5][:, 2:3]),
                   waits=[(CT, ctb2)], inc=CV)
            em.ins('vector', lambda e: e.tensor_scalar(
                thr_s[:, 1:2], PS[5][:, 2:3], -1.0, None, ALU.mult), inc=CV)
            em.ins('vector', lambda e: e.tensor_copy(thr_s[:, 2:3], PS[5][:, 3:4]), inc=CV)
            cv_thr = em.ins('vector', lambda e: e.tensor_scalar(
                thr_s[:, 3:4], PS[5][:, 3:4], -1.0, None, ALU.mult), inc=CV)

            lg_wo = em.ins('gpsimd', lambda e: e.dma_start(
                out=wo_s.rearrange("p (ch c) -> p ch c", ch=4), in_=wo_r.rearrange("(ch p) c -> p ch c", p=128)),
                waits=[(CT, CT.v)], inc=LG, inc_n=16)

            if debug:
                em.ins('sync', lambda e: e.dma_start(out=d_kT, in_=kT_s[:]),
                       waits=[(CV, CV.v)], inc=LQ, inc_n=16)
                em.ins('sync', lambda e: e.dma_start(out=d_v, in_=v_s[:]), inc=LQ, inc_n=16)
                em.ins('sync', lambda e: e.dma_start(out=d_qT, in_=qT_s[:]), inc=LQ, inc_n=16)
                em.ins('sync', lambda e: e.dma_start(out=d_mb, in_=mb_s[:]), inc=LQ, inc_n=16)
                em.ins('sync', lambda e: e.dma_start(out=d_thr, in_=thr_s[:]), inc=LQ, inc_n=16)
                em.ins('sync', lambda e: e.dma_start(out=d_abn, in_=abn_s[:]), inc=LQ, inc_n=16)

            qchunks = []
            for r in range(64):
                qchunks.append((w1[r * 128:(r + 1) * 128, :],
                                w1qS[r * 128:(r + 1) * 128, :], 0))
            for r in range(16):
                for c in range(4):
                    qchunks.append((w2[r * 128:(r + 1) * 128, c * 2048:(c + 1) * 2048],
                                    w2qS[r * 128:(r + 1) * 128, c * 2048:(c + 1) * 2048], 2))
            qs = dict(i=0, free=[(CV, 0), (CP, 0)], outlq=[0, 0], laststore=0,
                      abuf_lq=[0, 0], sq_ca=0, w1done=0, w2done=0)

            def emit_quant_chunk():
                if qs['i'] >= len(qchunks):
                    return
                src, dst, tsel = qchunks[qs['i']]
                par = qs['i'] % 2
                role = 'vector'
                CX = CV
                wf = xb_s[:, par * 2048:(par + 1) * 2048]
                abuf = lnb_s[:, par * 2048:(par + 1) * 2048]
                obuf = sqd_s[:] if par == 0 else qC_s[:]
                lq1 = em.ins('sync', lambda e: e.dma_start(out=wf, in_=src),
                             waits=[qs['free'][par]], inc=LQ, inc_n=16)
                em.ins(role, lambda e: e.tensor_scalar(
                    obuf, wf, thr_s[:, tsel + 1:tsel + 2], 1.0, ALU.is_ge, ALU.subtract),
                    waits=[(LQ, lq1), (CV, cv_thr), (LQ, qs['outlq'][par]),
                           (CA, qs['sq_ca'])], inc=CX)
                c2x = em.ins(role, lambda e: e.tensor_scalar(
                    abuf, wf, thr_s[:, tsel:tsel + 1], None, ALU.is_ge),
                    waits=[(LQ, qs['abuf_lq'][par])], inc=CX)
                c3 = em.ins('gpsimd', lambda e: e.tensor_tensor(
                    obuf, obuf, abuf, ALU.add), waits=[(CV, c2x)], inc=CP)
                qs['free'][par] = (CP, c3)
                lqs = em.ins('sync', lambda e: e.dma_start(out=dst, in_=obuf),
                             waits=[(CP, c3)], inc=LQ, inc_n=16)
                qs['outlq'][par] = lqs
                qs['laststore'] = lqs
                if qs['i'] == 63:
                    qs['w1done'] = lqs
                if qs['i'] == len(qchunks) - 1:
                    qs['w2done'] = lqs
                qs['i'] += 1

            p25_state = dict(lq_xm_prev=0, cv_xm=0)
            p25_done = set()

            def emit_p25_tile(l):
                nonlocal sumsq_free
                p25_done.add(l)
                g = l // 4
                lqa = em.ins('sync', lambda e: e.dma_start(
                    out=xb_s[:, 0:2048], in_=attnR[l * 128:(l + 1) * 128, :]),
                    waits=[(CCg[g], 1), qs['free'][0],
                           (LQ, p25_state['lq_xm_prev'])],
                    inc=LQ, inc_n=16)
                lqb = em.ins('sync', lambda e: e.dma_start(
                    out=xb_s[:, 2048:4096], in_=xown[l * 128:(l + 1) * 128, :]),
                    waits=[qs['free'][1]], inc=LQ, inc_n=16)
                em.ins('vector', lambda e: e.reduce_sum(
                    stat_s[:, 3:4], xb_s[:, 0:2048], axis=AX.X),
                    waits=[(LQ, lqa)], inc=CV)
                em.fence('vector', CV)
                em.ins('vector', lambda e: e.tensor_scalar(
                    stat_s[:, 0:1], stat_s[:, 3:4], 1.0 / 2048, None, ALU.mult), inc=CV)
                caq = em.ins('scalar', lambda e: e.activation(
                    sqd_s[:], xb_s[:, 0:2048], AF.Square, accum_out=stat_s[:, 4:5]),
                    waits=[(LQ, lqa), (LQ, qs['laststore']), (CV, sumsq_free)], inc=CA)
                cv1 = em.ins('vector', lambda e: e.tensor_scalar(
                    stat_s[:, 6:7], stat_s[:, 4:5], 1.0 / 2048, EPS, ALU.mult, ALU.add),
                    waits=[(CA, caq)], inc=CV)
                sumsq_free = cv1
                em.ins('vector', lambda e: e.tensor_tensor(
                    stat_s[:, 7:8], stat_s[:, 0:1], stat_s[:, 0:1], ALU.mult), inc=CV)
                em.fence('vector', CV)
                cv3 = em.ins('vector', lambda e: e.tensor_tensor(
                    stat_s[:, 6:7], stat_s[:, 6:7], stat_s[:, 7:8], ALU.subtract), inc=CV)
                em.ins('scalar', lambda e: e.activation(
                    stat_s[:, 8:9], stat_s[:, 6:7], AF.Ln), waits=[(CV, cv3)], inc=CA)
                ca2 = em.ins('scalar', lambda e: e.activation(
                    stat_s[:, 1:2], stat_s[:, 8:9], AF.Exp, scale=-0.5), inc=CA)
                cv4 = em.ins('vector', lambda e: e.tensor_scalar(
                    stat_s[:, 2:3], stat_s[:, 0:1], stat_s[:, 1:2], -1.0,
                    ALU.mult, ALU.mult), waits=[(CA, ca2)], inc=CV)
                em.fence('vector', CV)
                em.ins('vector', lambda e: e.tensor_scalar(
                    xb_s[:, 0:2048], xb_s[:, 0:2048], stat_s[:, 1:2], stat_s[:, 2:3],
                    ALU.mult, ALU.add), waits=[(CV, cv4)], inc=CV)
                cv_xm = em.ins('vector', lambda e: e.tensor_tensor(
                    xb_s[:, 0:2048], xb_s[:, 0:2048], xb_s[:, 2048:4096], ALU.add),
                    waits=[(LQ, lqb)], inc=CV)
                p25_state['cv_xm'] = cv_xm
                p25_state['lq_xm_prev'] = em.ins('sync', lambda e: e.dma_start(
                    out=xmS[l * 128:(l + 1) * 128, :], in_=xb_s[:, 0:2048]),
                    waits=[(CV, cv_xm)], inc=LQ, inc_n=16)

            bounce_tiles = []
            for f0 in range(8):
                for d0 in range(16):
                    bounce_tiles.append((
                        w1qS[f0 * 1024:(f0 + 1) * 1024, d0 * 128:(d0 + 1) * 128],
                        w1qT[d0 * 128:(d0 + 1) * 128, f0 * 1024:(f0 + 1) * 1024], 0))
            for f0 in range(64):
                for m0 in range(2):
                    bounce_tiles.append((
                        w2qS[m0 * 1024:(m0 + 1) * 1024, f0 * 128:(f0 + 1) * 128],
                        w2qT[f0 * 128:(f0 + 1) * 128, m0 * 1024:(m0 + 1) * 1024], 1))
            bounce_state = dict(i=0, lq=0, plq=[0, 0])

            def emit_bounce_pair():
                bi = bounce_state['i']
                if bi >= len(bounce_tiles):
                    return
                bsrc, bdst, m = bounce_tiles[bi]
                par = bi % 2
                gate = qs['w1done'] if m == 0 else qs['w2done']
                lsb = em.ins('sync', lambda e: e.dma_start(
                    out=w2qT_s[:, par * 1024:(par + 1) * 1024], in_=bsrc,
                    transpose=True),
                    waits=[(LQ, gate), (LQ, bounce_state['plq'][par])],
                    inc=LQ, inc_n=16)
                bounce_state['plq'][par] = em.ins('sync', lambda e: e.dma_start(
                    out=bdst, in_=w2qT_s[:, par * 1024:(par + 1) * 1024]),
                    waits=[(LQ, lsb)], inc=LQ, inc_n=16)
                bounce_state['lq'] = bounce_state['plq'][par]
                bounce_state['i'] = bi + 1

            qs['abuf_lq'] = [lnb_lq[0], lnb_lq[1]]
            qs['sq_ca'] = CA.v
            cv_p1 = CV.v
            em.wait('tensor', CV, cv_p1)
            em.wait('scalar', CV, cv_p1)
            em.wait('scalar', CP, CP.v)

            PSA = [PS[0], PS[1], PS[5], PS[6]]
            psa_free = [0, 0, 0, 0]
            pt_free = [0] * 8
            cv_t4_par = [0, 0]
            ct_t4z_prev = 0
            ct_p1_stripes = CT.v
            cv_onorm_prev = 0
            cv_evac_prev = 0
            cv_evac_pair = [0, 0]
            ca_rb_prev = 0
            ct_B_prev = 0
            ct_wo_prev = 0
            cv_zp_prev = 0
            lq_evac_prev = 0
            ca_last_exp = 0
            for t in range(NT):
                ct_pv_last = 0
                nk = t + 1
                ngrp = (nk + 3) // 4
                exp_of = {}
                exp_last_grp = {-1: ca_last_exp}
                for g in range(ngrp + 1):
                    if g < ngrp:
                        kts = range(4 * g, min(4 * g + 4, nk))
                        for kt in kts:
                            bank = kt % 4
                            wS = [(CA, psa_free[bank])]
                            ctS = em.ins('tensor', lambda e: e.matmul(
                                PSA[bank][:], kT_s[:, kt * 128:(kt + 1) * 128],
                                qT_r[:, :, t * 128:(t + 1) * 128],
                                start=True, stop=True), waits=wS, inc=CT)
                            if kt == t:
                                cvtri = em.ins('vector', lambda e: e.tensor_tensor(
                                    PSA[bank][:], PSA[bank][:], tri_s[:], ALU.add),
                                    waits=[(CT, ctS)], inc=CV)
                                exp_of[kt] = ('cv', cvtri)
                            else:
                                exp_of[kt] = ('ct', ctS)
                    if g >= 1:
                        gp = (g - 1) % 2
                        hb2 = gp * 2048
                        kts = range(4 * (g - 1), min(4 * (g - 1) + 4, nk))
                        first = True
                        for kt in kts:
                            bank = kt % 4
                            wpv = [(CA, exp_last_grp[g - 1])] if first else []
                            first = False
                            if kt == 0:
                                wpv.append((CV, cv_onorm_prev))
                            ctP = em.ins('tensor', lambda e: e.matmul(
                                PS[2][:], v_s[:, kt * 128:(kt + 1) * 128],
                                pT_s[:, hb2 + bank * 512:hb2 + (bank + 1) * 512],
                                start=(kt == 0), stop=(kt == t)), waits=wpv, inc=CT)
                            ct_pv_last = ctP
                            pt_free[gp * 4 + bank] = ctP
                        g0 = 4 * (g - 1)
                        w = min(4, nk - g0)
                        if g == 1:
                            cv_t4 = em.ins('vector', lambda e: e.tensor_copy(
                                T4_s[:, 0:w * 512], pT_s[:, hb2:hb2 + w * 512]),
                                waits=[(CA, exp_last_grp[g - 1]), (CT, ct_t4z_prev),
                                       (CT, ct_p1_stripes)], inc=CV)
                            if w < 4:
                                cv_t4 = em.ins('vector', lambda e: e.memset(
                                    T4_s[:, w * 512:2048], 0.0), inc=CV)
                        else:
                            cv_t4 = em.ins('vector', lambda e: e.tensor_tensor(
                                T4_s[:, 0:w * 512], T4_s[:, 0:w * 512],
                                pT_s[:, hb2:hb2 + w * 512], ALU.add),
                                waits=[(CA, exp_last_grp[g - 1])], inc=CV)
                        cv_t4_par[gp] = cv_t4
                    if g < ngrp:
                        kts = range(4 * g, min(4 * g + 4, nk))
                        for kt in kts:
                            bank = kt % 4
                            kind, val = exp_of[kt]
                            wexp = [(CV, val)] if kind == 'cv' else [(CT, val)]
                            gp2 = g % 2
                            caE = em.ins('scalar', lambda e: e.activation(
                                pT_s[:, gp2 * 2048 + bank * 512:
                                     gp2 * 2048 + (bank + 1) * 512], PSA[bank][:],
                                AF.Exp, bias=mb_s[:, kt:kt + 1], scale=SCALE),
                                waits=wexp + [(CT, pt_free[gp2 * 4 + bank]),
                                              (CV, cv_t4_par[gp2])], inc=CA)
                            psa_free[bank] = caE
                            ca_last_exp = caE
                        exp_last_grp[g] = ca_last_exp

                cv_t4b = em.ins('vector', lambda e: e.tensor_copy(
                    T4b_s[:], T4_s[:]), waits=[(CT, ct_t4z_prev)], inc=CV)
                nzm = min(4, nk)
                for j in range(nzm):
                    wz = [(CV, cv_t4b)]
                    if j == 0:
                        wz.append((CV, cv_zp_prev))
                    em.ins('tensor', lambda e: e.matmul(
                        PS[3][0:1, :], ones_cb[:], T4b_s[:, j * 512:(j + 1) * 512],
                        start=(j == 0), stop=(j == nzm - 1)), waits=wz, inc=CT)
                ct_t4z_prev = CT.v
                cv_zp = em.ins('vector', lambda e: e.tensor_scalar(
                    rz_s[0:1, :], PS[3][0:1, :], 1e-30, None, ALU.add),
                    waits=[(CT, ct_t4z_prev), (CA, ca_rb_prev), (LQ, lq_evac_prev)],
                    inc=CV)
                cv_zp_prev = cv_zp
                em.fence('vector', CV)
                em.ins('vector', lambda e: e.reciprocal(rz_s[0:1, :], rz_s[0:1, :]),
                       inc=CV)
                em.fence('vector', CV)
                cv_rzb = em.ins('vector', lambda e: e.tensor_copy(
                    rzb_s[0:1, :], rz_s[0:1, :]),
                    waits=[(CT, ct_B_prev), (CT, ct_wo_prev)], inc=CV)
                ctB = em.ins('tensor', lambda e: e.matmul(
                    PS[4][:], ones_rb[0:1, :], rzb_s[0:1, :], start=True, stop=True),
                    waits=[(CV, cv_rzb), (CA, ca_rb_prev), (CV, cv_evac_prev)], inc=CT)
                ct_B_prev = ctB
                ca_rb = em.ins('scalar', lambda e: e.copy(rb_s[:], PS[4][:]),
                               waits=[(CT, ctB), (CV, cv_onorm_prev)], inc=CA)
                ca_rb_prev = ca_rb
                cv_on = em.ins('vector', lambda e: e.tensor_tensor(
                    onorm_s[:], PS[2][:], rb_s[:], ALU.mult),
                    waits=[(CA, ca_rb), (CT, ct_pv_last)], inc=CV)
                cv_onorm_prev = cv_on

                WOB = [PS7, PS[4]]
                for pair in range(2):
                    for hl in range(4):
                        for oc2 in range(2):
                            oc = pair * 2 + oc2
                            w = [(CV, cv_on), (LG, lg_wo)]
                            if hl == 0:
                                w.append((CV, cv_evac_pair[1] if pair == 0
                                          else cv_evac_pair[0]))
                            em.ins('tensor', lambda e: e.matmul(
                                WOB[oc2][:], onorm_s[:, hl * 128:(hl + 1) * 128],
                                wo_s[:, hl * 2048 + oc * 512:hl * 2048 + (oc + 1) * 512],
                                start=(hl == 0), stop=(hl == 3)), waits=w, inc=CT)
                    ct_wo = CT.v
                    ct_wo_prev = ct_wo
                    for oc2 in range(2):
                        oc = pair * 2 + oc2
                        cvs = em.ins('vector', lambda e: e.tensor_copy(
                            evac_s[:], WOB[oc2][:]),
                            waits=[(CT, ct_wo), (LQ, lq_evac_prev)], inc=CV)
                        lq_evac_prev = em.ins('sync', lambda e: e.dma_start(
                            out=attnP[t * 128:(t + 1) * 128,
                                      oc * 512:(oc + 1) * 512],
                            in_=evac_s[:]), waits=[(CV, cvs)], inc=LQ, inc_n=16)
                    cv_evac_pair[pair] = cvs
                cv_evac_prev = cvs


                if t >= 8:
                    for _ in range(4):
                        emit_quant_chunk()
                if t >= 40:
                    for _ in range(10):
                        emit_bounce_pair()
                if t in (45, 49, 53):
                    for l in range(4 * ((t - 45) // 4), 4 * ((t - 45) // 4) + 4):
                        emit_p25_tile(l)

                if t % 16 == 15:
                    g = t // 16
                    em.ins('gpsimd', lambda e: e.collective_compute(
                        "ReduceScatter", ALU.add,
                        replica_groups=[[0, 1, 2, 3], [4, 5, 6, 7]],
                        ins=[attnP[g * 2048:(g + 1) * 2048, :].opt()],
                        outs=[attnR[g * 512:(g + 1) * 512, :].opt()]),
                        waits=[(LQ, LQ.v)], inc=CCg[g])

            while qs['i'] < len(qchunks):
                emit_quant_chunk()
            while bounce_state['i'] < len(bounce_tiles):
                emit_bounce_pair()
            lq_bounce_all = bounce_state['lq']
            for l in range(16):
                if l not in p25_done:
                    emit_p25_tile(l)
            lq_xm_all = p25_state['lq_xm_prev']

            ln3_lq = [0, 0]
            for l in range(16):
                par = l % 2
                can, lqs = emit_ln_tile(
                    xmS[l * 128:(l + 1) * 128, :], ln3S[l * 128:(l + 1) * 128, :],
                    par, par,
                    load_waits=[(LQ, lq_xm_all)],
                    norm_waits=[(LQ, ln3_lq[par]), qs['free'][par]])
                ln3_lq[par] = lqs
            lq_ln3_all = LQ.v

            lqb2 = em.ins('sync', lambda e: e.dma_start(out=b2r_s[0:1, :], in_=b2row),
                          waits=[(CT, CT.v)], inc=LQ, inc_n=16)
            cv_now = CV.v
            ca_now = CA.v
            for ocb in range(4):
                ctb = em.ins('tensor', lambda e: e.matmul(
                    PS[ocb][:], ones_rf[0:1, :], b2r_s[0:1, ocb * 512:(ocb + 1) * 512],
                    start=True, stop=True),
                    waits=[(LQ, lqb2), (CV, cv_now), (CA, ca_now)], inc=CT)
                em.ins('vector', lambda e: e.tensor_copy(
                    b2b_s[:, ocb * 512:(ocb + 1) * 512], PS[ocb][:]),
                    waits=[(CT, ctb)], inc=CV)
            cv_b2b = CV.v

            ct_f1_half = [CT.v, CT.v]
            ct_w2blk = [0, 0]
            ca_hs = [0, 0]
            cv_out2 = cv_b2b
            cv_final = 0
            lq_out_last = 0
            for sg2 in range(4):
                ls_ln3 = 0
                for dc in range(16):
                    ls_ln3 = em.ins('scalar', lambda e: e.dma_start(
                        out=ln3T_s[:, dc * 512:(dc + 1) * 512],
                        in_=ln3S[sg2 * 512:(sg2 + 1) * 512, dc * 128:(dc + 1) * 128],
                        transpose=True),
                        waits=[(LQ, lq_ln3_all), (CT, ct_f1_half[0])],
                        inc=LS, inc_n=16)

                w1qT_r = w1qT.rearrange("(dc p) f -> p dc f", p=128)
                for fbg in range(64):
                    wpar = fbg % 2
                    lss = em.ins('sync', lambda e: e.dma_start(
                        out=w1b_s[:, wpar * 2048:(wpar + 1) * 2048],
                        in_=w1qT_r[:, :, fbg * 128:(fbg + 1) * 128]),
                        waits=[(LQ, lq_bounce_all), (CT, ct_f1_half[wpar])],
                        inc=LQ, inc_n=16)
                    pb = fbg % 2
                    for dc in range(16):
                        w = [(LQ, lss), (LS, ls_ln3)]
                        if dc == 0:
                            w.append((CA, ca_hs[pb]))
                            if fbg < 2 and sg2 > 0:
                                w.append((CV, cv_out2))
                        em.ins('tensor', lambda e: e.matmul(
                            PS[pb][:],
                            w1b_s[:, wpar * 2048 + dc * 128:
                                  wpar * 2048 + (dc + 1) * 128],
                            ln3T_s[:, dc * 512:(dc + 1) * 512],
                            start=(dc == 0), stop=(dc == 15)), waits=w, inc=CT)
                    ct_f1_half[wpar] = CT.v
                    ca_hs[pb] = em.ins('scalar', lambda e: e.activation(
                        Hs_s[:, fbg * 512:(fbg + 1) * 512], PS[pb][:], AF.Gelu,
                        bias=b1c_s[:, fbg:fbg + 1]),
                        waits=[(CT, CT.v), (CV, cv_b1)], inc=CA)
                ca_f1_done = max(ca_hs)

                for dmh in range(2):
                    for fb in range(64):
                        par = fb % 2
                        lsw = em.ins('scalar', lambda e: e.dma_start(
                            out=w2qT_s[:, par * 1024:(par + 1) * 1024],
                            in_=w2qT[fb * 128:(fb + 1) * 128,
                                     dmh * 1024:(dmh + 1) * 1024]),
                            waits=[(LQ, lq_bounce_all), (CT, ct_w2blk[par])],
                            inc=LS, inc_n=16)
                        for ts in range(4):
                            for oc2 in range(2):
                                bi = ts * 2 + oc2
                                bank = PS[bi] if bi < 7 else PS7
                                w = [(LS, lsw), (CA, ca_f1_done)]
                                if fb == 0:
                                    w.append((CV, cv_out2))
                                em.ins('tensor', lambda e: e.matmul(
                                    bank[:],
                                    Hs_s[:, fb * 512 + ts * 128:fb * 512 + (ts + 1) * 128],
                                    w2qT_s[:, par * 1024 + oc2 * 512:
                                           par * 1024 + (oc2 + 1) * 512],
                                    start=(fb == 0), stop=(fb == 63)), waits=w, inc=CT)
                        ct_w2blk[par] = CT.v
                    ct_f2 = CT.v
                    for ts in range(4):
                        for oc2 in range(2):
                            bi = ts * 2 + oc2
                            bank = PS[bi] if bi < 7 else PS7
                            cv_out2 = em.ins('vector', lambda e: e.tensor_tensor(
                                out2_s[:, ts * 2048 + dmh * 1024 + oc2 * 512:
                                       ts * 2048 + dmh * 1024 + (oc2 + 1) * 512],
                                bank[:],
                                b2b_s[:, dmh * 1024 + oc2 * 512:
                                      dmh * 1024 + (oc2 + 1) * 512],
                                ALU.add), waits=[(CT, ct_f2), (CV, cv_b2b),
                                                 (CV, cv_final)], inc=CV)

                for ts in range(4):
                    l = sg2 * 4 + ts
                    base = ts * 2048
                    lqx = em.ins('sync', lambda e: e.dma_start(
                        out=xb_s[:, 0:2048], in_=xmS[l * 128:(l + 1) * 128, :]),
                        waits=[(LQ, lq_xm_all), (CA, xb_free[0]),
                               (CV, cv_final)], inc=LQ, inc_n=16)
                    xb_free[0] = 0
                    em.ins('vector', lambda e: e.reduce_sum(
                        stat_s[:, 3:4], out2_s[:, base:base + 2048], axis=AX.X),
                        waits=[(CV, cv_out2)], inc=CV)
                    em.fence('vector', CV)
                    em.ins('vector', lambda e: e.tensor_scalar(
                        stat_s[:, 0:1], stat_s[:, 3:4], 1.0 / 2048, None, ALU.mult),
                        inc=CV)
                    caq = em.ins('scalar', lambda e: e.activation(
                        sqd_s[:], out2_s[:, base:base + 2048], AF.Square,
                        accum_out=stat_s[:, 4:5]),
                        waits=[(CV, cv_out2), (CV, sumsq_free)], inc=CA)
                    cv1 = em.ins('vector', lambda e: e.tensor_scalar(
                        stat_s[:, 6:7], stat_s[:, 4:5], 1.0 / 2048, EPS,
                        ALU.mult, ALU.add), waits=[(CA, caq)], inc=CV)
                    sumsq_free = cv1
                    em.ins('vector', lambda e: e.tensor_tensor(
                        stat_s[:, 7:8], stat_s[:, 0:1], stat_s[:, 0:1], ALU.mult),
                        inc=CV)
                    em.fence('vector', CV)
                    cv3 = em.ins('vector', lambda e: e.tensor_tensor(
                        stat_s[:, 6:7], stat_s[:, 6:7], stat_s[:, 7:8], ALU.subtract),
                        inc=CV)
                    em.ins('scalar', lambda e: e.activation(
                        stat_s[:, 8:9], stat_s[:, 6:7], AF.Ln), waits=[(CV, cv3)],
                        inc=CA)
                    ca2 = em.ins('scalar', lambda e: e.activation(
                        stat_s[:, 1:2], stat_s[:, 8:9], AF.Exp, scale=-0.5), inc=CA)
                    cv4 = em.ins('vector', lambda e: e.tensor_scalar(
                        stat_s[:, 2:3], stat_s[:, 0:1], stat_s[:, 1:2], -1.0,
                        ALU.mult, ALU.mult), waits=[(CA, ca2)], inc=CV)
                    em.fence('vector', CV)
                    em.ins('vector', lambda e: e.tensor_scalar(
                        out2_s[:, base:base + 2048], out2_s[:, base:base + 2048],
                        stat_s[:, 1:2], stat_s[:, 2:3], ALU.mult, ALU.add),
                        waits=[(CV, cv4)], inc=CV)
                    cv_final = em.ins('vector', lambda e: e.tensor_tensor(
                        out2_s[:, base:base + 2048], out2_s[:, base:base + 2048],
                        xb_s[:, 0:2048], ALU.add), waits=[(LQ, lqx)], inc=CV)
                    lq_out_last = em.ins('sync', lambda e: e.dma_start(
                        out=out[l * 128:(l + 1) * 128, :],
                        in_=out2_s[:, base:base + 2048]),
                        waits=[(CV, cv_final)], inc=LQ, inc_n=16)

            if debug:
                em.ins('gpsimd', lambda e: e.dma_start(out=d_attnP, in_=attnP),
                       waits=[(LQ, lq_out_last)], inc=LG, inc_n=16)
                em.ins('gpsimd', lambda e: e.dma_start(out=d_attnR, in_=attnR),
                       inc=LG, inc_n=16)
            em.wait('sync', LQ, lq_out_last)
            em.wait('gpsimd', LG, LG.v)
            em.wait('scalar', LS, LS.v)

        @blk.sync
        def _(eng):
            program(Em('sync', eng))

        @blk.gpsimd
        def _(eng):
            program(Em('gpsimd', eng))

        @blk.vector
        def _(eng):
            program(Em('vector', eng))

        @blk.scalar
        def _(eng):
            program(Em('scalar', eng))

        @blk.tensor
        def _(eng):
            program(Em('tensor', eng))

    return nc


def make_in_maps(X, Wq, Wk, Wv, Wg, Wd, Wo, w1, b1, w2, b2):
    in_maps = []
    for c in range(8):
        b, n = c // 4, c % 4
        own_rows = np.concatenate(
            [np.arange(g * 2048 + n * 512, g * 2048 + (n + 1) * 512)
             for g in range(4)])
        in_maps.append({
            "x": np.ascontiguousarray(X[b]),
            "xown": np.ascontiguousarray(X[b][own_rows]),
            "wq_g": np.ascontiguousarray(Wq[:, n * 512:(n + 1) * 512]),
            "wk_h": np.ascontiguousarray(Wk[:, n * 128:(n + 1) * 128]),
            "wv_h": np.ascontiguousarray(Wv[:, n * 128:(n + 1) * 128]),
            "wq_full": Wq,
            "wv_full": Wv,
            "wg_n": np.ascontiguousarray(Wg[:, n:n + 1]),
            "wd_n": np.ascontiguousarray(Wd[:, n:n + 1]),
            "wo_r": np.ascontiguousarray(Wo[n * 512:(n + 1) * 512, :]),
            "w1": w1,
            "w2": w2,
            "b1r": np.ascontiguousarray(b1.reshape(64, 128)),
            "b2row": np.ascontiguousarray(b2.reshape(1, 2048)),
        })
    return in_maps


def assemble(results):
    OUT = np.empty((B, S, D), dtype=np.float32)
    for c in range(8):
        b, n = c // 4, c % 4
        o = results[c]["out"]
        for g in range(4):
            OUT[b, g * 2048 + n * 512:g * 2048 + (n + 1) * 512, :] = \
                o[g * 512:(g + 1) * 512, :]
    return OUT


def kernel(L0_attn_hidden_states, L0_attn_Wq, L0_attn_Wk, L0_attn_Wv,
           L0_attn_Wg, L0_attn_Wd, L0_attn_Wo, L0_ffn_w1, L0_ffn_b1,
           L0_ffn_w2, L0_ffn_b2):
    f = lambda a: np.ascontiguousarray(np.asarray(a, dtype=np.float32))
    X = f(L0_attn_hidden_states)
    in_maps = make_in_maps(X, f(L0_attn_Wq), f(L0_attn_Wk), f(L0_attn_Wv),
                           f(L0_attn_Wg), f(L0_attn_Wd), f(L0_attn_Wo),
                           f(L0_ffn_w1), f(L0_ffn_b1), f(L0_ffn_w2),
                           f(L0_ffn_b2))
    nc = build_program(debug=False)
    res = run_bass_kernel_spmd(nc, in_maps, core_ids=list(range(8)))
    return assemble(res.results)


# version 15
# speedup vs baseline: 1.1984x; 1.0215x over previous
import numpy as np
import ml_dtypes
import concourse.bass as bass
import concourse.mybir as mybir
from concourse.bass_utils import run_bass_kernel_spmd

F32 = mybir.dt.float32
BF16 = mybir.dt.bfloat16
AF = mybir.ActivationFunctionType
ALU = mybir.AluOpType
AX = mybir.AxisListType
bf16 = ml_dtypes.bfloat16

B, S, D = 2, 8192, 2048
HD = 128
DFF = 8192
NT = S // 128
SCALE = HD ** -0.5
EPS = 1e-5

SG = 16
SGT = 4


class Sem:
    def __init__(self, handle, name):
        self.h = handle
        self.name = name
        self.v = 0

    def bump(self, n):
        self.v += n
        return self.v


class Em:

    def __init__(self, role, engine):
        self.role = role
        self.eng = engine
        self.waited = {}

    def _wait(self, s, v):
        if v <= 0 or self.waited.get(s.name, -1) >= v:
            return
        self.eng.wait_ge(s.h, v)
        self.waited[s.name] = v

    def ins(self, role, fn, waits=(), inc=None, inc_n=1):
        if role == self.role:
            for (s, v) in waits:
                self._wait(s, v)
            inst = fn(self.eng)
            if inc is not None:
                inst.then_inc(inc.h, inc_n)
        if inc is not None:
            return inc.bump(inc_n)
        return None

    def wait(self, role, s, v):
        if role == self.role:
            self._wait(s, v)

    def fence(self, role, s):
        if role == self.role:
            self._wait(s, s.v)


def build_program(debug=False):
    nc = bass.Bass("TRN2", target_bir_lowering=False, debug=False, num_devices=8)

    def inp(name, shape, dt=F32):
        return nc.dram_tensor(name, shape, dt, kind="ExternalInput").ap()

    x = inp("x", [S, D])
    xown = inp("xown", [2048, D])
    wq_g = inp("wq_g", [D, 512])
    wk_h = inp("wk_h", [D, HD])
    wv_h = inp("wv_h", [D, HD])
    wq_full = inp("wq_full", [D, D])
    wv_full = inp("wv_full", [D, 512])
    wg_n = inp("wg_n", [D, 1])
    wd_n = inp("wd_n", [512, 1])
    wo_r = inp("wo_r", [512, D])
    w1 = inp("w1", [DFF, D])
    w2 = inp("w2", [D, DFF])
    b1r = inp("b1r", [64, 128])
    b2row = inp("b2row", [1, D])

    out = nc.dram_tensor("out", [2048, D], F32, kind="ExternalOutput").ap()

    dbg_kind = "ExternalOutput" if debug else "Internal"

    def scr(name, shape, dt):
        return nc.dram_tensor(name, shape, dt, kind=dbg_kind).ap()

    lnS = scr("lnS", [S, D], BF16)
    attnP = nc.dram_tensor("attnP", [S, D], F32, kind="Internal").ap()
    attnR = nc.dram_tensor("attnR", [2048, D], F32, kind="Internal").ap()
    xmS = scr("xmS", [2048, D], F32)
    ln3S = scr("ln3S", [2048, D], BF16)
    w1qS = scr("w1qS", [DFF, D], BF16)
    w2qS = scr("w2qS", [D, DFF], BF16)
    w1qT = scr("w1qT", [D, DFF], BF16)
    w2qT = scr("w2qT", [DFF, D], BF16)
    wqbfS = nc.dram_tensor("wqbfS", [D, D], BF16, kind="Internal").ap()
    wvbfS = nc.dram_tensor("wvbfS", [D, 512], BF16, kind="Internal").ap()
    if debug:
        d_kT = nc.dram_tensor("d_kT", [128, S], BF16, kind="ExternalOutput").ap()
        d_v = nc.dram_tensor("d_v", [128, S], BF16, kind="ExternalOutput").ap()
        d_qT = nc.dram_tensor("d_qT", [128, 4 * S], BF16, kind="ExternalOutput").ap()
        d_mb = nc.dram_tensor("d_mb", [128, 64], F32, kind="ExternalOutput").ap()
        d_thr = nc.dram_tensor("d_thr", [128, 4], F32, kind="ExternalOutput").ap()
        d_abn = nc.dram_tensor("d_abn", [128, 32], mybir.dt.bfloat16, kind="ExternalOutput").ap()
        d_attnP = nc.dram_tensor("d_attnP", [S, D], F32, kind="ExternalOutput").ap()
        d_attnR = nc.dram_tensor("d_attnR", [2048, D], F32, kind="ExternalOutput").ap()

    identB = nc.inline_tensor(np.eye(128, dtype=bf16), name="identB")
    identF = nc.inline_tensor(np.eye(64, dtype=np.float32), name="identF")
    tq = np.arange(128)
    tri_blk = np.where(tq[None, :] >= tq[:, None], 0.0, -1e32).astype(bf16)
    triC = nc.inline_tensor(np.tile(tri_blk, (1, 4)), name="triC")

    def sb(name, shape, dt):
        return nc.alloc_sbuf_tensor(name, shape, dt).ap()

    def sb_at(name, shape, dt, base_of, byte_off=0):
        off = nc.lookup_mloc(base_of.tensor).addr + byte_off
        return nc.alloc_sbuf_tensor_at(name, shape, dt, offset=off).ap()

    identB_s = sb("identB_s", [128, 128], BF16)
    identF_s = sb("identF_s", [64, 64], F32)
    tri_s = sb("tri_s", [128, 512], BF16)
    ones_cb = sb("ones_cb", [128, 1], BF16)
    ones_rb = sb("ones_rb", [1, 128], BF16)
    ones_rf = sb("ones_rf", [1, 128], F32)
    wkh_s = sb("wkh_s", [128, 16 * 128], BF16)
    wvh_s = sb("wvh_s", [128, 16 * 128], BF16)
    wqg_s = sb("wqg_s", [128, 16 * 512], BF16)
    wgn_s = sb("wgn_s", [128, 16], BF16)
    wdn_s = sb("wdn_s", [128, 4], BF16)
    abn_s = sb("abn_s", [128, 32], BF16)
    kT_s = sb("kT_s", [128, S], BF16)
    v_s = sb("v_s", [128, S], BF16)
    qT_s = sb("qT_s", [128, 4 * S], BF16)
    stripes = sb("stripes", [128, 16 * 1024], BF16)
    w1b_s = sb("w1b_s", [128, 2 * 2048], BF16)
    mb_s = sb("mb_s", [128, 64], F32)
    gdc_s = sb("gdc_s", [128, 16], F32)
    xb_s = sb("xb_s", [128, 2 * 2048], F32)
    lnb_s = sb("lnb_s", [128, 2 * 2048], BF16)
    sqd_s = sb("sqd_s", [128, 2048], BF16)
    pT_s = sb("pT_s", [128, 8 * 512], BF16)
    onorm_s = sb("onorm_s", [128, 512], BF16)
    rb_s = sb("rb_s", [128, 512], F32)
    w2qT_s = sb("w2qT_s", [128, 2 * 1024], BF16)
    b1c_s = sb("b1c_s", [128, 64], F32)
    b1r_s = sb("b1r_s", [64, 128], F32)
    stat_s = sb("stat_s", [128, 40], F32)
    absacc_v = sb("absacc_v", [128, 2], F32)
    absacc_p = sb("absacc_p", [128, 2], F32)
    thr_s = sb("thr_s", [128, 4], F32)
    trow_s = sb("trow_s", [1, 4], F32)

    rem = nc.sbuf_bytes_remaining
    assert rem >= 0, f"SBUF over budget: {rem}"

    b2b_s = sb_at("b2b_s", [128, 2048], F32, wkh_s)
    ln3T_s = sb_at("ln3T_s", [128, 16 * 512], BF16, wqg_s)
    out2_s = sb_at("out2_s", [128, 8192], F32, kT_s)
    Hs_s = sb_at("Hs_s", [128, 64 * 512], BF16, qT_s)
    wo_s = sb_at("wo_s", [128, 4 * 2048], BF16, stripes)
    b2r_s = sb_at("b2r_s", [1, 2048], F32, stripes, byte_off=16384)
    qC_s = sb_at("qC_s", [128, 2048], BF16, w1b_s)
    evac_s = sb_at("evac_s", [128, 512], F32, rb_s)
    rz_s = sb_at("rz_s", [1, 512], F32, rb_s)
    rzb_s = sb_at("rzb_s", [1, 512], BF16, onorm_s)
    absb_s = sb_at("absb_s", [128, 2048], BF16, pT_s)
    T4_s = sb_at("T4_s", [128, 2048], F32, stripes, byte_off=16384)
    T4b_s = sb_at("T4b_s", [128, 2048], BF16, stripes, byte_off=24576)
    w2p_s = sb_at("w2p_s", [128, 2048], BF16, stripes, byte_off=28672)
    assert nc.lookup_mloc(v_s.tensor).addr == nc.lookup_mloc(kT_s.tensor).addr + S * 2
    assert nc.lookup_mloc(wvh_s.tensor).addr == nc.lookup_mloc(wkh_s.tensor).addr + 16 * 128 * 2

    PS = [nc.alloc_psum_tensor(f"ps{i}", [128, 512], F32).ap() for i in range(7)]
    PS7 = nc.alloc_psum_tensor("ps7", [128, 512], F32).ap()

    qT_r = qT_s.rearrange("p (c u) -> p c u", c=4)

    with (
        nc.semaphore("LQ") as hLQ,
        nc.semaphore("LS") as hLS,
        nc.semaphore("LG") as hLG,
        nc.semaphore("CT") as hCT,
        nc.semaphore("CV") as hCV,
        nc.semaphore("CA") as hCA,
        nc.semaphore("CP") as hCP,
        nc.semaphore("CC0") as hCC0,
        nc.semaphore("CC1") as hCC1,
        nc.semaphore("CC2") as hCC2,
        nc.semaphore("CC3") as hCC3,
        nc.Block() as blk,
    ):
        handles = dict(LQ=hLQ, LS=hLS, LG=hLG, CT=hCT, CV=hCV, CA=hCA,
                       CP=hCP, CC0=hCC0, CC1=hCC1, CC2=hCC2, CC3=hCC3)

        def program(em):
            sm = {k: Sem(h, k) for k, h in handles.items()}
            LQ, LS, LG = sm["LQ"], sm["LS"], sm["LG"]
            CT, CV, CA, CP = sm["CT"], sm["CV"], sm["CA"], sm["CP"]
            CCg = [sm["CC0"], sm["CC1"], sm["CC2"], sm["CC3"]]

            st = {}

            em.ins('vector', lambda e: e.memset(ones_cb[:], 1.0), inc=CV)
            em.ins('vector', lambda e: e.memset(ones_rb[:], 1.0), inc=CV)
            em.ins('vector', lambda e: e.memset(ones_rf[:], 1.0), inc=CV)
            em.ins('vector', lambda e: e.memset(stat_s[:, 35:36], 1.0), inc=CV)
            v_ones = CV.v

            em.ins('sync', lambda e: e.dma_start(out=identB_s[:], in_=identB.ap()), inc=LQ, inc_n=16)
            em.ins('sync', lambda e: e.dma_start(out=identF_s[:], in_=identF.ap()), inc=LQ, inc_n=16)
            em.ins('sync', lambda e: e.dma_start(out=tri_s[:], in_=triC.ap()), inc=LQ, inc_n=16)
            em.ins('sync', lambda e: e.dma_start(out=b1r_s[:], in_=b1r), inc=LQ, inc_n=16)
            lq_consts = LQ.v

            em.ins('gpsimd', lambda e: e.dma_start(
                out=wkh_s.rearrange("p (dc c) -> p dc c", dc=16), in_=wk_h.rearrange("(dc p) c -> p dc c", p=128)), inc=LG, inc_n=16)
            em.ins('gpsimd', lambda e: e.dma_start(
                out=wvh_s.rearrange("p (dc c) -> p dc c", dc=16), in_=wv_h.rearrange("(dc p) c -> p dc c", p=128)), inc=LG, inc_n=16)
            em.ins('gpsimd', lambda e: e.dma_start(
                out=wqg_s.rearrange("p (dc c) -> p dc c", dc=16), in_=wq_g.rearrange("(dc p) c -> p dc c", p=128)), inc=LG, inc_n=16)
            em.ins('gpsimd', lambda e: e.dma_start(
                out=wgn_s.rearrange("p (cc o) -> p cc o", cc=16),
                in_=wg_n.rearrange("(cc p) o -> p cc o", p=128),
                allow_slow_non_contiguous=True), inc=LG, inc_n=16)
            em.ins('gpsimd', lambda e: e.dma_start(
                out=wdn_s.rearrange("p (cc o) -> p cc o", cc=4),
                in_=wd_n.rearrange("(cc p) o -> p cc o", p=128),
                allow_slow_non_contiguous=True), inc=LG, inc_n=16)
            em.ins('gpsimd', lambda e: e.dma_start(out=wqbfS, in_=wq_full),
                   inc=LG, inc_n=16)
            em.ins('gpsimd', lambda e: e.dma_start(out=wvbfS, in_=wv_full),
                   inc=LG, inc_n=16)
            lg_p0w = LG.v

            ctb1 = em.ins('tensor', lambda e: e.matmul(
                PS[6][0:128, 0:64], b1r_s[:], identF_s[:],
                start=True, stop=True),
                waits=[(LQ, lq_consts)], inc=CT)
            cv_b1 = em.ins('vector', lambda e: e.tensor_copy(
                b1c_s[:], PS[6][0:128, 0:64]), waits=[(CT, ctb1)], inc=CV)

            ct_abp = 0
            cv_abn_evac = 0
            for dh in range(2):
                lsx = 0
                for cc in range(16):
                    lsx = em.ins('scalar', lambda e: e.dma_start(
                        out=stripes[:, cc * 1024:(cc + 1) * 1024],
                        in_=wqbfS[dh * 1024:(dh + 1) * 1024, cc * 128:(cc + 1) * 128],
                        transpose=True),
                        waits=[(LG, lg_p0w), (CT, ct_abp)], inc=LS, inc_n=16)
                for d8 in range(8):
                    dc = dh * 8 + d8
                    for cc in range(16):
                        w5 = [(LS, lsx)]
                        if cc == 0:
                            w5.append((CV, cv_abn_evac))
                        em.ins('tensor', lambda e: e.matmul(
                            PS[5][:, 0:1],
                            stripes[:, cc * 1024 + d8 * 128:cc * 1024 + (d8 + 1) * 128],
                            wgn_s[:, cc:cc + 1], start=(cc == 0), stop=(cc == 15)),
                            waits=w5, inc=CT)
                    cv_abn_evac = em.ins('vector', lambda e: e.tensor_copy(
                        abn_s[:, dc * 2:dc * 2 + 1], PS[5][:, 0:1]),
                        waits=[(CT, CT.v)], inc=CV)
                ct_abp = CT.v
            lsx = 0
            for vc in range(4):
                lsx = em.ins('scalar', lambda e: e.dma_start(
                    out=stripes[:, vc * 2048:(vc + 1) * 2048],
                    in_=wvbfS[:, vc * 128:(vc + 1) * 128], transpose=True),
                    waits=[(LG, lg_p0w), (CT, ct_abp)], inc=LS, inc_n=16)
            for dc in range(16):
                for vc in range(4):
                    w5 = [(LS, lsx)]
                    if vc == 0:
                        w5.append((CV, cv_abn_evac))
                    em.ins('tensor', lambda e: e.matmul(
                        PS[5][:, 0:1],
                        stripes[:, vc * 2048 + dc * 128:vc * 2048 + (dc + 1) * 128],
                        wdn_s[:, vc:vc + 1], start=(vc == 0), stop=(vc == 3)),
                        waits=w5, inc=CT)
                cv_abn_evac = em.ins('vector', lambda e: e.tensor_copy(
                    abn_s[:, dc * 2 + 1:dc * 2 + 2], PS[5][:, 0:1]),
                    waits=[(CT, CT.v)], inc=CV)
            ct_abp = CT.v
            cv_ab = CV.v
            ct_ab_final = CT.v
            cv_abn_done = CV.v

            xb_free = [0, 0]
            sumsq_free = 0
            sumsq_free_p = [0, 0]

            def emit_ln_tile(src_ap, dst_ap, xpar, lpar, load_waits, norm_waits):
                sb0 = xpar * 10
                lq1 = em.ins('sync', lambda e: e.dma_start(
                    out=xb_s[:, xpar * 2048:(xpar + 1) * 2048], in_=src_ap),
                    waits=list(load_waits) + [(CA, xb_free[xpar])], inc=LQ, inc_n=16)
                em.ins('vector', lambda e: e.reduce_sum(
                    stat_s[:, sb0 + 3:sb0 + 4], xb_s[:, xpar * 2048:(xpar + 1) * 2048],
                    axis=AX.X), waits=[(LQ, lq1)], inc=CV)
                em.fence('vector', CV)
                em.ins('vector', lambda e: e.tensor_scalar(
                    stat_s[:, sb0:sb0 + 1], stat_s[:, sb0 + 3:sb0 + 4], 1.0 / 2048,
                    None, ALU.mult), inc=CV)
                caq = em.ins('scalar', lambda e: e.activation(
                    sqd_s[:, xpar * 1024:(xpar + 1) * 1024],
                    xb_s[:, xpar * 2048:(xpar + 1) * 2048].rearrange(
                        "p (a b) -> p a b", a=2)[:, 0:1, :], AF.Square,
                    accum_out=stat_s[:, sb0 + 4:sb0 + 5]),
                    waits=[(LQ, lq1), (CV, sumsq_free_p[xpar])], inc=CA)
                caq = em.ins('scalar', lambda e: e.activation(
                    sqd_s[:, xpar * 1024:(xpar + 1) * 1024],
                    xb_s[:, xpar * 2048:(xpar + 1) * 2048].rearrange(
                        "p (a b) -> p a b", a=2)[:, 1:2, :], AF.Square,
                    accum_out=stat_s[:, sb0 + 5:sb0 + 6]), inc=CA)
                cv1 = em.ins('vector', lambda e: e.tensor_tensor(
                    stat_s[:, sb0 + 4:sb0 + 5], stat_s[:, sb0 + 4:sb0 + 5],
                    stat_s[:, sb0 + 5:sb0 + 6], ALU.add),
                    waits=[(CA, caq)], inc=CV)
                em.fence('vector', CV)
                cv1 = em.ins('vector', lambda e: e.tensor_scalar(
                    stat_s[:, sb0 + 6:sb0 + 7], stat_s[:, sb0 + 4:sb0 + 5], 1.0 / 2048,
                    EPS, ALU.mult, ALU.add), inc=CV)
                sumsq_free_p[xpar] = cv1
                em.ins('vector', lambda e: e.tensor_tensor(
                    stat_s[:, sb0 + 7:sb0 + 8], stat_s[:, sb0:sb0 + 1],
                    stat_s[:, sb0:sb0 + 1], ALU.mult), inc=CV)
                em.fence('vector', CV)
                cv3 = em.ins('vector', lambda e: e.tensor_tensor(
                    stat_s[:, sb0 + 6:sb0 + 7], stat_s[:, sb0 + 6:sb0 + 7],
                    stat_s[:, sb0 + 7:sb0 + 8], ALU.subtract), inc=CV)
                em.ins('scalar', lambda e: e.activation(
                    stat_s[:, sb0 + 8:sb0 + 9], stat_s[:, sb0 + 6:sb0 + 7], AF.Ln),
                    waits=[(CV, cv3)], inc=CA)
                ca2 = em.ins('scalar', lambda e: e.activation(
                    stat_s[:, sb0 + 1:sb0 + 2], stat_s[:, sb0 + 8:sb0 + 9], AF.Exp,
                    scale=-0.5), inc=CA)
                cv4 = em.ins('vector', lambda e: e.tensor_scalar(
                    stat_s[:, sb0 + 2:sb0 + 3], stat_s[:, sb0:sb0 + 1],
                    stat_s[:, sb0 + 1:sb0 + 2], -1.0,
                    ALU.mult, ALU.mult), waits=[(CA, ca2)], inc=CV)
                can = em.ins('scalar', lambda e: e.activation(
                    lnb_s[:, lpar * 2048:(lpar + 1) * 2048],
                    xb_s[:, xpar * 2048:(xpar + 1) * 2048], AF.Identity,
                    bias=stat_s[:, sb0 + 2:sb0 + 3], scale=stat_s[:, sb0 + 1:sb0 + 2]),
                    waits=[(CV, cv4)] + list(norm_waits), inc=CA)
                xb_free[xpar] = can
                lqs = em.ins('sync', lambda e: e.dma_start(
                    out=dst_ap, in_=lnb_s[:, lpar * 2048:(lpar + 1) * 2048]),
                    waits=[(CA, can)], inc=LQ, inc_n=16)
                return can, lqs

            abs_chunks = []
            for r in range(64):
                for c in range(2):
                    abs_chunks.append((w1[r * 128:(r + 1) * 128, c * 1024:(c + 1) * 1024], 0))
            for r in range(16):
                for c in range(8):
                    abs_chunks.append((w2[r * 128:(r + 1) * 128, c * 1024:(c + 1) * 1024], 1))
            abs_state = dict(i=0, sv=[False, False], sp=[False, False],
                             vfree=0, pfree=0, cvd=0, cpd=0)

            def emit_abs_chunk():
                a = abs_state
                if a['i'] >= len(abs_chunks):
                    return
                src, m = abs_chunks[a['i']]
                par = a['i'] % 2
                prevfree = [(CV, a['vfree'])] if par == 0 else [(CV, a['pfree'])]
                lg1 = em.ins('gpsimd', lambda e: e.dma_start(
                    out=absb_s[:, par * 1024:(par + 1) * 1024], in_=src),
                    waits=prevfree, inc=LG, inc_n=16)
                if par == 0:
                    em.ins('vector', lambda e: e.reduce_sum(
                        stat_s[:, 33:34], absb_s[:, 0:1024],
                        axis=AX.X, apply_absolute_value=True),
                        waits=[(LG, lg1)], inc=CV)
                    em.fence('vector', CV)
                    started = a['sv']
                    acc, scol = absacc_v, 33
                else:
                    em.ins('scalar', lambda e: e.activation(
                        sqd_s[:, 0:1024], absb_s[:, 1024:2048], AF.Abs,
                        accum_out=stat_s[:, 34:35]),
                        waits=[(LG, lg1), (CV, a['pfree'])], inc=CA)
                    em.wait('vector', CA, CA.v)
                    started = a['sp']
                    acc, scol = absacc_p, 34
                if not started[m]:
                    c2 = em.ins('vector', lambda e: e.tensor_copy(
                        acc[:, m:m + 1], stat_s[:, scol:scol + 1]), inc=CV)
                    started[m] = True
                else:
                    c2 = em.ins('vector', lambda e: e.tensor_tensor(
                        acc[:, m:m + 1], acc[:, m:m + 1], stat_s[:, scol:scol + 1],
                        ALU.add), inc=CV)
                if par == 0:
                    a['vfree'] = c2
                    a['cvd'] = c2
                else:
                    a['pfree'] = c2
                    a['cvd'] = c2
                a['i'] += 1

            lnb_lq = [0, 0]
            ca_mb_prev = 0
            cv_evK = 0
            cv_evV = 0
            cv_evG = cv_ab
            cv_evQ = 0
            ct_half = [ct_ab_final, ct_ab_final]
            lq_spill = {}
            for sg in range(SG):
                for tt in range(SGT):
                    t = sg * SGT + tt
                    par = t % 2
                    can, lqs = emit_ln_tile(
                        x[t * 128:(t + 1) * 128, :], lnS[t * 128:(t + 1) * 128, :],
                        par, par,
                        load_waits=[],
                        norm_waits=[(LQ, lnb_lq[par]),
                                    (CT, ct_ab_final if t < 2 else 0)])
                    lnb_lq[par] = lqs
                    for _ in range(4):
                        emit_abs_chunk()
                lq_spill[sg] = LQ.v

                half = sg % 2
                hb = half * 8192

                ls_str = 0
                for dc in range(16):
                    ls_str = em.ins('scalar', lambda e: e.dma_start(
                        out=stripes[:, hb + dc * 512:hb + (dc + 1) * 512],
                        in_=lnS[sg * 512:(sg + 1) * 512, dc * 128:(dc + 1) * 128],
                        transpose=True),
                        waits=[(LQ, lq_spill[sg]), (CT, ct_half[half])],
                        inc=LS, inc_n=16)

                def stripe(dc, lo, n):
                    return stripes[:, hb + dc * 512 + lo:hb + dc * 512 + lo + n]

                for dc in range(16):
                    em.ins('tensor', lambda e: e.matmul(
                        PS[0][:], wkh_s[:, dc * 128:(dc + 1) * 128], stripe(dc, 0, 512),
                        start=(dc == 0), stop=(dc == 15)),
                        waits=[(LS, ls_str), (CV, cv_evK), (LG, lg_p0w)], inc=CT)
                cv_evK = em.ins('vector', lambda e: e.tensor_copy(
                    kT_s[:, sg * 512:(sg + 1) * 512], PS[0][:]),
                    waits=[(CT, CT.v)], inc=CV)

                for tt in range(SGT):
                    for dc in range(16):
                        em.ins('tensor', lambda e: e.matmul(
                            PS[1][:, 0:128], stripe(dc, tt * 128, 128),
                            wvh_s[:, dc * 128:(dc + 1) * 128],
                            start=(dc == 0), stop=(dc == 15)),
                            waits=[(CV, cv_evV)], inc=CT)
                    cv_evV = em.ins('vector', lambda e: e.tensor_copy(
                        v_s[:, (sg * SGT + tt) * 128:(sg * SGT + tt + 1) * 128],
                        PS[1][:, 0:128]), waits=[(CT, CT.v)], inc=CV)

                for tt in range(SGT):
                    for dc in range(16):
                        em.ins('tensor', lambda e: e.matmul(
                            PS[3][:, 0:2], stripe(dc, tt * 128, 128),
                            abn_s[:, dc * 2:dc * 2 + 2],
                            start=(dc == 0), stop=(dc == 15)),
                            waits=[(CV, cv_evG)], inc=CT)
                    em.ins('vector', lambda e: e.tensor_copy(
                        gdc_s[:, tt:tt + 1], PS[3][:, 0:1]),
                        waits=[(CT, CT.v), (CA, ca_mb_prev)], inc=CV)
                    cv_evG = em.ins('vector', lambda e: e.tensor_copy(
                        gdc_s[:, 8 + tt:9 + tt], PS[3][:, 1:2]), inc=CV)

                ca_t = em.ins('scalar', lambda e: e.activation(
                    stat_s[:, 20:24], gdc_s[:, 0:4], AF.Tanh, scale=0.5),
                    waits=[(CV, cv_evG)], inc=CA)
                em.ins('vector', lambda e: e.tensor_scalar(
                    stat_s[:, 20:24], stat_s[:, 20:24], 0.5, 0.5, ALU.mult, ALU.add),
                    waits=[(CA, ca_t)], inc=CV)
                em.fence('vector', CV)
                cv6 = em.ins('vector', lambda e: e.tensor_tensor(
                    stat_s[:, 24:28], stat_s[:, 20:24], gdc_s[:, 8:12], ALU.mult),
                    waits=[(CA, ca_mb_prev)], inc=CV)
                ca_s = em.ins('scalar', lambda e: e.activation(
                    stat_s[:, 28:32], stat_s[:, 24:28], AF.Sign),
                    waits=[(CV, cv6)], inc=CA)
                ca_mb_prev = ca_s
                em.ins('vector', lambda e: e.tensor_scalar(
                    stat_s[:, 28:32], stat_s[:, 28:32], 1.0, 5e29, ALU.subtract, ALU.mult),
                    waits=[(CA, ca_s)], inc=CV)
                em.fence('vector', CV)
                em.ins('vector', lambda e: e.tensor_tensor(
                    mb_s[:, sg * 4:(sg + 1) * 4], stat_s[:, 24:28], stat_s[:, 28:32],
                    ALU.add), inc=CV)

                for cc in range(4):
                    for dc in range(16):
                        em.ins('tensor', lambda e: e.matmul(
                            PS[2][:], wqg_s[:, dc * 512 + cc * 128:dc * 512 + (cc + 1) * 128],
                            stripe(dc, 0, 512), start=(dc == 0), stop=(dc == 15)),
                            waits=[(CV, cv_evQ)], inc=CT)
                    cv_evQ = em.ins('vector', lambda e: e.tensor_copy(
                        qT_s[:, cc * S + sg * 512:cc * S + (sg + 1) * 512], PS[2][:]),
                        waits=[(CT, CT.v)], inc=CV)

                ct_half[half] = CT.v

            while abs_state['i'] < len(abs_chunks):
                emit_abs_chunk()

            cvt = em.ins('vector', lambda e: e.tensor_tensor(
                absacc_v[:, 0:2], absacc_v[:, 0:2], absacc_p[:, 0:2], ALU.add),
                waits=[(CV, abs_state['cvd'])], inc=CV)
            ctt1 = em.ins('tensor', lambda e: e.matmul(
                PS[5][0:1, 0:2], stat_s[:, 35:36], absacc_v[:, 0:2],
                start=True, stop=True), waits=[(CV, cvt)], inc=CT)
            em.ins('vector', lambda e: e.tensor_copy(trow_s[0:1, 0:2], PS[5][0:1, 0:2]),
                   waits=[(CT, ctt1)], inc=CV)
            em.fence('vector', CV)
            cvc = em.ins('vector', lambda e: e.tensor_scalar(
                trow_s[0:1, 2:4], trow_s[0:1, 0:2], 0.7 / (DFF * D), None, ALU.mult),
                inc=CV)
            em.ins('tensor', lambda e: e.matmul(
                PS[5][:, 2:3], ones_rf[0:1, :], trow_s[0:1, 2:3], start=True, stop=True),
                waits=[(CV, cvc)], inc=CT)
            ctb2 = em.ins('tensor', lambda e: e.matmul(
                PS[5][:, 3:4], ones_rf[0:1, :], trow_s[0:1, 3:4], start=True, stop=True),
                inc=CT)
            em.ins('vector', lambda e: e.tensor_copy(thr_s[:, 0:1], PS[5][:, 2:3]),
                   waits=[(CT, ctb2)], inc=CV)
            em.ins('vector', lambda e: e.tensor_scalar(
                thr_s[:, 1:2], PS[5][:, 2:3], -1.0, None, ALU.mult), inc=CV)
            em.ins('vector', lambda e: e.tensor_copy(thr_s[:, 2:3], PS[5][:, 3:4]), inc=CV)
            cv_thr = em.ins('vector', lambda e: e.tensor_scalar(
                thr_s[:, 3:4], PS[5][:, 3:4], -1.0, None, ALU.mult), inc=CV)

            lg_wo = em.ins('gpsimd', lambda e: e.dma_start(
                out=wo_s.rearrange("p (ch c) -> p ch c", ch=4), in_=wo_r.rearrange("(ch p) c -> p ch c", p=128)),
                waits=[(CT, CT.v)], inc=LG, inc_n=16)

            if debug:
                em.ins('sync', lambda e: e.dma_start(out=d_kT, in_=kT_s[:]),
                       waits=[(CV, CV.v)], inc=LQ, inc_n=16)
                em.ins('sync', lambda e: e.dma_start(out=d_v, in_=v_s[:]), inc=LQ, inc_n=16)
                em.ins('sync', lambda e: e.dma_start(out=d_qT, in_=qT_s[:]), inc=LQ, inc_n=16)
                em.ins('sync', lambda e: e.dma_start(out=d_mb, in_=mb_s[:]), inc=LQ, inc_n=16)
                em.ins('sync', lambda e: e.dma_start(out=d_thr, in_=thr_s[:]), inc=LQ, inc_n=16)
                em.ins('sync', lambda e: e.dma_start(out=d_abn, in_=abn_s[:]), inc=LQ, inc_n=16)

            qchunks = []
            for r in range(64):
                qchunks.append((w1[r * 128:(r + 1) * 128, :],
                                w1qS[r * 128:(r + 1) * 128, :], 0))
            for r in range(16):
                for c in range(4):
                    qchunks.append((w2[r * 128:(r + 1) * 128, c * 2048:(c + 1) * 2048],
                                    w2qS[r * 128:(r + 1) * 128, c * 2048:(c + 1) * 2048], 2))
            qs = dict(i=0, free=[(CV, 0), (CP, 0)], outlq=[0, 0], laststore=0,
                      abuf_lq=[0, 0], sq_ca=0, w1done=0, w2done=0)

            def emit_quant_chunk():
                if qs['i'] >= len(qchunks):
                    return
                src, dst, tsel = qchunks[qs['i']]
                par = qs['i'] % 2
                role = 'vector'
                CX = CV
                wf = xb_s[:, par * 2048:(par + 1) * 2048]
                abuf = lnb_s[:, par * 2048:(par + 1) * 2048]
                obuf = sqd_s[:] if par == 0 else qC_s[:]
                lq1 = em.ins('sync', lambda e: e.dma_start(out=wf, in_=src),
                             waits=[qs['free'][par]], inc=LQ, inc_n=16)
                em.ins(role, lambda e: e.tensor_scalar(
                    obuf, wf, thr_s[:, tsel + 1:tsel + 2], 1.0, ALU.is_ge, ALU.subtract),
                    waits=[(LQ, lq1), (CV, cv_thr), (LQ, qs['outlq'][par]),
                           (CA, qs['sq_ca'])], inc=CX)
                c2x = em.ins(role, lambda e: e.tensor_scalar(
                    abuf, wf, thr_s[:, tsel:tsel + 1], None, ALU.is_ge),
                    waits=[(LQ, qs['abuf_lq'][par])], inc=CX)
                c3 = em.ins('gpsimd', lambda e: e.tensor_tensor(
                    obuf, obuf, abuf, ALU.add), waits=[(CV, c2x)], inc=CP)
                qs['free'][par] = (CP, c3)
                lqs = em.ins('sync', lambda e: e.dma_start(out=dst, in_=obuf),
                             waits=[(CP, c3)], inc=LQ, inc_n=16)
                qs['outlq'][par] = lqs
                qs['laststore'] = lqs
                if qs['i'] == 63:
                    qs['w1done'] = lqs
                if qs['i'] == len(qchunks) - 1:
                    qs['w2done'] = lqs
                qs['i'] += 1

            p25_state = dict(lq_xm_prev=0, cv_xm=0)
            p25_done = set()

            def emit_p25_tile(l):
                nonlocal sumsq_free
                p25_done.add(l)
                g = l // 4
                lqa = em.ins('sync', lambda e: e.dma_start(
                    out=xb_s[:, 0:2048], in_=attnR[l * 128:(l + 1) * 128, :]),
                    waits=[(CCg[g], 1), qs['free'][0],
                           (LQ, p25_state['lq_xm_prev'])],
                    inc=LQ, inc_n=16)
                lqb = em.ins('sync', lambda e: e.dma_start(
                    out=xb_s[:, 2048:4096], in_=xown[l * 128:(l + 1) * 128, :]),
                    waits=[qs['free'][1]], inc=LQ, inc_n=16)
                em.ins('vector', lambda e: e.reduce_sum(
                    stat_s[:, 3:4], xb_s[:, 0:2048], axis=AX.X),
                    waits=[(LQ, lqa)], inc=CV)
                em.fence('vector', CV)
                em.ins('vector', lambda e: e.tensor_scalar(
                    stat_s[:, 0:1], stat_s[:, 3:4], 1.0 / 2048, None, ALU.mult), inc=CV)
                caq = em.ins('scalar', lambda e: e.activation(
                    sqd_s[:], xb_s[:, 0:2048], AF.Square, accum_out=stat_s[:, 4:5]),
                    waits=[(LQ, lqa), (LQ, qs['laststore']), (CV, sumsq_free)], inc=CA)
                cv1 = em.ins('vector', lambda e: e.tensor_scalar(
                    stat_s[:, 6:7], stat_s[:, 4:5], 1.0 / 2048, EPS, ALU.mult, ALU.add),
                    waits=[(CA, caq)], inc=CV)
                sumsq_free = cv1
                em.ins('vector', lambda e: e.tensor_tensor(
                    stat_s[:, 7:8], stat_s[:, 0:1], stat_s[:, 0:1], ALU.mult), inc=CV)
                em.fence('vector', CV)
                cv3 = em.ins('vector', lambda e: e.tensor_tensor(
                    stat_s[:, 6:7], stat_s[:, 6:7], stat_s[:, 7:8], ALU.subtract), inc=CV)
                em.ins('scalar', lambda e: e.activation(
                    stat_s[:, 8:9], stat_s[:, 6:7], AF.Ln), waits=[(CV, cv3)], inc=CA)
                ca2 = em.ins('scalar', lambda e: e.activation(
                    stat_s[:, 1:2], stat_s[:, 8:9], AF.Exp, scale=-0.5), inc=CA)
                cv4 = em.ins('vector', lambda e: e.tensor_scalar(
                    stat_s[:, 2:3], stat_s[:, 0:1], stat_s[:, 1:2], -1.0,
                    ALU.mult, ALU.mult), waits=[(CA, ca2)], inc=CV)
                em.fence('vector', CV)
                em.ins('vector', lambda e: e.tensor_scalar(
                    xb_s[:, 0:2048], xb_s[:, 0:2048], stat_s[:, 1:2], stat_s[:, 2:3],
                    ALU.mult, ALU.add), waits=[(CV, cv4)], inc=CV)
                cv_xm = em.ins('vector', lambda e: e.tensor_tensor(
                    xb_s[:, 0:2048], xb_s[:, 0:2048], xb_s[:, 2048:4096], ALU.add),
                    waits=[(LQ, lqb)], inc=CV)
                p25_state['cv_xm'] = cv_xm
                p25_state['lq_xm_prev'] = em.ins('sync', lambda e: e.dma_start(
                    out=xmS[l * 128:(l + 1) * 128, :], in_=xb_s[:, 0:2048]),
                    waits=[(CV, cv_xm)], inc=LQ, inc_n=16)

            bounce_tiles = []
            for f0 in range(8):
                for d0 in range(16):
                    bounce_tiles.append((
                        w1qS[f0 * 1024:(f0 + 1) * 1024, d0 * 128:(d0 + 1) * 128],
                        w1qT[d0 * 128:(d0 + 1) * 128, f0 * 1024:(f0 + 1) * 1024], 0))
            for f0 in range(64):
                for m0 in range(2):
                    bounce_tiles.append((
                        w2qS[m0 * 1024:(m0 + 1) * 1024, f0 * 128:(f0 + 1) * 128],
                        w2qT[f0 * 128:(f0 + 1) * 128, m0 * 1024:(m0 + 1) * 1024], 1))
            bounce_state = dict(i=0, lq=0, plq=[0, 0])

            def emit_bounce_pair():
                bi = bounce_state['i']
                if bi >= len(bounce_tiles):
                    return
                bsrc, bdst, m = bounce_tiles[bi]
                par = bi % 2
                gate = qs['w1done'] if m == 0 else qs['w2done']
                lsb = em.ins('sync', lambda e: e.dma_start(
                    out=w2qT_s[:, par * 1024:(par + 1) * 1024], in_=bsrc,
                    transpose=True),
                    waits=[(LQ, gate), (LQ, bounce_state['plq'][par])],
                    inc=LQ, inc_n=16)
                bounce_state['plq'][par] = em.ins('sync', lambda e: e.dma_start(
                    out=bdst, in_=w2qT_s[:, par * 1024:(par + 1) * 1024]),
                    waits=[(LQ, lsb)], inc=LQ, inc_n=16)
                bounce_state['lq'] = bounce_state['plq'][par]
                bounce_state['i'] = bi + 1

            qs['abuf_lq'] = [lnb_lq[0], lnb_lq[1]]
            qs['sq_ca'] = CA.v
            cv_p1 = CV.v
            em.wait('tensor', CV, cv_p1)
            em.wait('scalar', CV, cv_p1)
            em.wait('scalar', CP, CP.v)

            PSA = [PS[0], PS[1], PS[5], PS[6]]
            psa_free = [0, 0, 0, 0]
            pt_free = [0] * 8
            cv_t4_par = [0, 0]
            ct_t4z_prev = 0
            ct_p1_stripes = CT.v
            cv_onorm_prev = 0
            cv_evac_prev = 0
            cv_evac_pair = [0, 0]
            ca_rb_prev = 0
            ct_B_prev = 0
            ct_wo_prev = 0
            cv_zp_prev = 0
            lq_evac_prev = 0
            ca_last_exp = 0
            for t in range(NT):
                ct_pv_last = 0
                nk = t + 1
                ngrp = (nk + 3) // 4
                exp_of = {}
                exp_last_grp = {-1: ca_last_exp}
                for g in range(ngrp + 1):
                    if g < ngrp:
                        kts = range(4 * g, min(4 * g + 4, nk))
                        for kt in kts:
                            bank = kt % 4
                            wS = [(CA, psa_free[bank])]
                            ctS = em.ins('tensor', lambda e: e.matmul(
                                PSA[bank][:], kT_s[:, kt * 128:(kt + 1) * 128],
                                qT_r[:, :, t * 128:(t + 1) * 128],
                                start=True, stop=True), waits=wS, inc=CT)
                            if kt == t:
                                cvtri = em.ins('vector', lambda e: e.tensor_tensor(
                                    PSA[bank][:], PSA[bank][:], tri_s[:], ALU.add),
                                    waits=[(CT, ctS)], inc=CV)
                                exp_of[kt] = ('cv', cvtri)
                            else:
                                exp_of[kt] = ('ct', ctS)
                    if g >= 1:
                        gp = (g - 1) % 2
                        hb2 = gp * 2048
                        kts = range(4 * (g - 1), min(4 * (g - 1) + 4, nk))
                        first = True
                        for kt in kts:
                            bank = kt % 4
                            wpv = [(CA, exp_last_grp[g - 1])] if first else []
                            first = False
                            if kt == 0:
                                wpv.append((CV, cv_onorm_prev))
                            ctP = em.ins('tensor', lambda e: e.matmul(
                                PS[2][:], v_s[:, kt * 128:(kt + 1) * 128],
                                pT_s[:, hb2 + bank * 512:hb2 + (bank + 1) * 512],
                                start=(kt == 0), stop=(kt == t)), waits=wpv, inc=CT)
                            ct_pv_last = ctP
                            pt_free[gp * 4 + bank] = ctP
                        g0 = 4 * (g - 1)
                        w = min(4, nk - g0)
                        if g == 1:
                            cv_t4 = em.ins('vector', lambda e: e.tensor_copy(
                                T4_s[:, 0:w * 512], pT_s[:, hb2:hb2 + w * 512]),
                                waits=[(CA, exp_last_grp[g - 1]), (CT, ct_t4z_prev),
                                       (CT, ct_p1_stripes)], inc=CV)
                            if w < 4:
                                cv_t4 = em.ins('vector', lambda e: e.memset(
                                    T4_s[:, w * 512:2048], 0.0), inc=CV)
                        else:
                            cv_t4 = em.ins('vector', lambda e: e.tensor_tensor(
                                T4_s[:, 0:w * 512], T4_s[:, 0:w * 512],
                                pT_s[:, hb2:hb2 + w * 512], ALU.add),
                                waits=[(CA, exp_last_grp[g - 1])], inc=CV)
                        cv_t4_par[gp] = cv_t4
                    if g < ngrp:
                        kts = range(4 * g, min(4 * g + 4, nk))
                        for kt in kts:
                            bank = kt % 4
                            kind, val = exp_of[kt]
                            wexp = [(CV, val)] if kind == 'cv' else [(CT, val)]
                            gp2 = g % 2
                            caE = em.ins('scalar', lambda e: e.activation(
                                pT_s[:, gp2 * 2048 + bank * 512:
                                     gp2 * 2048 + (bank + 1) * 512], PSA[bank][:],
                                AF.Exp, bias=mb_s[:, kt:kt + 1], scale=SCALE),
                                waits=wexp + [(CT, pt_free[gp2 * 4 + bank]),
                                              (CV, cv_t4_par[gp2])], inc=CA)
                            psa_free[bank] = caE
                            ca_last_exp = caE
                        exp_last_grp[g] = ca_last_exp

                cv_t4b = em.ins('vector', lambda e: e.tensor_copy(
                    T4b_s[:], T4_s[:]), waits=[(CT, ct_t4z_prev)], inc=CV)
                nzm = min(4, nk)
                for j in range(nzm):
                    wz = [(CV, cv_t4b)]
                    if j == 0:
                        wz.append((CV, cv_zp_prev))
                    em.ins('tensor', lambda e: e.matmul(
                        PS[3][0:1, :], ones_cb[:], T4b_s[:, j * 512:(j + 1) * 512],
                        start=(j == 0), stop=(j == nzm - 1)), waits=wz, inc=CT)
                ct_t4z_prev = CT.v
                cv_zp = em.ins('vector', lambda e: e.tensor_scalar(
                    rz_s[0:1, :], PS[3][0:1, :], 1e-30, None, ALU.add),
                    waits=[(CT, ct_t4z_prev), (CA, ca_rb_prev), (LQ, lq_evac_prev)],
                    inc=CV)
                cv_zp_prev = cv_zp
                em.fence('vector', CV)
                em.ins('vector', lambda e: e.reciprocal(rz_s[0:1, :], rz_s[0:1, :]),
                       inc=CV)
                em.fence('vector', CV)
                cv_rzb = em.ins('vector', lambda e: e.tensor_copy(
                    rzb_s[0:1, :], rz_s[0:1, :]),
                    waits=[(CT, ct_B_prev), (CT, ct_wo_prev)], inc=CV)
                ctB = em.ins('tensor', lambda e: e.matmul(
                    PS[4][:], ones_rb[0:1, :], rzb_s[0:1, :], start=True, stop=True),
                    waits=[(CV, cv_rzb), (CA, ca_rb_prev), (CV, cv_evac_prev)], inc=CT)
                ct_B_prev = ctB
                ca_rb = em.ins('scalar', lambda e: e.copy(rb_s[:], PS[4][:]),
                               waits=[(CT, ctB), (CV, cv_onorm_prev)], inc=CA)
                ca_rb_prev = ca_rb
                cv_on = em.ins('vector', lambda e: e.tensor_tensor(
                    onorm_s[:], PS[2][:], rb_s[:], ALU.mult),
                    waits=[(CA, ca_rb), (CT, ct_pv_last)], inc=CV)
                cv_onorm_prev = cv_on

                WOB = [PS7, PS[4]]
                for pair in range(2):
                    for hl in range(4):
                        for oc2 in range(2):
                            oc = pair * 2 + oc2
                            w = [(CV, cv_on), (LG, lg_wo)]
                            if hl == 0:
                                w.append((CV, cv_evac_pair[1] if pair == 0
                                          else cv_evac_pair[0]))
                            em.ins('tensor', lambda e: e.matmul(
                                WOB[oc2][:], onorm_s[:, hl * 128:(hl + 1) * 128],
                                wo_s[:, hl * 2048 + oc * 512:hl * 2048 + (oc + 1) * 512],
                                start=(hl == 0), stop=(hl == 3)), waits=w, inc=CT)
                    ct_wo = CT.v
                    ct_wo_prev = ct_wo
                    for oc2 in range(2):
                        oc = pair * 2 + oc2
                        cvs = em.ins('vector', lambda e: e.tensor_copy(
                            evac_s[:], WOB[oc2][:]),
                            waits=[(CT, ct_wo), (LQ, lq_evac_prev)], inc=CV)
                        lq_evac_prev = em.ins('sync', lambda e: e.dma_start(
                            out=attnP[t * 128:(t + 1) * 128,
                                      oc * 512:(oc + 1) * 512],
                            in_=evac_s[:]), waits=[(CV, cvs)], inc=LQ, inc_n=16)
                    cv_evac_pair[pair] = cvs
                cv_evac_prev = cvs


                if t >= 8:
                    for _ in range(4):
                        emit_quant_chunk()
                if t >= 40:
                    for _ in range(10):
                        emit_bounce_pair()
                if t in (45, 49, 53):
                    for l in range(4 * ((t - 45) // 4), 4 * ((t - 45) // 4) + 4):
                        emit_p25_tile(l)

                if t % 16 == 15:
                    g = t // 16
                    em.ins('gpsimd', lambda e: e.collective_compute(
                        "ReduceScatter", ALU.add,
                        replica_groups=[[0, 1, 2, 3], [4, 5, 6, 7]],
                        ins=[attnP[g * 2048:(g + 1) * 2048, :].opt()],
                        outs=[attnR[g * 512:(g + 1) * 512, :].opt()]),
                        waits=[(LQ, LQ.v)], inc=CCg[g])

            while qs['i'] < len(qchunks):
                emit_quant_chunk()
            while bounce_state['i'] < len(bounce_tiles):
                emit_bounce_pair()
            lq_bounce_all = bounce_state['lq']
            for l in range(16):
                if l not in p25_done:
                    emit_p25_tile(l)
            lq_xm_all = p25_state['lq_xm_prev']

            ln3_lq = [0, 0]
            for l in range(16):
                par = l % 2
                can, lqs = emit_ln_tile(
                    xmS[l * 128:(l + 1) * 128, :], ln3S[l * 128:(l + 1) * 128, :],
                    par, par,
                    load_waits=[(LQ, lq_xm_all)],
                    norm_waits=[(LQ, ln3_lq[par]), qs['free'][par]])
                ln3_lq[par] = lqs
            lq_ln3_all = LQ.v

            lqb2 = em.ins('sync', lambda e: e.dma_start(out=b2r_s[0:1, :], in_=b2row),
                          waits=[(CT, CT.v)], inc=LQ, inc_n=16)
            cv_now = CV.v
            ca_now = CA.v
            for ocb in range(4):
                ctb = em.ins('tensor', lambda e: e.matmul(
                    PS[ocb][:], ones_rf[0:1, :], b2r_s[0:1, ocb * 512:(ocb + 1) * 512],
                    start=True, stop=True),
                    waits=[(LQ, lqb2), (CV, cv_now), (CA, ca_now)], inc=CT)
                em.ins('vector', lambda e: e.tensor_copy(
                    b2b_s[:, ocb * 512:(ocb + 1) * 512], PS[ocb][:]),
                    waits=[(CT, ctb)], inc=CV)
            cv_b2b = CV.v

            ct_f1_half = [CT.v, CT.v]
            ct_w2blk = [0, 0]
            ca_hs = [0, 0]
            cv_out2 = cv_b2b
            cv_final = 0
            lq_out_last = 0
            for sg2 in range(4):
                ls_ln3 = 0
                for dc in range(16):
                    ls_ln3 = em.ins('scalar', lambda e: e.dma_start(
                        out=ln3T_s[:, dc * 512:(dc + 1) * 512],
                        in_=ln3S[sg2 * 512:(sg2 + 1) * 512, dc * 128:(dc + 1) * 128],
                        transpose=True),
                        waits=[(LQ, lq_ln3_all), (CT, ct_f1_half[0])],
                        inc=LS, inc_n=16)

                w1qT_r = w1qT.rearrange("(dc p) f -> p dc f", p=128)
                for fbg in range(64):
                    wpar = fbg % 2
                    lss = em.ins('sync', lambda e: e.dma_start(
                        out=w1b_s[:, wpar * 2048:(wpar + 1) * 2048],
                        in_=w1qT_r[:, :, fbg * 128:(fbg + 1) * 128]),
                        waits=[(LQ, lq_bounce_all), (CT, ct_f1_half[wpar])],
                        inc=LQ, inc_n=16)
                    pb = fbg % 2
                    for dc in range(16):
                        w = [(LQ, lss), (LS, ls_ln3)]
                        if dc == 0:
                            w.append((CA, ca_hs[pb]))
                            if fbg < 2 and sg2 > 0:
                                w.append((CV, cv_out2))
                        em.ins('tensor', lambda e: e.matmul(
                            PS[pb][:],
                            w1b_s[:, wpar * 2048 + dc * 128:
                                  wpar * 2048 + (dc + 1) * 128],
                            ln3T_s[:, dc * 512:(dc + 1) * 512],
                            start=(dc == 0), stop=(dc == 15)), waits=w, inc=CT)
                    ct_f1_half[wpar] = CT.v
                    ca_hs[pb] = em.ins('scalar', lambda e: e.activation(
                        Hs_s[:, fbg * 512:(fbg + 1) * 512], PS[pb][:], AF.Gelu,
                        bias=b1c_s[:, fbg:fbg + 1]),
                        waits=[(CT, CT.v), (CV, cv_b1)], inc=CA)
                ca_f1_done = max(ca_hs)

                for dmh in range(2):
                    for fbp in range(32):
                        par = fbp % 2
                        wbuf = w2qT_s if par == 0 else w2p_s
                        lsw = em.ins('scalar', lambda e: e.dma_start(
                            out=wbuf.rearrange("p (two m) -> p two m", two=2),
                            in_=w2qT[fbp * 256:(fbp + 1) * 256,
                                     dmh * 1024:(dmh + 1) * 1024].rearrange(
                                         "(two p) m -> p two m", p=128)),
                            waits=[(LQ, lq_bounce_all), (CT, ct_w2blk[par])],
                            inc=LS, inc_n=16)
                        for half in range(2):
                            fb = fbp * 2 + half
                            for ts in range(4):
                                for oc2 in range(2):
                                    bi = ts * 2 + oc2
                                    bank = PS[bi] if bi < 7 else PS7
                                    w = [(LS, lsw), (CA, ca_f1_done)]
                                    if fb == 0:
                                        w.append((CV, cv_out2))
                                    em.ins('tensor', lambda e: e.matmul(
                                        bank[:],
                                        Hs_s[:, fb * 512 + ts * 128:
                                             fb * 512 + (ts + 1) * 128],
                                        wbuf[:, half * 1024 + oc2 * 512:
                                             half * 1024 + (oc2 + 1) * 512],
                                        start=(fb == 0), stop=(fb == 63)),
                                        waits=w, inc=CT)
                        ct_w2blk[par] = CT.v
                    ct_f2 = CT.v
                    for ts in range(4):
                        for oc2 in range(2):
                            bi = ts * 2 + oc2
                            bank = PS[bi] if bi < 7 else PS7
                            cv_out2 = em.ins('vector', lambda e: e.tensor_tensor(
                                out2_s[:, ts * 2048 + dmh * 1024 + oc2 * 512:
                                       ts * 2048 + dmh * 1024 + (oc2 + 1) * 512],
                                bank[:],
                                b2b_s[:, dmh * 1024 + oc2 * 512:
                                      dmh * 1024 + (oc2 + 1) * 512],
                                ALU.add), waits=[(CT, ct_f2), (CV, cv_b2b),
                                                 (CV, cv_final)], inc=CV)

                for ts in range(4):
                    l = sg2 * 4 + ts
                    base = ts * 2048
                    lqx = em.ins('sync', lambda e: e.dma_start(
                        out=xb_s[:, 0:2048], in_=xmS[l * 128:(l + 1) * 128, :]),
                        waits=[(LQ, lq_xm_all), (CA, xb_free[0]),
                               (CV, cv_final)], inc=LQ, inc_n=16)
                    xb_free[0] = 0
                    em.ins('vector', lambda e: e.reduce_sum(
                        stat_s[:, 3:4], out2_s[:, base:base + 2048], axis=AX.X),
                        waits=[(CV, cv_out2)], inc=CV)
                    em.fence('vector', CV)
                    em.ins('vector', lambda e: e.tensor_scalar(
                        stat_s[:, 0:1], stat_s[:, 3:4], 1.0 / 2048, None, ALU.mult),
                        inc=CV)
                    caq = em.ins('scalar', lambda e: e.activation(
                        sqd_s[:], out2_s[:, base:base + 2048], AF.Square,
                        accum_out=stat_s[:, 4:5]),
                        waits=[(CV, cv_out2), (CV, sumsq_free)], inc=CA)
                    cv1 = em.ins('vector', lambda e: e.tensor_scalar(
                        stat_s[:, 6:7], stat_s[:, 4:5], 1.0 / 2048, EPS,
                        ALU.mult, ALU.add), waits=[(CA, caq)], inc=CV)
                    sumsq_free = cv1
                    em.ins('vector', lambda e: e.tensor_tensor(
                        stat_s[:, 7:8], stat_s[:, 0:1], stat_s[:, 0:1], ALU.mult),
                        inc=CV)
                    em.fence('vector', CV)
                    cv3 = em.ins('vector', lambda e: e.tensor_tensor(
                        stat_s[:, 6:7], stat_s[:, 6:7], stat_s[:, 7:8], ALU.subtract),
                        inc=CV)
                    em.ins('scalar', lambda e: e.activation(
                        stat_s[:, 8:9], stat_s[:, 6:7], AF.Ln), waits=[(CV, cv3)],
                        inc=CA)
                    ca2 = em.ins('scalar', lambda e: e.activation(
                        stat_s[:, 1:2], stat_s[:, 8:9], AF.Exp, scale=-0.5), inc=CA)
                    cv4 = em.ins('vector', lambda e: e.tensor_scalar(
                        stat_s[:, 2:3], stat_s[:, 0:1], stat_s[:, 1:2], -1.0,
                        ALU.mult, ALU.mult), waits=[(CA, ca2)], inc=CV)
                    em.fence('vector', CV)
                    em.ins('vector', lambda e: e.tensor_scalar(
                        out2_s[:, base:base + 2048], out2_s[:, base:base + 2048],
                        stat_s[:, 1:2], stat_s[:, 2:3], ALU.mult, ALU.add),
                        waits=[(CV, cv4)], inc=CV)
                    cv_final = em.ins('vector', lambda e: e.tensor_tensor(
                        out2_s[:, base:base + 2048], out2_s[:, base:base + 2048],
                        xb_s[:, 0:2048], ALU.add), waits=[(LQ, lqx)], inc=CV)
                    lq_out_last = em.ins('sync', lambda e: e.dma_start(
                        out=out[l * 128:(l + 1) * 128, :],
                        in_=out2_s[:, base:base + 2048]),
                        waits=[(CV, cv_final)], inc=LQ, inc_n=16)

            if debug:
                em.ins('gpsimd', lambda e: e.dma_start(out=d_attnP, in_=attnP),
                       waits=[(LQ, lq_out_last)], inc=LG, inc_n=16)
                em.ins('gpsimd', lambda e: e.dma_start(out=d_attnR, in_=attnR),
                       inc=LG, inc_n=16)
            em.wait('sync', LQ, lq_out_last)
            em.wait('gpsimd', LG, LG.v)
            em.wait('scalar', LS, LS.v)

        @blk.sync
        def _(eng):
            program(Em('sync', eng))

        @blk.gpsimd
        def _(eng):
            program(Em('gpsimd', eng))

        @blk.vector
        def _(eng):
            program(Em('vector', eng))

        @blk.scalar
        def _(eng):
            program(Em('scalar', eng))

        @blk.tensor
        def _(eng):
            program(Em('tensor', eng))

    return nc


def make_in_maps(X, Wq, Wk, Wv, Wg, Wd, Wo, w1, b1, w2, b2):
    in_maps = []
    for c in range(8):
        b, n = c // 4, c % 4
        own_rows = np.concatenate(
            [np.arange(g * 2048 + n * 512, g * 2048 + (n + 1) * 512)
             for g in range(4)])
        in_maps.append({
            "x": np.ascontiguousarray(X[b]),
            "xown": np.ascontiguousarray(X[b][own_rows]),
            "wq_g": np.ascontiguousarray(Wq[:, n * 512:(n + 1) * 512]),
            "wk_h": np.ascontiguousarray(Wk[:, n * 128:(n + 1) * 128]),
            "wv_h": np.ascontiguousarray(Wv[:, n * 128:(n + 1) * 128]),
            "wq_full": Wq,
            "wv_full": Wv,
            "wg_n": np.ascontiguousarray(Wg[:, n:n + 1]),
            "wd_n": np.ascontiguousarray(Wd[:, n:n + 1]),
            "wo_r": np.ascontiguousarray(Wo[n * 512:(n + 1) * 512, :]),
            "w1": w1,
            "w2": w2,
            "b1r": np.ascontiguousarray(b1.reshape(64, 128)),
            "b2row": np.ascontiguousarray(b2.reshape(1, 2048)),
        })
    return in_maps


def assemble(results):
    OUT = np.empty((B, S, D), dtype=np.float32)
    for c in range(8):
        b, n = c // 4, c % 4
        o = results[c]["out"]
        for g in range(4):
            OUT[b, g * 2048 + n * 512:g * 2048 + (n + 1) * 512, :] = \
                o[g * 512:(g + 1) * 512, :]
    return OUT


def kernel(L0_attn_hidden_states, L0_attn_Wq, L0_attn_Wk, L0_attn_Wv,
           L0_attn_Wg, L0_attn_Wd, L0_attn_Wo, L0_ffn_w1, L0_ffn_b1,
           L0_ffn_w2, L0_ffn_b2):
    f = lambda a: np.ascontiguousarray(np.asarray(a, dtype=np.float32))
    X = f(L0_attn_hidden_states)
    in_maps = make_in_maps(X, f(L0_attn_Wq), f(L0_attn_Wk), f(L0_attn_Wv),
                           f(L0_attn_Wg), f(L0_attn_Wd), f(L0_attn_Wo),
                           f(L0_ffn_w1), f(L0_ffn_b1), f(L0_ffn_w2),
                           f(L0_ffn_b2))
    nc = build_program(debug=False)
    res = run_bass_kernel_spmd(nc, in_maps, core_ids=list(range(8)))
    return assemble(res.results)


# version 17
# speedup vs baseline: 1.2846x; 1.0719x over previous
import numpy as np
import ml_dtypes
import concourse.bass as bass
import concourse.mybir as mybir
from concourse.bass_utils import run_bass_kernel_spmd

F32 = mybir.dt.float32
BF16 = mybir.dt.bfloat16
AF = mybir.ActivationFunctionType
ALU = mybir.AluOpType
AX = mybir.AxisListType
bf16 = ml_dtypes.bfloat16

B, S, D = 2, 8192, 2048
HD = 128
DFF = 8192
NT = S // 128
SCALE = HD ** -0.5
EPS = 1e-5

SG = 16
SGT = 4


class Sem:
    def __init__(self, handle, name):
        self.h = handle
        self.name = name
        self.v = 0

    def bump(self, n):
        self.v += n
        return self.v


class Em:

    def __init__(self, role, engine):
        self.role = role
        self.eng = engine
        self.waited = {}

    def _wait(self, s, v):
        if v <= 0 or self.waited.get(s.name, -1) >= v:
            return
        self.eng.wait_ge(s.h, v)
        self.waited[s.name] = v

    def ins(self, role, fn, waits=(), inc=None, inc_n=1):
        if role == self.role:
            for (s, v) in waits:
                self._wait(s, v)
            inst = fn(self.eng)
            if inc is not None:
                inst.then_inc(inc.h, inc_n)
        if inc is not None:
            return inc.bump(inc_n)
        return None

    def wait(self, role, s, v):
        if role == self.role:
            self._wait(s, v)

    def fence(self, role, s):
        if role == self.role:
            self._wait(s, s.v)


def build_program(debug=False):
    nc = bass.Bass("TRN2", target_bir_lowering=False, debug=False, num_devices=8)

    def inp(name, shape, dt=F32):
        return nc.dram_tensor(name, shape, dt, kind="ExternalInput").ap()

    x = inp("x", [S, D])
    xown = inp("xown", [2048, D])
    wq_g = inp("wq_g", [D, 512])
    wk_h = inp("wk_h", [D, HD])
    wv_h = inp("wv_h", [D, HD])
    wq_full = inp("wq_full", [D, D])
    wv_full = inp("wv_full", [D, 512])
    wg_n = inp("wg_n", [D, 1])
    wd_n = inp("wd_n", [512, 1])
    wo_r = inp("wo_r", [512, D])
    w1 = inp("w1", [DFF, D])
    w2 = inp("w2", [D, DFF])
    b1r = inp("b1r", [64, 128])
    b2row = inp("b2row", [1, D])

    out = nc.dram_tensor("out", [2048, D], F32, kind="ExternalOutput").ap()

    dbg_kind = "ExternalOutput" if debug else "Internal"

    def scr(name, shape, dt):
        return nc.dram_tensor(name, shape, dt, kind=dbg_kind).ap()

    lnS = scr("lnS", [S, D], BF16)
    attnP = nc.dram_tensor("attnP", [S, D], F32, kind="Internal").ap()
    attnR = nc.dram_tensor("attnR", [2048, D], F32, kind="Internal").ap()
    xmS = scr("xmS", [2048, D], F32)
    ln3S = scr("ln3S", [2048, D], BF16)
    w1qS = scr("w1qS", [DFF, D], BF16)
    w2qS = scr("w2qS", [D, DFF], BF16)
    w1qT = scr("w1qT", [D, DFF], BF16)
    w2qT = scr("w2qT", [DFF, D], BF16)
    wqbfS = nc.dram_tensor("wqbfS", [D, D], BF16, kind="Internal").ap()
    wvbfS = nc.dram_tensor("wvbfS", [D, 512], BF16, kind="Internal").ap()
    if debug:
        d_kT = nc.dram_tensor("d_kT", [128, S], BF16, kind="ExternalOutput").ap()
        d_v = nc.dram_tensor("d_v", [128, S], BF16, kind="ExternalOutput").ap()
        d_qT = nc.dram_tensor("d_qT", [128, 4 * S], BF16, kind="ExternalOutput").ap()
        d_mb = nc.dram_tensor("d_mb", [128, 64], F32, kind="ExternalOutput").ap()
        d_thr = nc.dram_tensor("d_thr", [128, 4], F32, kind="ExternalOutput").ap()
        d_abn = nc.dram_tensor("d_abn", [128, 32], mybir.dt.bfloat16, kind="ExternalOutput").ap()
        d_attnP = nc.dram_tensor("d_attnP", [S, D], F32, kind="ExternalOutput").ap()
        d_attnR = nc.dram_tensor("d_attnR", [2048, D], F32, kind="ExternalOutput").ap()

    identB = nc.inline_tensor(np.eye(128, dtype=bf16), name="identB")
    identF = nc.inline_tensor(np.eye(64, dtype=np.float32), name="identF")
    tq = np.arange(128)
    tri_blk = np.where(tq[None, :] >= tq[:, None], 0.0, -1e32).astype(bf16)
    triC = nc.inline_tensor(np.tile(tri_blk, (1, 4)), name="triC")

    def sb(name, shape, dt):
        return nc.alloc_sbuf_tensor(name, shape, dt).ap()

    def sb_at(name, shape, dt, base_of, byte_off=0):
        off = nc.lookup_mloc(base_of.tensor).addr + byte_off
        return nc.alloc_sbuf_tensor_at(name, shape, dt, offset=off).ap()

    identB_s = sb("identB_s", [128, 128], BF16)
    identF_s = sb("identF_s", [64, 64], F32)
    tri_s = sb("tri_s", [128, 512], BF16)
    ones_cb = sb("ones_cb", [128, 1], BF16)
    ones_rb = sb("ones_rb", [1, 128], BF16)
    ones_rf = sb("ones_rf", [1, 128], F32)
    wkh_s = sb("wkh_s", [128, 16 * 128], BF16)
    wvh_s = sb("wvh_s", [128, 16 * 128], BF16)
    wqg_s = sb("wqg_s", [128, 16 * 512], BF16)
    wgn_s = sb("wgn_s", [128, 16], BF16)
    wdn_s = sb("wdn_s", [128, 4], BF16)
    abn_s = sb("abn_s", [128, 32], BF16)
    kT_s = sb("kT_s", [128, S], BF16)
    v_s = sb("v_s", [128, S], BF16)
    qT_s = sb("qT_s", [128, 4 * S], BF16)
    stripes = sb("stripes", [128, 16 * 1024], BF16)
    w1b_s = sb("w1b_s", [128, 2 * 2048], BF16)
    mb_s = sb("mb_s", [128, 64], F32)
    gdc_s = sb("gdc_s", [128, 16], F32)
    xb_s = sb("xb_s", [128, 2 * 2048], F32)
    lnb_s = sb("lnb_s", [128, 2 * 2048], BF16)
    sqd_s = sb("sqd_s", [128, 2048], BF16)
    pT_s = sb("pT_s", [128, 8 * 512], BF16)
    onorm_s = sb("onorm_s", [128, 512], BF16)
    rb_s = sb("rb_s", [128, 512], F32)
    w2qT_s = sb("w2qT_s", [128, 2 * 1024], BF16)
    b1c_s = sb("b1c_s", [128, 64], F32)
    b1r_s = sb("b1r_s", [64, 128], F32)
    stat_s = sb("stat_s", [128, 40], F32)
    absacc_v = sb("absacc_v", [128, 2], F32)
    absacc_p = sb("absacc_p", [128, 2], F32)
    thr_s = sb("thr_s", [128, 4], F32)
    trow_s = sb("trow_s", [1, 4], F32)

    rem = nc.sbuf_bytes_remaining
    assert rem >= 0, f"SBUF over budget: {rem}"

    b2b_s = sb_at("b2b_s", [128, 2048], F32, wkh_s)
    ln3T_s = sb_at("ln3T_s", [128, 16 * 512], BF16, wqg_s)
    out2_s = sb_at("out2_s", [128, 8192], F32, kT_s)
    Hs_s = sb_at("Hs_s", [128, 64 * 512], BF16, qT_s)
    wo_s = sb_at("wo_s", [128, 4 * 2048], BF16, stripes)
    b2r_s = sb_at("b2r_s", [1, 2048], F32, stripes, byte_off=16384)
    qC_s = sb_at("qC_s", [128, 2048], BF16, w1b_s)
    evac_s = sb_at("evac_s", [128, 512], F32, rb_s)
    rz_s = sb_at("rz_s", [1, 512], F32, rb_s)
    rzb_s = sb_at("rzb_s", [1, 512], BF16, onorm_s)
    absb_s = sb_at("absb_s", [128, 2048], BF16, pT_s)
    T4_s = sb_at("T4_s", [128, 2048], F32, stripes, byte_off=16384)
    T4b_s = sb_at("T4b_s", [128, 2048], BF16, stripes, byte_off=24576)
    w2p_s = sb_at("w2p_s", [128, 2048], BF16, stripes, byte_off=28672)
    assert nc.lookup_mloc(v_s.tensor).addr == nc.lookup_mloc(kT_s.tensor).addr + S * 2
    assert nc.lookup_mloc(wvh_s.tensor).addr == nc.lookup_mloc(wkh_s.tensor).addr + 16 * 128 * 2

    PS = [nc.alloc_psum_tensor(f"ps{i}", [128, 512], F32).ap() for i in range(7)]
    PS7 = nc.alloc_psum_tensor("ps7", [128, 512], F32).ap()

    qT_r = qT_s.rearrange("p (c u) -> p c u", c=4)

    with (
        nc.semaphore("LQ") as hLQ,
        nc.semaphore("LS") as hLS,
        nc.semaphore("LG") as hLG,
        nc.semaphore("CT") as hCT,
        nc.semaphore("CV") as hCV,
        nc.semaphore("CA") as hCA,
        nc.semaphore("CP") as hCP,
        nc.semaphore("CC0") as hCC0,
        nc.semaphore("CC1") as hCC1,
        nc.semaphore("CC2") as hCC2,
        nc.semaphore("CC3") as hCC3,
        nc.Block() as blk,
    ):
        handles = dict(LQ=hLQ, LS=hLS, LG=hLG, CT=hCT, CV=hCV, CA=hCA,
                       CP=hCP, CC0=hCC0, CC1=hCC1, CC2=hCC2, CC3=hCC3)

        def program(em):
            sm = {k: Sem(h, k) for k, h in handles.items()}
            LQ, LS, LG = sm["LQ"], sm["LS"], sm["LG"]
            CT, CV, CA, CP = sm["CT"], sm["CV"], sm["CA"], sm["CP"]
            CCg = [sm["CC0"], sm["CC1"], sm["CC2"], sm["CC3"]]

            st = {}

            em.ins('vector', lambda e: e.memset(ones_cb[:], 1.0), inc=CV)
            em.ins('vector', lambda e: e.memset(ones_rb[:], 1.0), inc=CV)
            em.ins('vector', lambda e: e.memset(ones_rf[:], 1.0), inc=CV)
            em.ins('vector', lambda e: e.memset(stat_s[:, 35:36], 1.0), inc=CV)
            v_ones = CV.v

            em.ins('sync', lambda e: e.dma_start(out=identB_s[:], in_=identB.ap()), inc=LQ, inc_n=16)
            em.ins('sync', lambda e: e.dma_start(out=identF_s[:], in_=identF.ap()), inc=LQ, inc_n=16)
            em.ins('sync', lambda e: e.dma_start(out=tri_s[:], in_=triC.ap()), inc=LQ, inc_n=16)
            em.ins('sync', lambda e: e.dma_start(out=b1r_s[:], in_=b1r), inc=LQ, inc_n=16)
            lq_consts = LQ.v

            em.ins('gpsimd', lambda e: e.dma_start(
                out=wkh_s.rearrange("p (dc c) -> p dc c", dc=16), in_=wk_h.rearrange("(dc p) c -> p dc c", p=128)), inc=LG, inc_n=16)
            em.ins('gpsimd', lambda e: e.dma_start(
                out=wvh_s.rearrange("p (dc c) -> p dc c", dc=16), in_=wv_h.rearrange("(dc p) c -> p dc c", p=128)), inc=LG, inc_n=16)
            em.ins('gpsimd', lambda e: e.dma_start(
                out=wqg_s.rearrange("p (dc c) -> p dc c", dc=16), in_=wq_g.rearrange("(dc p) c -> p dc c", p=128)), inc=LG, inc_n=16)
            em.ins('gpsimd', lambda e: e.dma_start(
                out=wgn_s.rearrange("p (cc o) -> p cc o", cc=16),
                in_=wg_n.rearrange("(cc p) o -> p cc o", p=128),
                allow_slow_non_contiguous=True), inc=LG, inc_n=16)
            em.ins('gpsimd', lambda e: e.dma_start(
                out=wdn_s.rearrange("p (cc o) -> p cc o", cc=4),
                in_=wd_n.rearrange("(cc p) o -> p cc o", p=128),
                allow_slow_non_contiguous=True), inc=LG, inc_n=16)
            em.ins('gpsimd', lambda e: e.dma_start(out=wqbfS, in_=wq_full),
                   inc=LG, inc_n=16)
            em.ins('gpsimd', lambda e: e.dma_start(out=wvbfS, in_=wv_full),
                   inc=LG, inc_n=16)
            lg_p0w = LG.v

            ctb1 = em.ins('tensor', lambda e: e.matmul(
                PS[6][0:128, 0:64], b1r_s[:], identF_s[:],
                start=True, stop=True),
                waits=[(LQ, lq_consts)], inc=CT)
            cv_b1 = em.ins('vector', lambda e: e.tensor_copy(
                b1c_s[:], PS[6][0:128, 0:64]), waits=[(CT, ctb1)], inc=CV)

            ct_abp = 0
            cv_abn_evac = 0
            for dh in range(2):
                lsx = 0
                for cc in range(16):
                    lsx = em.ins('scalar', lambda e: e.dma_start(
                        out=stripes[:, cc * 1024:(cc + 1) * 1024],
                        in_=wqbfS[dh * 1024:(dh + 1) * 1024, cc * 128:(cc + 1) * 128],
                        transpose=True),
                        waits=[(LG, lg_p0w), (CT, ct_abp)], inc=LS, inc_n=16)
                for d8 in range(8):
                    dc = dh * 8 + d8
                    for cc in range(16):
                        w5 = [(LS, lsx)]
                        if cc == 0:
                            w5.append((CV, cv_abn_evac))
                        em.ins('tensor', lambda e: e.matmul(
                            PS[5][:, 0:1],
                            stripes[:, cc * 1024 + d8 * 128:cc * 1024 + (d8 + 1) * 128],
                            wgn_s[:, cc:cc + 1], start=(cc == 0), stop=(cc == 15)),
                            waits=w5, inc=CT)
                    cv_abn_evac = em.ins('vector', lambda e: e.tensor_copy(
                        abn_s[:, dc * 2:dc * 2 + 1], PS[5][:, 0:1]),
                        waits=[(CT, CT.v)], inc=CV)
                ct_abp = CT.v
            lsx = 0
            for vc in range(4):
                lsx = em.ins('scalar', lambda e: e.dma_start(
                    out=stripes[:, vc * 2048:(vc + 1) * 2048],
                    in_=wvbfS[:, vc * 128:(vc + 1) * 128], transpose=True),
                    waits=[(LG, lg_p0w), (CT, ct_abp)], inc=LS, inc_n=16)
            for dc in range(16):
                for vc in range(4):
                    w5 = [(LS, lsx)]
                    if vc == 0:
                        w5.append((CV, cv_abn_evac))
                    em.ins('tensor', lambda e: e.matmul(
                        PS[5][:, 0:1],
                        stripes[:, vc * 2048 + dc * 128:vc * 2048 + (dc + 1) * 128],
                        wdn_s[:, vc:vc + 1], start=(vc == 0), stop=(vc == 3)),
                        waits=w5, inc=CT)
                cv_abn_evac = em.ins('vector', lambda e: e.tensor_copy(
                    abn_s[:, dc * 2 + 1:dc * 2 + 2], PS[5][:, 0:1]),
                    waits=[(CT, CT.v)], inc=CV)
            ct_abp = CT.v
            cv_ab = CV.v
            ct_ab_final = CT.v
            cv_abn_done = CV.v

            xb_free = [0, 0]
            sumsq_free = 0
            sumsq_free_p = [0, 0]

            def emit_ln_tile(src_ap, dst_ap, xpar, lpar, load_waits, norm_waits):
                sb0 = xpar * 10
                lq1 = em.ins('sync', lambda e: e.dma_start(
                    out=xb_s[:, xpar * 2048:(xpar + 1) * 2048], in_=src_ap),
                    waits=list(load_waits) + [(CA, xb_free[xpar])], inc=LQ, inc_n=16)
                em.ins('vector', lambda e: e.reduce_sum(
                    stat_s[:, sb0 + 3:sb0 + 4], xb_s[:, xpar * 2048:(xpar + 1) * 2048],
                    axis=AX.X), waits=[(LQ, lq1)], inc=CV)
                em.fence('vector', CV)
                em.ins('vector', lambda e: e.tensor_scalar(
                    stat_s[:, sb0:sb0 + 1], stat_s[:, sb0 + 3:sb0 + 4], 1.0 / 2048,
                    None, ALU.mult), inc=CV)
                caq = em.ins('scalar', lambda e: e.activation(
                    sqd_s[:, xpar * 1024:(xpar + 1) * 1024],
                    xb_s[:, xpar * 2048:(xpar + 1) * 2048].rearrange(
                        "p (a b) -> p a b", a=2)[:, 0:1, :], AF.Square,
                    accum_out=stat_s[:, sb0 + 4:sb0 + 5]),
                    waits=[(LQ, lq1), (CV, sumsq_free_p[xpar])], inc=CA)
                caq = em.ins('scalar', lambda e: e.activation(
                    sqd_s[:, xpar * 1024:(xpar + 1) * 1024],
                    xb_s[:, xpar * 2048:(xpar + 1) * 2048].rearrange(
                        "p (a b) -> p a b", a=2)[:, 1:2, :], AF.Square,
                    accum_out=stat_s[:, sb0 + 5:sb0 + 6]), inc=CA)
                cv1 = em.ins('vector', lambda e: e.tensor_tensor(
                    stat_s[:, sb0 + 4:sb0 + 5], stat_s[:, sb0 + 4:sb0 + 5],
                    stat_s[:, sb0 + 5:sb0 + 6], ALU.add),
                    waits=[(CA, caq)], inc=CV)
                em.fence('vector', CV)
                cv1 = em.ins('vector', lambda e: e.tensor_scalar(
                    stat_s[:, sb0 + 6:sb0 + 7], stat_s[:, sb0 + 4:sb0 + 5], 1.0 / 2048,
                    EPS, ALU.mult, ALU.add), inc=CV)
                sumsq_free_p[xpar] = cv1
                em.ins('vector', lambda e: e.tensor_tensor(
                    stat_s[:, sb0 + 7:sb0 + 8], stat_s[:, sb0:sb0 + 1],
                    stat_s[:, sb0:sb0 + 1], ALU.mult), inc=CV)
                em.fence('vector', CV)
                cv3 = em.ins('vector', lambda e: e.tensor_tensor(
                    stat_s[:, sb0 + 6:sb0 + 7], stat_s[:, sb0 + 6:sb0 + 7],
                    stat_s[:, sb0 + 7:sb0 + 8], ALU.subtract), inc=CV)
                em.ins('scalar', lambda e: e.activation(
                    stat_s[:, sb0 + 8:sb0 + 9], stat_s[:, sb0 + 6:sb0 + 7], AF.Ln),
                    waits=[(CV, cv3)], inc=CA)
                ca2 = em.ins('scalar', lambda e: e.activation(
                    stat_s[:, sb0 + 1:sb0 + 2], stat_s[:, sb0 + 8:sb0 + 9], AF.Exp,
                    scale=-0.5), inc=CA)
                cv4 = em.ins('vector', lambda e: e.tensor_scalar(
                    stat_s[:, sb0 + 2:sb0 + 3], stat_s[:, sb0:sb0 + 1],
                    stat_s[:, sb0 + 1:sb0 + 2], -1.0,
                    ALU.mult, ALU.mult), waits=[(CA, ca2)], inc=CV)
                can = em.ins('scalar', lambda e: e.activation(
                    lnb_s[:, lpar * 2048:(lpar + 1) * 2048],
                    xb_s[:, xpar * 2048:(xpar + 1) * 2048], AF.Identity,
                    bias=stat_s[:, sb0 + 2:sb0 + 3], scale=stat_s[:, sb0 + 1:sb0 + 2]),
                    waits=[(CV, cv4)] + list(norm_waits), inc=CA)
                xb_free[xpar] = can
                lqs = em.ins('sync', lambda e: e.dma_start(
                    out=dst_ap, in_=lnb_s[:, lpar * 2048:(lpar + 1) * 2048]),
                    waits=[(CA, can)], inc=LQ, inc_n=16)
                return can, lqs

            abs_chunks = []
            for r in range(64):
                for c in range(2):
                    abs_chunks.append((w1[r * 128:(r + 1) * 128, c * 1024:(c + 1) * 1024], 0))
            for r in range(16):
                for c in range(8):
                    abs_chunks.append((w2[r * 128:(r + 1) * 128, c * 1024:(c + 1) * 1024], 1))
            abs_state = dict(i=0, sv=[False, False], sp=[False, False],
                             vfree=0, pfree=0, cvd=0, cpd=0)

            def emit_abs_chunk():
                a = abs_state
                if a['i'] >= len(abs_chunks):
                    return
                src, m = abs_chunks[a['i']]
                par = a['i'] % 2
                prevfree = [(CV, a['vfree'])] if par == 0 else [(CV, a['pfree'])]
                lg1 = em.ins('gpsimd', lambda e: e.dma_start(
                    out=absb_s[:, par * 1024:(par + 1) * 1024], in_=src),
                    waits=prevfree, inc=LG, inc_n=16)
                if par == 0:
                    em.ins('vector', lambda e: e.reduce_sum(
                        stat_s[:, 33:34], absb_s[:, 0:1024],
                        axis=AX.X, apply_absolute_value=True),
                        waits=[(LG, lg1)], inc=CV)
                    em.fence('vector', CV)
                    started = a['sv']
                    acc, scol = absacc_v, 33
                else:
                    em.ins('scalar', lambda e: e.activation(
                        sqd_s[:, 0:1024], absb_s[:, 1024:2048], AF.Abs,
                        accum_out=stat_s[:, 34:35]),
                        waits=[(LG, lg1), (CV, a['pfree'])], inc=CA)
                    em.wait('vector', CA, CA.v)
                    started = a['sp']
                    acc, scol = absacc_p, 34
                if not started[m]:
                    c2 = em.ins('vector', lambda e: e.tensor_copy(
                        acc[:, m:m + 1], stat_s[:, scol:scol + 1]), inc=CV)
                    started[m] = True
                else:
                    c2 = em.ins('vector', lambda e: e.tensor_tensor(
                        acc[:, m:m + 1], acc[:, m:m + 1], stat_s[:, scol:scol + 1],
                        ALU.add), inc=CV)
                if par == 0:
                    a['vfree'] = c2
                    a['cvd'] = c2
                else:
                    a['pfree'] = c2
                    a['cvd'] = c2
                a['i'] += 1

            lnb_lq = [0, 0]
            ca_mb_prev = 0
            cv_evK = 0
            cv_evV = 0
            cv_evG = cv_ab
            cv_evQ = 0
            ct_half = [ct_ab_final, ct_ab_final]
            lq_spill = {}
            for sg in range(SG):
                for tt in range(SGT):
                    t = sg * SGT + tt
                    par = t % 2
                    can, lqs = emit_ln_tile(
                        x[t * 128:(t + 1) * 128, :], lnS[t * 128:(t + 1) * 128, :],
                        par, par,
                        load_waits=[],
                        norm_waits=[(LQ, lnb_lq[par]),
                                    (CT, ct_ab_final if t < 2 else 0)])
                    lnb_lq[par] = lqs
                    for _ in range(4):
                        emit_abs_chunk()
                lq_spill[sg] = LQ.v

            for sg in range(SG):
                half = sg % 2
                hb = half * 8192

                ls_str = 0
                for dc in range(16):
                    ls_str = em.ins('scalar', lambda e: e.dma_start(
                        out=stripes[:, hb + dc * 512:hb + (dc + 1) * 512],
                        in_=lnS[sg * 512:(sg + 1) * 512, dc * 128:(dc + 1) * 128],
                        transpose=True),
                        waits=[(LQ, lq_spill[sg]), (CT, ct_half[half])],
                        inc=LS, inc_n=16)

                def stripe(dc, lo, n):
                    return stripes[:, hb + dc * 512 + lo:hb + dc * 512 + lo + n]

                for dc in range(16):
                    em.ins('tensor', lambda e: e.matmul(
                        PS[0][:], wkh_s[:, dc * 128:(dc + 1) * 128], stripe(dc, 0, 512),
                        start=(dc == 0), stop=(dc == 15)),
                        waits=[(LS, ls_str), (CV, cv_evK), (LG, lg_p0w)], inc=CT)
                cv_evK = em.ins('vector', lambda e: e.tensor_copy(
                    kT_s[:, sg * 512:(sg + 1) * 512], PS[0][:]),
                    waits=[(CT, CT.v)], inc=CV)

                for tt in range(SGT):
                    for dc in range(16):
                        em.ins('tensor', lambda e: e.matmul(
                            PS[1][:, 0:128], stripe(dc, tt * 128, 128),
                            wvh_s[:, dc * 128:(dc + 1) * 128],
                            start=(dc == 0), stop=(dc == 15)),
                            waits=[(CV, cv_evV)], inc=CT)
                    cv_evV = em.ins('vector', lambda e: e.tensor_copy(
                        v_s[:, (sg * SGT + tt) * 128:(sg * SGT + tt + 1) * 128],
                        PS[1][:, 0:128]), waits=[(CT, CT.v)], inc=CV)

                for tt in range(SGT):
                    for dc in range(16):
                        em.ins('tensor', lambda e: e.matmul(
                            PS[3][:, 0:2], stripe(dc, tt * 128, 128),
                            abn_s[:, dc * 2:dc * 2 + 2],
                            start=(dc == 0), stop=(dc == 15)),
                            waits=[(CV, cv_evG)], inc=CT)
                    em.ins('vector', lambda e: e.tensor_copy(
                        gdc_s[:, tt:tt + 1], PS[3][:, 0:1]),
                        waits=[(CT, CT.v), (CA, ca_mb_prev)], inc=CV)
                    cv_evG = em.ins('vector', lambda e: e.tensor_copy(
                        gdc_s[:, 8 + tt:9 + tt], PS[3][:, 1:2]), inc=CV)

                ca_t = em.ins('scalar', lambda e: e.activation(
                    stat_s[:, 20:24], gdc_s[:, 0:4], AF.Tanh, scale=0.5),
                    waits=[(CV, cv_evG)], inc=CA)
                em.ins('vector', lambda e: e.tensor_scalar(
                    stat_s[:, 20:24], stat_s[:, 20:24], 0.5, 0.5, ALU.mult, ALU.add),
                    waits=[(CA, ca_t)], inc=CV)
                em.fence('vector', CV)
                cv6 = em.ins('vector', lambda e: e.tensor_tensor(
                    stat_s[:, 24:28], stat_s[:, 20:24], gdc_s[:, 8:12], ALU.mult),
                    waits=[(CA, ca_mb_prev)], inc=CV)
                ca_s = em.ins('scalar', lambda e: e.activation(
                    stat_s[:, 28:32], stat_s[:, 24:28], AF.Sign),
                    waits=[(CV, cv6)], inc=CA)
                ca_mb_prev = ca_s
                em.ins('vector', lambda e: e.tensor_scalar(
                    stat_s[:, 28:32], stat_s[:, 28:32], 1.0, 5e29, ALU.subtract, ALU.mult),
                    waits=[(CA, ca_s)], inc=CV)
                em.fence('vector', CV)
                em.ins('vector', lambda e: e.tensor_tensor(
                    mb_s[:, sg * 4:(sg + 1) * 4], stat_s[:, 24:28], stat_s[:, 28:32],
                    ALU.add), inc=CV)

                for cc in range(4):
                    for dc in range(16):
                        em.ins('tensor', lambda e: e.matmul(
                            PS[2][:], wqg_s[:, dc * 512 + cc * 128:dc * 512 + (cc + 1) * 128],
                            stripe(dc, 0, 512), start=(dc == 0), stop=(dc == 15)),
                            waits=[(CV, cv_evQ)], inc=CT)
                    cv_evQ = em.ins('vector', lambda e: e.tensor_copy(
                        qT_s[:, cc * S + sg * 512:cc * S + (sg + 1) * 512], PS[2][:]),
                        waits=[(CT, CT.v)], inc=CV)

                ct_half[half] = CT.v

            while abs_state['i'] < len(abs_chunks):
                emit_abs_chunk()

            cvt = em.ins('vector', lambda e: e.tensor_tensor(
                absacc_v[:, 0:2], absacc_v[:, 0:2], absacc_p[:, 0:2], ALU.add),
                waits=[(CV, abs_state['cvd'])], inc=CV)
            ctt1 = em.ins('tensor', lambda e: e.matmul(
                PS[5][0:1, 0:2], stat_s[:, 35:36], absacc_v[:, 0:2],
                start=True, stop=True), waits=[(CV, cvt)], inc=CT)
            em.ins('vector', lambda e: e.tensor_copy(trow_s[0:1, 0:2], PS[5][0:1, 0:2]),
                   waits=[(CT, ctt1)], inc=CV)
            em.fence('vector', CV)
            cvc = em.ins('vector', lambda e: e.tensor_scalar(
                trow_s[0:1, 2:4], trow_s[0:1, 0:2], 0.7 / (DFF * D), None, ALU.mult),
                inc=CV)
            em.ins('tensor', lambda e: e.matmul(
                PS[5][:, 2:3], ones_rf[0:1, :], trow_s[0:1, 2:3], start=True, stop=True),
                waits=[(CV, cvc)], inc=CT)
            ctb2 = em.ins('tensor', lambda e: e.matmul(
                PS[5][:, 3:4], ones_rf[0:1, :], trow_s[0:1, 3:4], start=True, stop=True),
                inc=CT)
            em.ins('vector', lambda e: e.tensor_copy(thr_s[:, 0:1], PS[5][:, 2:3]),
                   waits=[(CT, ctb2)], inc=CV)
            em.ins('vector', lambda e: e.tensor_scalar(
                thr_s[:, 1:2], PS[5][:, 2:3], -1.0, None, ALU.mult), inc=CV)
            em.ins('vector', lambda e: e.tensor_copy(thr_s[:, 2:3], PS[5][:, 3:4]), inc=CV)
            cv_thr = em.ins('vector', lambda e: e.tensor_scalar(
                thr_s[:, 3:4], PS[5][:, 3:4], -1.0, None, ALU.mult), inc=CV)

            lg_wo = em.ins('gpsimd', lambda e: e.dma_start(
                out=wo_s.rearrange("p (ch c) -> p ch c", ch=4), in_=wo_r.rearrange("(ch p) c -> p ch c", p=128)),
                waits=[(CT, CT.v)], inc=LG, inc_n=16)

            if debug:
                em.ins('sync', lambda e: e.dma_start(out=d_kT, in_=kT_s[:]),
                       waits=[(CV, CV.v)], inc=LQ, inc_n=16)
                em.ins('sync', lambda e: e.dma_start(out=d_v, in_=v_s[:]), inc=LQ, inc_n=16)
                em.ins('sync', lambda e: e.dma_start(out=d_qT, in_=qT_s[:]), inc=LQ, inc_n=16)
                em.ins('sync', lambda e: e.dma_start(out=d_mb, in_=mb_s[:]), inc=LQ, inc_n=16)
                em.ins('sync', lambda e: e.dma_start(out=d_thr, in_=thr_s[:]), inc=LQ, inc_n=16)
                em.ins('sync', lambda e: e.dma_start(out=d_abn, in_=abn_s[:]), inc=LQ, inc_n=16)

            qchunks = []
            for r in range(64):
                qchunks.append((w1[r * 128:(r + 1) * 128, :],
                                w1qS[r * 128:(r + 1) * 128, :], 0))
            for r in range(16):
                for c in range(4):
                    qchunks.append((w2[r * 128:(r + 1) * 128, c * 2048:(c + 1) * 2048],
                                    w2qS[r * 128:(r + 1) * 128, c * 2048:(c + 1) * 2048], 2))
            qs = dict(i=0, free=[(CV, 0), (CP, 0)], outlq=[0, 0], laststore=0,
                      abuf_lq=[0, 0], sq_ca=0, w1done=0, w2done=0)

            def emit_quant_chunk():
                if qs['i'] >= len(qchunks):
                    return
                src, dst, tsel = qchunks[qs['i']]
                par = qs['i'] % 2
                role = 'vector'
                CX = CV
                wf = xb_s[:, par * 2048:(par + 1) * 2048]
                abuf = lnb_s[:, par * 2048:(par + 1) * 2048]
                obuf = sqd_s[:] if par == 0 else qC_s[:]
                lq1 = em.ins('gpsimd', lambda e: e.dma_start(out=wf, in_=src),
                             waits=[qs['free'][par]], inc=LG, inc_n=16)
                em.ins(role, lambda e: e.tensor_scalar(
                    obuf, wf, thr_s[:, tsel + 1:tsel + 2], 1.0, ALU.is_ge, ALU.subtract),
                    waits=[(LG, lq1), (CV, cv_thr), (LG, qs['outlq'][par]),
                           (CA, qs['sq_ca'])], inc=CX)
                c2x = em.ins(role, lambda e: e.tensor_scalar(
                    abuf, wf, thr_s[:, tsel:tsel + 1], None, ALU.is_ge),
                    waits=[(LQ, qs['abuf_lq'][par])], inc=CX)
                c3 = em.ins('vector', lambda e: e.tensor_tensor(
                    obuf, obuf, abuf, ALU.add), inc=CV)
                qs['free'][par] = (CV, c3)
                lqs = em.ins('gpsimd', lambda e: e.dma_start(out=dst, in_=obuf),
                             waits=[(CV, c3)], inc=LG, inc_n=16)
                qs['outlq'][par] = lqs
                qs['laststore'] = lqs
                if qs['i'] == 63:
                    qs['w1done'] = lqs
                if qs['i'] == len(qchunks) - 1:
                    qs['w2done'] = lqs
                qs['i'] += 1

            p25_state = dict(lq_xm_prev=0, cv_xm=0)
            p25_done = set()

            def emit_p25_tile(l):
                nonlocal sumsq_free
                p25_done.add(l)
                g = l // 4
                lqa = em.ins('sync', lambda e: e.dma_start(
                    out=xb_s[:, 0:2048], in_=attnR[l * 128:(l + 1) * 128, :]),
                    waits=[(CCg[g], 1), qs['free'][0],
                           (LQ, p25_state['lq_xm_prev'])],
                    inc=LQ, inc_n=16)
                lqb = em.ins('sync', lambda e: e.dma_start(
                    out=xb_s[:, 2048:4096], in_=xown[l * 128:(l + 1) * 128, :]),
                    waits=[qs['free'][1]], inc=LQ, inc_n=16)
                em.ins('vector', lambda e: e.reduce_sum(
                    stat_s[:, 3:4], xb_s[:, 0:2048], axis=AX.X),
                    waits=[(LQ, lqa)], inc=CV)
                em.fence('vector', CV)
                em.ins('vector', lambda e: e.tensor_scalar(
                    stat_s[:, 0:1], stat_s[:, 3:4], 1.0 / 2048, None, ALU.mult), inc=CV)
                caq = em.ins('scalar', lambda e: e.activation(
                    sqd_s[:], xb_s[:, 0:2048], AF.Square, accum_out=stat_s[:, 4:5]),
                    waits=[(LQ, lqa), (LG, qs['laststore']), (CV, sumsq_free)], inc=CA)
                cv1 = em.ins('vector', lambda e: e.tensor_scalar(
                    stat_s[:, 6:7], stat_s[:, 4:5], 1.0 / 2048, EPS, ALU.mult, ALU.add),
                    waits=[(CA, caq)], inc=CV)
                sumsq_free = cv1
                em.ins('vector', lambda e: e.tensor_tensor(
                    stat_s[:, 7:8], stat_s[:, 0:1], stat_s[:, 0:1], ALU.mult), inc=CV)
                em.fence('vector', CV)
                cv3 = em.ins('vector', lambda e: e.tensor_tensor(
                    stat_s[:, 6:7], stat_s[:, 6:7], stat_s[:, 7:8], ALU.subtract), inc=CV)
                em.ins('scalar', lambda e: e.activation(
                    stat_s[:, 8:9], stat_s[:, 6:7], AF.Ln), waits=[(CV, cv3)], inc=CA)
                ca2 = em.ins('scalar', lambda e: e.activation(
                    stat_s[:, 1:2], stat_s[:, 8:9], AF.Exp, scale=-0.5), inc=CA)
                cv4 = em.ins('vector', lambda e: e.tensor_scalar(
                    stat_s[:, 2:3], stat_s[:, 0:1], stat_s[:, 1:2], -1.0,
                    ALU.mult, ALU.mult), waits=[(CA, ca2)], inc=CV)
                em.fence('vector', CV)
                em.ins('vector', lambda e: e.tensor_scalar(
                    xb_s[:, 0:2048], xb_s[:, 0:2048], stat_s[:, 1:2], stat_s[:, 2:3],
                    ALU.mult, ALU.add), waits=[(CV, cv4)], inc=CV)
                cv_xm = em.ins('vector', lambda e: e.tensor_tensor(
                    xb_s[:, 0:2048], xb_s[:, 0:2048], xb_s[:, 2048:4096], ALU.add),
                    waits=[(LQ, lqb)], inc=CV)
                p25_state['cv_xm'] = cv_xm
                p25_state['lq_xm_prev'] = em.ins('sync', lambda e: e.dma_start(
                    out=xmS[l * 128:(l + 1) * 128, :], in_=xb_s[:, 0:2048]),
                    waits=[(CV, cv_xm)], inc=LQ, inc_n=16)

            bounce_tiles = []
            for f0 in range(8):
                for d0 in range(16):
                    bounce_tiles.append((
                        w1qS[f0 * 1024:(f0 + 1) * 1024, d0 * 128:(d0 + 1) * 128],
                        w1qT[d0 * 128:(d0 + 1) * 128, f0 * 1024:(f0 + 1) * 1024], 0))
            for f0 in range(64):
                for m0 in range(2):
                    bounce_tiles.append((
                        w2qS[m0 * 1024:(m0 + 1) * 1024, f0 * 128:(f0 + 1) * 128],
                        w2qT[f0 * 128:(f0 + 1) * 128, m0 * 1024:(m0 + 1) * 1024], 1))
            bounce_state = dict(i=0, lq=0, plq=[0, 0])

            def emit_bounce_pair():
                bi = bounce_state['i']
                if bi >= len(bounce_tiles):
                    return
                bsrc, bdst, m = bounce_tiles[bi]
                par = bi % 2
                gate = qs['w1done'] if m == 0 else qs['w2done']
                lsb = em.ins('sync', lambda e: e.dma_start(
                    out=w2qT_s[:, par * 1024:(par + 1) * 1024], in_=bsrc,
                    transpose=True),
                    waits=[(LG, gate), (LQ, bounce_state['plq'][par])],
                    inc=LQ, inc_n=16)
                bounce_state['plq'][par] = em.ins('sync', lambda e: e.dma_start(
                    out=bdst, in_=w2qT_s[:, par * 1024:(par + 1) * 1024]),
                    waits=[(LQ, lsb)], inc=LQ, inc_n=16)
                bounce_state['lq'] = bounce_state['plq'][par]
                bounce_state['i'] = bi + 1

            qs['abuf_lq'] = [lnb_lq[0], lnb_lq[1]]
            qs['sq_ca'] = CA.v
            cv_p1 = CV.v
            em.wait('tensor', CV, cv_p1)
            em.wait('scalar', CV, cv_p1)
            em.wait('scalar', CP, CP.v)

            PSA = [PS[0], PS[1], PS[5], PS[6]]
            psa_free = [0, 0, 0, 0]
            pt_free = [0] * 8
            cv_t4_par = [0, 0]
            ct_t4z_prev = 0
            ct_p1_stripes = CT.v
            cv_onorm_prev = 0
            cv_evac_prev = 0
            cv_evac_pair = [0, 0]
            ca_rb_prev = 0
            ct_B_prev = 0
            ct_wo_prev = 0
            cv_zp_prev = 0
            lq_evac_prev = 0
            ca_last_exp = 0
            for t in range(NT):
                ct_pv_last = 0
                nk = t + 1
                ngrp = (nk + 3) // 4
                exp_of = {}
                exp_last_grp = {-1: ca_last_exp}
                for g in range(ngrp + 1):
                    if g < ngrp:
                        kts = range(4 * g, min(4 * g + 4, nk))
                        for kt in kts:
                            bank = kt % 4
                            wS = [(CA, psa_free[bank])]
                            ctS = em.ins('tensor', lambda e: e.matmul(
                                PSA[bank][:], kT_s[:, kt * 128:(kt + 1) * 128],
                                qT_r[:, :, t * 128:(t + 1) * 128],
                                start=True, stop=True), waits=wS, inc=CT)
                            if kt == t:
                                cvtri = em.ins('vector', lambda e: e.tensor_tensor(
                                    PSA[bank][:], PSA[bank][:], tri_s[:], ALU.add),
                                    waits=[(CT, ctS)], inc=CV)
                                exp_of[kt] = ('cv', cvtri)
                            else:
                                exp_of[kt] = ('ct', ctS)
                    if g >= 1:
                        gp = (g - 1) % 2
                        hb2 = gp * 2048
                        kts = range(4 * (g - 1), min(4 * (g - 1) + 4, nk))
                        first = True
                        for kt in kts:
                            bank = kt % 4
                            wpv = [(CA, exp_last_grp[g - 1])] if first else []
                            first = False
                            if kt == 0:
                                wpv.append((CV, cv_onorm_prev))
                            ctP = em.ins('tensor', lambda e: e.matmul(
                                PS[2][:], v_s[:, kt * 128:(kt + 1) * 128],
                                pT_s[:, hb2 + bank * 512:hb2 + (bank + 1) * 512],
                                start=(kt == 0), stop=(kt == t)), waits=wpv, inc=CT)
                            ct_pv_last = ctP
                            pt_free[gp * 4 + bank] = ctP
                        g0 = 4 * (g - 1)
                        w = min(4, nk - g0)
                        if g == 1:
                            cv_t4 = em.ins('vector', lambda e: e.tensor_copy(
                                T4_s[:, 0:w * 512], pT_s[:, hb2:hb2 + w * 512]),
                                waits=[(CA, exp_last_grp[g - 1]), (CT, ct_t4z_prev),
                                       (CT, ct_p1_stripes)], inc=CV)
                            if w < 4:
                                cv_t4 = em.ins('vector', lambda e: e.memset(
                                    T4_s[:, w * 512:2048], 0.0), inc=CV)
                        else:
                            cv_t4 = em.ins('vector', lambda e: e.tensor_tensor(
                                T4_s[:, 0:w * 512], T4_s[:, 0:w * 512],
                                pT_s[:, hb2:hb2 + w * 512], ALU.add),
                                waits=[(CA, exp_last_grp[g - 1])], inc=CV)
                        cv_t4_par[gp] = cv_t4
                    if g < ngrp:
                        kts = range(4 * g, min(4 * g + 4, nk))
                        for kt in kts:
                            bank = kt % 4
                            kind, val = exp_of[kt]
                            wexp = [(CV, val)] if kind == 'cv' else [(CT, val)]
                            gp2 = g % 2
                            caE = em.ins('scalar', lambda e: e.activation(
                                pT_s[:, gp2 * 2048 + bank * 512:
                                     gp2 * 2048 + (bank + 1) * 512], PSA[bank][:],
                                AF.Exp, bias=mb_s[:, kt:kt + 1], scale=SCALE),
                                waits=wexp + [(CT, pt_free[gp2 * 4 + bank]),
                                              (CV, cv_t4_par[gp2])], inc=CA)
                            psa_free[bank] = caE
                            ca_last_exp = caE
                        exp_last_grp[g] = ca_last_exp

                cv_t4b = em.ins('vector', lambda e: e.tensor_copy(
                    T4b_s[:], T4_s[:]), waits=[(CT, ct_t4z_prev)], inc=CV)
                nzm = min(4, nk)
                for j in range(nzm):
                    wz = [(CV, cv_t4b)]
                    if j == 0:
                        wz.append((CV, cv_zp_prev))
                    em.ins('tensor', lambda e: e.matmul(
                        PS[3][0:1, :], ones_cb[:], T4b_s[:, j * 512:(j + 1) * 512],
                        start=(j == 0), stop=(j == nzm - 1)), waits=wz, inc=CT)
                ct_t4z_prev = CT.v
                cv_zp = em.ins('vector', lambda e: e.tensor_scalar(
                    rz_s[0:1, :], PS[3][0:1, :], 1e-30, None, ALU.add),
                    waits=[(CT, ct_t4z_prev), (CA, ca_rb_prev), (LQ, lq_evac_prev)],
                    inc=CV)
                cv_zp_prev = cv_zp
                em.fence('vector', CV)
                em.ins('vector', lambda e: e.reciprocal(rz_s[0:1, :], rz_s[0:1, :]),
                       inc=CV)
                em.fence('vector', CV)
                cv_rzb = em.ins('vector', lambda e: e.tensor_copy(
                    rzb_s[0:1, :], rz_s[0:1, :]),
                    waits=[(CT, ct_B_prev), (CT, ct_wo_prev)], inc=CV)
                ctB = em.ins('tensor', lambda e: e.matmul(
                    PS[4][:], ones_rb[0:1, :], rzb_s[0:1, :], start=True, stop=True),
                    waits=[(CV, cv_rzb), (CA, ca_rb_prev), (CV, cv_evac_prev)], inc=CT)
                ct_B_prev = ctB
                ca_rb = em.ins('scalar', lambda e: e.copy(rb_s[:], PS[4][:]),
                               waits=[(CT, ctB), (CV, cv_onorm_prev)], inc=CA)
                ca_rb_prev = ca_rb
                cv_on = em.ins('vector', lambda e: e.tensor_tensor(
                    onorm_s[:], PS[2][:], rb_s[:], ALU.mult),
                    waits=[(CA, ca_rb), (CT, ct_pv_last)], inc=CV)
                cv_onorm_prev = cv_on

                WOB = [PS7, PS[4]]
                for pair in range(2):
                    for hl in range(4):
                        for oc2 in range(2):
                            oc = pair * 2 + oc2
                            w = [(CV, cv_on), (LG, lg_wo)]
                            if hl == 0:
                                w.append((CV, cv_evac_pair[1] if pair == 0
                                          else cv_evac_pair[0]))
                            em.ins('tensor', lambda e: e.matmul(
                                WOB[oc2][:], onorm_s[:, hl * 128:(hl + 1) * 128],
                                wo_s[:, hl * 2048 + oc * 512:hl * 2048 + (oc + 1) * 512],
                                start=(hl == 0), stop=(hl == 3)), waits=w, inc=CT)
                    ct_wo = CT.v
                    ct_wo_prev = ct_wo
                    for oc2 in range(2):
                        oc = pair * 2 + oc2
                        cvs = em.ins('vector', lambda e: e.tensor_copy(
                            evac_s[:], WOB[oc2][:]),
                            waits=[(CT, ct_wo), (LQ, lq_evac_prev)], inc=CV)
                        lq_evac_prev = em.ins('sync', lambda e: e.dma_start(
                            out=attnP[t * 128:(t + 1) * 128,
                                      oc * 512:(oc + 1) * 512],
                            in_=evac_s[:]), waits=[(CV, cvs)], inc=LQ, inc_n=16)
                    cv_evac_pair[pair] = cvs
                cv_evac_prev = cvs


                if t >= 8:
                    for _ in range(4):
                        emit_quant_chunk()
                if t >= 40:
                    for _ in range(10):
                        emit_bounce_pair()
                if t in (45, 49, 53):
                    for l in range(4 * ((t - 45) // 4), 4 * ((t - 45) // 4) + 4):
                        emit_p25_tile(l)

                if t % 16 == 15:
                    g = t // 16
                    em.ins('gpsimd', lambda e: e.collective_compute(
                        "ReduceScatter", ALU.add,
                        replica_groups=[[0, 1, 2, 3], [4, 5, 6, 7]],
                        ins=[attnP[g * 2048:(g + 1) * 2048, :].opt()],
                        outs=[attnR[g * 512:(g + 1) * 512, :].opt()]),
                        waits=[(LQ, lq_evac_prev)], inc=CCg[g])

            while qs['i'] < len(qchunks):
                emit_quant_chunk()
            while bounce_state['i'] < len(bounce_tiles):
                emit_bounce_pair()
            lq_bounce_all = bounce_state['lq']
            for l in range(16):
                if l not in p25_done:
                    emit_p25_tile(l)
            lq_xm_all = p25_state['lq_xm_prev']

            ln3_lq = [0, 0]
            for l in range(16):
                par = l % 2
                can, lqs = emit_ln_tile(
                    xmS[l * 128:(l + 1) * 128, :], ln3S[l * 128:(l + 1) * 128, :],
                    par, par,
                    load_waits=[(LQ, lq_xm_all)],
                    norm_waits=[(LQ, ln3_lq[par]), qs['free'][par]])
                ln3_lq[par] = lqs
            lq_ln3_all = LQ.v

            lqb2 = em.ins('sync', lambda e: e.dma_start(out=b2r_s[0:1, :], in_=b2row),
                          waits=[(CT, CT.v)], inc=LQ, inc_n=16)
            cv_now = CV.v
            ca_now = CA.v
            for ocb in range(4):
                ctb = em.ins('tensor', lambda e: e.matmul(
                    PS[ocb][:], ones_rf[0:1, :], b2r_s[0:1, ocb * 512:(ocb + 1) * 512],
                    start=True, stop=True),
                    waits=[(LQ, lqb2), (CV, cv_now), (CA, ca_now)], inc=CT)
                em.ins('vector', lambda e: e.tensor_copy(
                    b2b_s[:, ocb * 512:(ocb + 1) * 512], PS[ocb][:]),
                    waits=[(CT, ctb)], inc=CV)
            cv_b2b = CV.v

            ct_f1_half = [CT.v, CT.v]
            ct_w2blk = [0, 0]
            ca_hs = [0, 0]
            cv_out2 = cv_b2b
            cv_final = 0
            lq_out_last = 0
            for sg2 in range(4):
                ls_ln3 = 0
                for dc in range(16):
                    ls_ln3 = em.ins('scalar', lambda e: e.dma_start(
                        out=ln3T_s[:, dc * 512:(dc + 1) * 512],
                        in_=ln3S[sg2 * 512:(sg2 + 1) * 512, dc * 128:(dc + 1) * 128],
                        transpose=True),
                        waits=[(LQ, lq_ln3_all), (CT, ct_f1_half[0])],
                        inc=LS, inc_n=16)

                w1qT_r = w1qT.rearrange("(dc p) f -> p dc f", p=128)
                for fbg in range(64):
                    wpar = fbg % 2
                    lss = em.ins('sync', lambda e: e.dma_start(
                        out=w1b_s[:, wpar * 2048:(wpar + 1) * 2048],
                        in_=w1qT_r[:, :, fbg * 128:(fbg + 1) * 128]),
                        waits=[(LQ, lq_bounce_all), (CT, ct_f1_half[wpar])],
                        inc=LQ, inc_n=16)
                    pb = fbg % 2
                    for dc in range(16):
                        w = [(LQ, lss), (LS, ls_ln3)]
                        if dc == 0:
                            w.append((CA, ca_hs[pb]))
                            if fbg < 2 and sg2 > 0:
                                w.append((CV, cv_out2))
                        em.ins('tensor', lambda e: e.matmul(
                            PS[pb][:],
                            w1b_s[:, wpar * 2048 + dc * 128:
                                  wpar * 2048 + (dc + 1) * 128],
                            ln3T_s[:, dc * 512:(dc + 1) * 512],
                            start=(dc == 0), stop=(dc == 15)), waits=w, inc=CT)
                    ct_f1_half[wpar] = CT.v
                    ca_hs[pb] = em.ins('scalar', lambda e: e.activation(
                        Hs_s[:, fbg * 512:(fbg + 1) * 512], PS[pb][:], AF.Gelu,
                        bias=b1c_s[:, fbg:fbg + 1]),
                        waits=[(CT, CT.v), (CV, cv_b1)], inc=CA)
                ca_f1_done = max(ca_hs)

                for dmh in range(2):
                    for fbp in range(32):
                        par = fbp % 2
                        wbuf = w2qT_s if par == 0 else w2p_s
                        lsw = em.ins('scalar', lambda e: e.dma_start(
                            out=wbuf.rearrange("p (two m) -> p two m", two=2),
                            in_=w2qT[fbp * 256:(fbp + 1) * 256,
                                     dmh * 1024:(dmh + 1) * 1024].rearrange(
                                         "(two p) m -> p two m", p=128)),
                            waits=[(LQ, lq_bounce_all), (CT, ct_w2blk[par])],
                            inc=LS, inc_n=16)
                        for half in range(2):
                            fb = fbp * 2 + half
                            for ts in range(4):
                                for oc2 in range(2):
                                    bi = ts * 2 + oc2
                                    bank = PS[bi] if bi < 7 else PS7
                                    w = [(LS, lsw), (CA, ca_f1_done)]
                                    if fb == 0:
                                        w.append((CV, cv_out2))
                                    em.ins('tensor', lambda e: e.matmul(
                                        bank[:],
                                        Hs_s[:, fb * 512 + ts * 128:
                                             fb * 512 + (ts + 1) * 128],
                                        wbuf[:, half * 1024 + oc2 * 512:
                                             half * 1024 + (oc2 + 1) * 512],
                                        start=(fb == 0), stop=(fb == 63)),
                                        waits=w, inc=CT)
                        ct_w2blk[par] = CT.v
                    ct_f2 = CT.v
                    for ts in range(4):
                        for oc2 in range(2):
                            bi = ts * 2 + oc2
                            bank = PS[bi] if bi < 7 else PS7
                            cv_out2 = em.ins('vector', lambda e: e.tensor_tensor(
                                out2_s[:, ts * 2048 + dmh * 1024 + oc2 * 512:
                                       ts * 2048 + dmh * 1024 + (oc2 + 1) * 512],
                                bank[:],
                                b2b_s[:, dmh * 1024 + oc2 * 512:
                                      dmh * 1024 + (oc2 + 1) * 512],
                                ALU.add), waits=[(CT, ct_f2), (CV, cv_b2b),
                                                 (CV, cv_final)], inc=CV)

                for ts in range(4):
                    l = sg2 * 4 + ts
                    base = ts * 2048
                    lqx = em.ins('sync', lambda e: e.dma_start(
                        out=xb_s[:, 0:2048], in_=xmS[l * 128:(l + 1) * 128, :]),
                        waits=[(LQ, lq_xm_all), (CA, xb_free[0]),
                               (CV, cv_final)], inc=LQ, inc_n=16)
                    xb_free[0] = 0
                    em.ins('vector', lambda e: e.reduce_sum(
                        stat_s[:, 3:4], out2_s[:, base:base + 2048], axis=AX.X),
                        waits=[(CV, cv_out2)], inc=CV)
                    em.fence('vector', CV)
                    em.ins('vector', lambda e: e.tensor_scalar(
                        stat_s[:, 0:1], stat_s[:, 3:4], 1.0 / 2048, None, ALU.mult),
                        inc=CV)
                    caq = em.ins('scalar', lambda e: e.activation(
                        sqd_s[:], out2_s[:, base:base + 2048], AF.Square,
                        accum_out=stat_s[:, 4:5]),
                        waits=[(CV, cv_out2), (CV, sumsq_free)], inc=CA)
                    cv1 = em.ins('vector', lambda e: e.tensor_scalar(
                        stat_s[:, 6:7], stat_s[:, 4:5], 1.0 / 2048, EPS,
                        ALU.mult, ALU.add), waits=[(CA, caq)], inc=CV)
                    sumsq_free = cv1
                    em.ins('vector', lambda e: e.tensor_tensor(
                        stat_s[:, 7:8], stat_s[:, 0:1], stat_s[:, 0:1], ALU.mult),
                        inc=CV)
                    em.fence('vector', CV)
                    cv3 = em.ins('vector', lambda e: e.tensor_tensor(
                        stat_s[:, 6:7], stat_s[:, 6:7], stat_s[:, 7:8], ALU.subtract),
                        inc=CV)
                    em.ins('scalar', lambda e: e.activation(
                        stat_s[:, 8:9], stat_s[:, 6:7], AF.Ln), waits=[(CV, cv3)],
                        inc=CA)
                    ca2 = em.ins('scalar', lambda e: e.activation(
                        stat_s[:, 1:2], stat_s[:, 8:9], AF.Exp, scale=-0.5), inc=CA)
                    cv4 = em.ins('vector', lambda e: e.tensor_scalar(
                        stat_s[:, 2:3], stat_s[:, 0:1], stat_s[:, 1:2], -1.0,
                        ALU.mult, ALU.mult), waits=[(CA, ca2)], inc=CV)
                    em.fence('vector', CV)
                    em.ins('vector', lambda e: e.tensor_scalar(
                        out2_s[:, base:base + 2048], out2_s[:, base:base + 2048],
                        stat_s[:, 1:2], stat_s[:, 2:3], ALU.mult, ALU.add),
                        waits=[(CV, cv4)], inc=CV)
                    cv_final = em.ins('vector', lambda e: e.tensor_tensor(
                        out2_s[:, base:base + 2048], out2_s[:, base:base + 2048],
                        xb_s[:, 0:2048], ALU.add), waits=[(LQ, lqx)], inc=CV)
                    lq_out_last = em.ins('sync', lambda e: e.dma_start(
                        out=out[l * 128:(l + 1) * 128, :],
                        in_=out2_s[:, base:base + 2048]),
                        waits=[(CV, cv_final)], inc=LQ, inc_n=16)

            if debug:
                em.ins('gpsimd', lambda e: e.dma_start(out=d_attnP, in_=attnP),
                       waits=[(LQ, lq_out_last)], inc=LG, inc_n=16)
                em.ins('gpsimd', lambda e: e.dma_start(out=d_attnR, in_=attnR),
                       inc=LG, inc_n=16)
            em.wait('sync', LQ, lq_out_last)
            em.wait('gpsimd', LG, LG.v)
            em.wait('scalar', LS, LS.v)

        @blk.sync
        def _(eng):
            program(Em('sync', eng))

        @blk.gpsimd
        def _(eng):
            program(Em('gpsimd', eng))

        @blk.vector
        def _(eng):
            program(Em('vector', eng))

        @blk.scalar
        def _(eng):
            program(Em('scalar', eng))

        @blk.tensor
        def _(eng):
            program(Em('tensor', eng))

    return nc


def make_in_maps(X, Wq, Wk, Wv, Wg, Wd, Wo, w1, b1, w2, b2):
    in_maps = []
    for c in range(8):
        b, n = c // 4, c % 4
        own_rows = np.concatenate(
            [np.arange(g * 2048 + n * 512, g * 2048 + (n + 1) * 512)
             for g in range(4)])
        in_maps.append({
            "x": np.ascontiguousarray(X[b]),
            "xown": np.ascontiguousarray(X[b][own_rows]),
            "wq_g": np.ascontiguousarray(Wq[:, n * 512:(n + 1) * 512]),
            "wk_h": np.ascontiguousarray(Wk[:, n * 128:(n + 1) * 128]),
            "wv_h": np.ascontiguousarray(Wv[:, n * 128:(n + 1) * 128]),
            "wq_full": Wq,
            "wv_full": Wv,
            "wg_n": np.ascontiguousarray(Wg[:, n:n + 1]),
            "wd_n": np.ascontiguousarray(Wd[:, n:n + 1]),
            "wo_r": np.ascontiguousarray(Wo[n * 512:(n + 1) * 512, :]),
            "w1": w1,
            "w2": w2,
            "b1r": np.ascontiguousarray(b1.reshape(64, 128)),
            "b2row": np.ascontiguousarray(b2.reshape(1, 2048)),
        })
    return in_maps


def assemble(results):
    OUT = np.empty((B, S, D), dtype=np.float32)
    for c in range(8):
        b, n = c // 4, c % 4
        o = results[c]["out"]
        for g in range(4):
            OUT[b, g * 2048 + n * 512:g * 2048 + (n + 1) * 512, :] = \
                o[g * 512:(g + 1) * 512, :]
    return OUT


def kernel(L0_attn_hidden_states, L0_attn_Wq, L0_attn_Wk, L0_attn_Wv,
           L0_attn_Wg, L0_attn_Wd, L0_attn_Wo, L0_ffn_w1, L0_ffn_b1,
           L0_ffn_w2, L0_ffn_b2):
    f = lambda a: np.ascontiguousarray(np.asarray(a, dtype=np.float32))
    X = f(L0_attn_hidden_states)
    in_maps = make_in_maps(X, f(L0_attn_Wq), f(L0_attn_Wk), f(L0_attn_Wv),
                           f(L0_attn_Wg), f(L0_attn_Wd), f(L0_attn_Wo),
                           f(L0_ffn_w1), f(L0_ffn_b1), f(L0_ffn_w2),
                           f(L0_ffn_b2))
    nc = build_program(debug=False)
    res = run_bass_kernel_spmd(nc, in_maps, core_ids=list(range(8)))
    return assemble(res.results)


# version 18
# speedup vs baseline: 1.3157x; 1.0243x over previous
import numpy as np
import ml_dtypes
import concourse.bass as bass
import concourse.mybir as mybir
from concourse.bass_utils import run_bass_kernel_spmd

F32 = mybir.dt.float32
BF16 = mybir.dt.bfloat16
AF = mybir.ActivationFunctionType
ALU = mybir.AluOpType
AX = mybir.AxisListType
bf16 = ml_dtypes.bfloat16

B, S, D = 2, 8192, 2048
HD = 128
DFF = 8192
NT = S // 128
SCALE = HD ** -0.5
EPS = 1e-5

SG = 16
SGT = 4


class Sem:
    def __init__(self, handle, name):
        self.h = handle
        self.name = name
        self.v = 0

    def bump(self, n):
        self.v += n
        return self.v


class Em:

    def __init__(self, role, engine):
        self.role = role
        self.eng = engine
        self.waited = {}

    def _wait(self, s, v):
        if v <= 0 or self.waited.get(s.name, -1) >= v:
            return
        self.eng.wait_ge(s.h, v)
        self.waited[s.name] = v

    def ins(self, role, fn, waits=(), inc=None, inc_n=1):
        if role == self.role:
            for (s, v) in waits:
                self._wait(s, v)
            inst = fn(self.eng)
            if inc is not None:
                inst.then_inc(inc.h, inc_n)
        if inc is not None:
            return inc.bump(inc_n)
        return None

    def wait(self, role, s, v):
        if role == self.role:
            self._wait(s, v)

    def fence(self, role, s):
        if role == self.role:
            self._wait(s, s.v)


def build_program(debug=False):
    nc = bass.Bass("TRN2", target_bir_lowering=False, debug=False, num_devices=8)

    def inp(name, shape, dt=F32):
        return nc.dram_tensor(name, shape, dt, kind="ExternalInput").ap()

    x = inp("x", [S, D])
    xown = inp("xown", [2048, D])
    wq_g = inp("wq_g", [D, 512])
    wk_h = inp("wk_h", [D, HD])
    wv_h = inp("wv_h", [D, HD])
    wq_full = inp("wq_full", [D, D])
    wv_full = inp("wv_full", [D, 512])
    wg_n = inp("wg_n", [D, 1])
    wd_n = inp("wd_n", [512, 1])
    wo_r = inp("wo_r", [512, D])
    w1 = inp("w1", [DFF, D])
    w2 = inp("w2", [D, DFF])
    b1r = inp("b1r", [64, 128])
    b2row = inp("b2row", [1, D])

    out = nc.dram_tensor("out", [2048, D], F32, kind="ExternalOutput").ap()

    dbg_kind = "ExternalOutput" if debug else "Internal"

    def scr(name, shape, dt):
        return nc.dram_tensor(name, shape, dt, kind=dbg_kind).ap()

    lnS = scr("lnS", [S, D], BF16)
    attnP = nc.dram_tensor("attnP", [S, D], F32, kind="Internal").ap()
    attnR = nc.dram_tensor("attnR", [2048, D], F32, kind="Internal").ap()
    xmS = scr("xmS", [2048, D], F32)
    ln3S = scr("ln3S", [2048, D], BF16)
    w1qS = scr("w1qS", [DFF, D], BF16)
    w2qS = scr("w2qS", [D, DFF], BF16)
    w1qT = scr("w1qT", [D, DFF], BF16)
    w2qT = scr("w2qT", [DFF, D], BF16)
    wqbfS = nc.dram_tensor("wqbfS", [D, D], BF16, kind="Internal").ap()
    wvbfS = nc.dram_tensor("wvbfS", [D, 512], BF16, kind="Internal").ap()
    if debug:
        d_kT = nc.dram_tensor("d_kT", [128, S], BF16, kind="ExternalOutput").ap()
        d_v = nc.dram_tensor("d_v", [128, S], BF16, kind="ExternalOutput").ap()
        d_qT = nc.dram_tensor("d_qT", [128, 4 * S], BF16, kind="ExternalOutput").ap()
        d_mb = nc.dram_tensor("d_mb", [128, 64], F32, kind="ExternalOutput").ap()
        d_thr = nc.dram_tensor("d_thr", [128, 4], F32, kind="ExternalOutput").ap()
        d_abn = nc.dram_tensor("d_abn", [128, 32], mybir.dt.bfloat16, kind="ExternalOutput").ap()
        d_attnP = nc.dram_tensor("d_attnP", [S, D], F32, kind="ExternalOutput").ap()
        d_attnR = nc.dram_tensor("d_attnR", [2048, D], F32, kind="ExternalOutput").ap()

    identB = nc.inline_tensor(np.eye(128, dtype=bf16), name="identB")
    identF = nc.inline_tensor(np.eye(64, dtype=np.float32), name="identF")
    tq = np.arange(128)
    tri_blk = np.where(tq[None, :] >= tq[:, None], 0.0, -1e32).astype(bf16)
    triC = nc.inline_tensor(np.tile(tri_blk, (1, 4)), name="triC")

    def sb(name, shape, dt):
        return nc.alloc_sbuf_tensor(name, shape, dt).ap()

    def sb_at(name, shape, dt, base_of, byte_off=0):
        off = nc.lookup_mloc(base_of.tensor).addr + byte_off
        return nc.alloc_sbuf_tensor_at(name, shape, dt, offset=off).ap()

    identB_s = sb("identB_s", [128, 128], BF16)
    identF_s = sb("identF_s", [64, 64], F32)
    tri_s = sb("tri_s", [128, 512], BF16)
    ones_cb = sb("ones_cb", [128, 1], BF16)
    ones_rb = sb("ones_rb", [1, 128], BF16)
    ones_rf = sb("ones_rf", [1, 128], F32)
    wkh_s = sb("wkh_s", [128, 16 * 128], BF16)
    wvh_s = sb("wvh_s", [128, 16 * 128], BF16)
    wqg_s = sb("wqg_s", [128, 16 * 512], BF16)
    wgn_s = sb("wgn_s", [128, 16], BF16)
    wdn_s = sb("wdn_s", [128, 4], BF16)
    abn_s = sb("abn_s", [128, 32], BF16)
    kT_s = sb("kT_s", [128, S], BF16)
    v_s = sb("v_s", [128, S], BF16)
    qT_s = sb("qT_s", [128, 4 * S], BF16)
    stripes = sb("stripes", [128, 16 * 1024], BF16)
    w1b_s = sb("w1b_s", [128, 2 * 2048], BF16)
    mb_s = sb("mb_s", [128, 64], F32)
    gdc_s = sb("gdc_s", [128, 16], F32)
    xb_s = sb("xb_s", [128, 2 * 2048], F32)
    lnb_s = sb("lnb_s", [128, 2 * 2048], BF16)
    sqd_s = sb("sqd_s", [128, 2048], BF16)
    pT_s = sb("pT_s", [128, 8 * 512], BF16)
    onorm_s = sb("onorm_s", [128, 512], BF16)
    rb_s = sb("rb_s", [128, 512], F32)
    w2qT_s = sb("w2qT_s", [128, 2 * 1024], BF16)
    b1c_s = sb("b1c_s", [128, 64], F32)
    b1r_s = sb("b1r_s", [64, 128], F32)
    stat_s = sb("stat_s", [128, 40], F32)
    absacc_v = sb("absacc_v", [128, 2], F32)
    absacc_p = sb("absacc_p", [128, 2], F32)
    thr_s = sb("thr_s", [128, 4], F32)
    trow_s = sb("trow_s", [1, 4], F32)

    rem = nc.sbuf_bytes_remaining
    assert rem >= 0, f"SBUF over budget: {rem}"

    b2b_s = sb_at("b2b_s", [128, 2048], F32, wkh_s)
    ln3T_s = sb_at("ln3T_s", [128, 16 * 512], BF16, wqg_s)
    out2_s = sb_at("out2_s", [128, 8192], F32, kT_s)
    Hs_s = sb_at("Hs_s", [128, 64 * 512], BF16, qT_s)
    wo_s = sb_at("wo_s", [128, 4 * 2048], BF16, stripes)
    b2r_s = sb_at("b2r_s", [1, 2048], F32, stripes, byte_off=16384)
    qC_s = sb_at("qC_s", [128, 2048], BF16, w1b_s)
    evac_s = sb_at("evac_s", [128, 512], F32, rb_s)
    rz_s = sb_at("rz_s", [1, 512], F32, rb_s)
    rzb_s = sb_at("rzb_s", [1, 512], BF16, onorm_s)
    absb_s = sb_at("absb_s", [128, 2048], BF16, pT_s)
    T4_s = sb_at("T4_s", [128, 2048], F32, stripes, byte_off=16384)
    T4b_s = sb_at("T4b_s", [128, 2048], BF16, stripes, byte_off=24576)
    w2p_s = sb_at("w2p_s", [128, 2048], BF16, stripes, byte_off=28672)
    assert nc.lookup_mloc(v_s.tensor).addr == nc.lookup_mloc(kT_s.tensor).addr + S * 2
    assert nc.lookup_mloc(wvh_s.tensor).addr == nc.lookup_mloc(wkh_s.tensor).addr + 16 * 128 * 2

    PS = [nc.alloc_psum_tensor(f"ps{i}", [128, 512], F32).ap() for i in range(7)]
    PS7 = nc.alloc_psum_tensor("ps7", [128, 512], F32).ap()

    qT_r = qT_s.rearrange("p (c u) -> p c u", c=4)

    with (
        nc.semaphore("LQ") as hLQ,
        nc.semaphore("LS") as hLS,
        nc.semaphore("LG") as hLG,
        nc.semaphore("CT") as hCT,
        nc.semaphore("CV") as hCV,
        nc.semaphore("CA") as hCA,
        nc.semaphore("CP") as hCP,
        nc.semaphore("CC0") as hCC0,
        nc.semaphore("CC1") as hCC1,
        nc.semaphore("CC2") as hCC2,
        nc.semaphore("CC3") as hCC3,
        nc.Block() as blk,
    ):
        handles = dict(LQ=hLQ, LS=hLS, LG=hLG, CT=hCT, CV=hCV, CA=hCA,
                       CP=hCP, CC0=hCC0, CC1=hCC1, CC2=hCC2, CC3=hCC3)

        def program(em):
            sm = {k: Sem(h, k) for k, h in handles.items()}
            LQ, LS, LG = sm["LQ"], sm["LS"], sm["LG"]
            CT, CV, CA, CP = sm["CT"], sm["CV"], sm["CA"], sm["CP"]
            CCg = [sm["CC0"], sm["CC1"], sm["CC2"], sm["CC3"]]

            st = {}

            em.ins('vector', lambda e: e.memset(ones_cb[:], 1.0), inc=CV)
            em.ins('vector', lambda e: e.memset(ones_rb[:], 1.0), inc=CV)
            em.ins('vector', lambda e: e.memset(ones_rf[:], 1.0), inc=CV)
            em.ins('vector', lambda e: e.memset(stat_s[:, 35:36], 1.0), inc=CV)
            v_ones = CV.v

            em.ins('sync', lambda e: e.dma_start(out=identB_s[:], in_=identB.ap()), inc=LQ, inc_n=16)
            em.ins('sync', lambda e: e.dma_start(out=identF_s[:], in_=identF.ap()), inc=LQ, inc_n=16)
            em.ins('sync', lambda e: e.dma_start(out=tri_s[:], in_=triC.ap()), inc=LQ, inc_n=16)
            em.ins('sync', lambda e: e.dma_start(out=b1r_s[:], in_=b1r), inc=LQ, inc_n=16)
            lq_consts = LQ.v

            em.ins('gpsimd', lambda e: e.dma_start(
                out=wkh_s.rearrange("p (dc c) -> p dc c", dc=16), in_=wk_h.rearrange("(dc p) c -> p dc c", p=128)), inc=LG, inc_n=16)
            em.ins('gpsimd', lambda e: e.dma_start(
                out=wvh_s.rearrange("p (dc c) -> p dc c", dc=16), in_=wv_h.rearrange("(dc p) c -> p dc c", p=128)), inc=LG, inc_n=16)
            em.ins('gpsimd', lambda e: e.dma_start(
                out=wqg_s.rearrange("p (dc c) -> p dc c", dc=16), in_=wq_g.rearrange("(dc p) c -> p dc c", p=128)), inc=LG, inc_n=16)
            em.ins('gpsimd', lambda e: e.dma_start(
                out=wgn_s.rearrange("p (cc o) -> p cc o", cc=16),
                in_=wg_n.rearrange("(cc p) o -> p cc o", p=128),
                allow_slow_non_contiguous=True), inc=LG, inc_n=16)
            em.ins('gpsimd', lambda e: e.dma_start(
                out=wdn_s.rearrange("p (cc o) -> p cc o", cc=4),
                in_=wd_n.rearrange("(cc p) o -> p cc o", p=128),
                allow_slow_non_contiguous=True), inc=LG, inc_n=16)
            em.ins('gpsimd', lambda e: e.dma_start(out=wqbfS, in_=wq_full),
                   inc=LG, inc_n=16)
            em.ins('gpsimd', lambda e: e.dma_start(out=wvbfS, in_=wv_full),
                   inc=LG, inc_n=16)
            lg_p0w = LG.v

            ctb1 = em.ins('tensor', lambda e: e.matmul(
                PS[6][0:128, 0:64], b1r_s[:], identF_s[:],
                start=True, stop=True),
                waits=[(LQ, lq_consts)], inc=CT)
            cv_b1 = em.ins('vector', lambda e: e.tensor_copy(
                b1c_s[:], PS[6][0:128, 0:64]), waits=[(CT, ctb1)], inc=CV)

            ct_abp = 0
            cv_abn_evac = 0
            for dh in range(2):
                lsx = 0
                for cc in range(16):
                    lsx = em.ins('scalar', lambda e: e.dma_start(
                        out=stripes[:, cc * 1024:(cc + 1) * 1024],
                        in_=wqbfS[dh * 1024:(dh + 1) * 1024, cc * 128:(cc + 1) * 128],
                        transpose=True),
                        waits=[(LG, lg_p0w), (CT, ct_abp)], inc=LS, inc_n=16)
                for d8 in range(8):
                    dc = dh * 8 + d8
                    for cc in range(16):
                        w5 = [(LS, lsx)]
                        if cc == 0:
                            w5.append((CV, cv_abn_evac))
                        em.ins('tensor', lambda e: e.matmul(
                            PS[5][:, 0:1],
                            stripes[:, cc * 1024 + d8 * 128:cc * 1024 + (d8 + 1) * 128],
                            wgn_s[:, cc:cc + 1], start=(cc == 0), stop=(cc == 15)),
                            waits=w5, inc=CT)
                    cv_abn_evac = em.ins('vector', lambda e: e.tensor_copy(
                        abn_s[:, dc * 2:dc * 2 + 1], PS[5][:, 0:1]),
                        waits=[(CT, CT.v)], inc=CV)
                ct_abp = CT.v
            lsx = 0
            for vc in range(4):
                lsx = em.ins('scalar', lambda e: e.dma_start(
                    out=stripes[:, vc * 2048:(vc + 1) * 2048],
                    in_=wvbfS[:, vc * 128:(vc + 1) * 128], transpose=True),
                    waits=[(LG, lg_p0w), (CT, ct_abp)], inc=LS, inc_n=16)
            for dc in range(16):
                for vc in range(4):
                    w5 = [(LS, lsx)]
                    if vc == 0:
                        w5.append((CV, cv_abn_evac))
                    em.ins('tensor', lambda e: e.matmul(
                        PS[5][:, 0:1],
                        stripes[:, vc * 2048 + dc * 128:vc * 2048 + (dc + 1) * 128],
                        wdn_s[:, vc:vc + 1], start=(vc == 0), stop=(vc == 3)),
                        waits=w5, inc=CT)
                cv_abn_evac = em.ins('vector', lambda e: e.tensor_copy(
                    abn_s[:, dc * 2 + 1:dc * 2 + 2], PS[5][:, 0:1]),
                    waits=[(CT, CT.v)], inc=CV)
            ct_abp = CT.v
            cv_ab = CV.v
            ct_ab_final = CT.v
            cv_abn_done = CV.v

            xb_free = [0, 0]
            sumsq_free = 0
            sumsq_free_p = [0, 0]

            def emit_ln_tile(src_ap, dst_ap, xpar, lpar, load_waits, norm_waits):
                sb0 = xpar * 10
                lq1 = em.ins('sync', lambda e: e.dma_start(
                    out=xb_s[:, xpar * 2048:(xpar + 1) * 2048], in_=src_ap),
                    waits=list(load_waits) + [(CA, xb_free[xpar])], inc=LQ, inc_n=16)
                em.ins('vector', lambda e: e.reduce_sum(
                    stat_s[:, sb0 + 3:sb0 + 4], xb_s[:, xpar * 2048:(xpar + 1) * 2048],
                    axis=AX.X), waits=[(LQ, lq1)], inc=CV)
                em.fence('vector', CV)
                em.ins('vector', lambda e: e.tensor_scalar(
                    stat_s[:, sb0:sb0 + 1], stat_s[:, sb0 + 3:sb0 + 4], 1.0 / 2048,
                    None, ALU.mult), inc=CV)
                caq = em.ins('scalar', lambda e: e.activation(
                    sqd_s[:, xpar * 1024:(xpar + 1) * 1024],
                    xb_s[:, xpar * 2048:(xpar + 1) * 2048].rearrange(
                        "p (a b) -> p a b", a=2)[:, 0:1, :], AF.Square,
                    accum_out=stat_s[:, sb0 + 4:sb0 + 5]),
                    waits=[(LQ, lq1), (CV, sumsq_free_p[xpar])], inc=CA)
                caq = em.ins('scalar', lambda e: e.activation(
                    sqd_s[:, xpar * 1024:(xpar + 1) * 1024],
                    xb_s[:, xpar * 2048:(xpar + 1) * 2048].rearrange(
                        "p (a b) -> p a b", a=2)[:, 1:2, :], AF.Square,
                    accum_out=stat_s[:, sb0 + 5:sb0 + 6]), inc=CA)
                cv1 = em.ins('vector', lambda e: e.tensor_tensor(
                    stat_s[:, sb0 + 4:sb0 + 5], stat_s[:, sb0 + 4:sb0 + 5],
                    stat_s[:, sb0 + 5:sb0 + 6], ALU.add),
                    waits=[(CA, caq)], inc=CV)
                em.fence('vector', CV)
                cv1 = em.ins('vector', lambda e: e.tensor_scalar(
                    stat_s[:, sb0 + 6:sb0 + 7], stat_s[:, sb0 + 4:sb0 + 5], 1.0 / 2048,
                    EPS, ALU.mult, ALU.add), inc=CV)
                sumsq_free_p[xpar] = cv1
                em.ins('vector', lambda e: e.tensor_tensor(
                    stat_s[:, sb0 + 7:sb0 + 8], stat_s[:, sb0:sb0 + 1],
                    stat_s[:, sb0:sb0 + 1], ALU.mult), inc=CV)
                em.fence('vector', CV)
                cv3 = em.ins('vector', lambda e: e.tensor_tensor(
                    stat_s[:, sb0 + 6:sb0 + 7], stat_s[:, sb0 + 6:sb0 + 7],
                    stat_s[:, sb0 + 7:sb0 + 8], ALU.subtract), inc=CV)
                em.ins('scalar', lambda e: e.activation(
                    stat_s[:, sb0 + 8:sb0 + 9], stat_s[:, sb0 + 6:sb0 + 7], AF.Ln),
                    waits=[(CV, cv3)], inc=CA)
                ca2 = em.ins('scalar', lambda e: e.activation(
                    stat_s[:, sb0 + 1:sb0 + 2], stat_s[:, sb0 + 8:sb0 + 9], AF.Exp,
                    scale=-0.5), inc=CA)
                cv4 = em.ins('vector', lambda e: e.tensor_scalar(
                    stat_s[:, sb0 + 2:sb0 + 3], stat_s[:, sb0:sb0 + 1],
                    stat_s[:, sb0 + 1:sb0 + 2], -1.0,
                    ALU.mult, ALU.mult), waits=[(CA, ca2)], inc=CV)
                can = em.ins('scalar', lambda e: e.activation(
                    lnb_s[:, lpar * 2048:(lpar + 1) * 2048],
                    xb_s[:, xpar * 2048:(xpar + 1) * 2048], AF.Identity,
                    bias=stat_s[:, sb0 + 2:sb0 + 3], scale=stat_s[:, sb0 + 1:sb0 + 2]),
                    waits=[(CV, cv4)] + list(norm_waits), inc=CA)
                xb_free[xpar] = can
                lqs = em.ins('sync', lambda e: e.dma_start(
                    out=dst_ap, in_=lnb_s[:, lpar * 2048:(lpar + 1) * 2048]),
                    waits=[(CA, can)], inc=LQ, inc_n=16)
                return can, lqs

            qchunks = []
            for r in range(64):
                qchunks.append((w1[r * 128:(r + 1) * 128, :],
                                w1qS[r * 128:(r + 1) * 128, :], 0))
            for r in range(16):
                for c in range(4):
                    qchunks.append((w2[r * 128:(r + 1) * 128, c * 2048:(c + 1) * 2048],
                                    w2qS[r * 128:(r + 1) * 128, c * 2048:(c + 1) * 2048], 2))
            qs = dict(i=0, free=[(CV, 0), (CP, 0)], outlq=[0, 0], laststore=0,
                      abuf_lq=[0, 0], sq_ca=0, w1done=0, w2done=0)

            def emit_quant_chunk():
                if qs['i'] >= len(qchunks):
                    return
                src, dst, tsel = qchunks[qs['i']]
                par = qs['i'] % 2
                role = 'vector'
                CX = CV
                wf = xb_s[:, par * 2048:(par + 1) * 2048]
                abuf = lnb_s[:, par * 2048:(par + 1) * 2048]
                obuf = sqd_s[:] if par == 0 else qC_s[:]
                lq1 = em.ins('gpsimd', lambda e: e.dma_start(out=wf, in_=src),
                             waits=[qs['free'][par]], inc=LG, inc_n=16)
                em.ins(role, lambda e: e.tensor_scalar(
                    obuf, wf, thr_s[:, tsel + 1:tsel + 2], 1.0, ALU.is_ge, ALU.subtract),
                    waits=[(LG, lq1), (CV, cv_thr), (LG, qs['outlq'][par]),
                           (CA, qs['sq_ca'])], inc=CX)
                c2x = em.ins(role, lambda e: e.tensor_scalar(
                    abuf, wf, thr_s[:, tsel:tsel + 1], None, ALU.is_ge),
                    waits=[(LQ, qs['abuf_lq'][par])], inc=CX)
                c3 = em.ins('vector', lambda e: e.tensor_tensor(
                    obuf, obuf, abuf, ALU.add), inc=CV)
                qs['free'][par] = (CV, c3)
                lqs = em.ins('gpsimd', lambda e: e.dma_start(out=dst, in_=obuf),
                             waits=[(CV, c3)], inc=LG, inc_n=16)
                qs['outlq'][par] = lqs
                qs['laststore'] = lqs
                if qs['i'] == 63:
                    qs['w1done'] = lqs
                if qs['i'] == len(qchunks) - 1:
                    qs['w2done'] = lqs
                qs['i'] += 1


            abs_chunks = []
            for r in range(64):
                for c in range(2):
                    abs_chunks.append((w1[r * 128:(r + 1) * 128, c * 1024:(c + 1) * 1024], 0))
            for r in range(16):
                for c in range(8):
                    abs_chunks.append((w2[r * 128:(r + 1) * 128, c * 1024:(c + 1) * 1024], 1))
            abs_state = dict(i=0, sv=[False, False], sp=[False, False],
                             vfree=0, pfree=0, cvd=0, cpd=0)

            def emit_abs_chunk():
                a = abs_state
                if a['i'] >= len(abs_chunks):
                    return
                src, m = abs_chunks[a['i']]
                par = a['i'] % 2
                prevfree = [(CV, a['vfree'])] if par == 0 else [(CV, a['pfree'])]
                lg1 = em.ins('gpsimd', lambda e: e.dma_start(
                    out=absb_s[:, par * 1024:(par + 1) * 1024], in_=src),
                    waits=prevfree, inc=LG, inc_n=16)
                if par == 0:
                    em.ins('vector', lambda e: e.reduce_sum(
                        stat_s[:, 33:34], absb_s[:, 0:1024],
                        axis=AX.X, apply_absolute_value=True),
                        waits=[(LG, lg1)], inc=CV)
                    em.fence('vector', CV)
                    started = a['sv']
                    acc, scol = absacc_v, 33
                else:
                    em.ins('scalar', lambda e: e.activation(
                        sqd_s[:, 0:1024], absb_s[:, 1024:2048], AF.Abs,
                        accum_out=stat_s[:, 34:35]),
                        waits=[(LG, lg1), (CV, a['pfree'])], inc=CA)
                    em.wait('vector', CA, CA.v)
                    started = a['sp']
                    acc, scol = absacc_p, 34
                if not started[m]:
                    c2 = em.ins('vector', lambda e: e.tensor_copy(
                        acc[:, m:m + 1], stat_s[:, scol:scol + 1]), inc=CV)
                    started[m] = True
                else:
                    c2 = em.ins('vector', lambda e: e.tensor_tensor(
                        acc[:, m:m + 1], acc[:, m:m + 1], stat_s[:, scol:scol + 1],
                        ALU.add), inc=CV)
                if par == 0:
                    a['vfree'] = c2
                    a['cvd'] = c2
                else:
                    a['pfree'] = c2
                    a['cvd'] = c2
                a['i'] += 1

            lnb_lq = [0, 0]
            ca_mb_prev = 0
            cv_evK = 0
            cv_evV = 0
            cv_evG = cv_ab
            cv_evQ = 0
            ct_half = [ct_ab_final, ct_ab_final]
            lq_spill = {}
            for sg in range(SG):
                for tt in range(SGT):
                    t = sg * SGT + tt
                    par = t % 2
                    can, lqs = emit_ln_tile(
                        x[t * 128:(t + 1) * 128, :], lnS[t * 128:(t + 1) * 128, :],
                        par, par,
                        load_waits=[],
                        norm_waits=[(LQ, lnb_lq[par]),
                                    (CT, ct_ab_final if t < 2 else 0)])
                    lnb_lq[par] = lqs
                    for _ in range(4):
                        emit_abs_chunk()
                lq_spill[sg] = LQ.v

            qs['abuf_lq'] = [lnb_lq[0], lnb_lq[1]]
            qs['sq_ca'] = CA.v
            cvt = em.ins('vector', lambda e: e.tensor_tensor(
                absacc_v[:, 0:2], absacc_v[:, 0:2], absacc_p[:, 0:2], ALU.add),
                waits=[(CV, abs_state['cvd'])], inc=CV)
            ctt1 = em.ins('tensor', lambda e: e.matmul(
                PS[5][0:1, 0:2], stat_s[:, 35:36], absacc_v[:, 0:2],
                start=True, stop=True), waits=[(CV, cvt)], inc=CT)
            em.ins('vector', lambda e: e.tensor_copy(trow_s[0:1, 0:2], PS[5][0:1, 0:2]),
                   waits=[(CT, ctt1)], inc=CV)
            em.fence('vector', CV)
            cvc = em.ins('vector', lambda e: e.tensor_scalar(
                trow_s[0:1, 2:4], trow_s[0:1, 0:2], 0.7 / (DFF * D), None, ALU.mult),
                inc=CV)
            em.ins('tensor', lambda e: e.matmul(
                PS[5][:, 2:3], ones_rf[0:1, :], trow_s[0:1, 2:3], start=True, stop=True),
                waits=[(CV, cvc)], inc=CT)
            ctb2 = em.ins('tensor', lambda e: e.matmul(
                PS[5][:, 3:4], ones_rf[0:1, :], trow_s[0:1, 3:4], start=True, stop=True),
                inc=CT)
            em.ins('vector', lambda e: e.tensor_copy(thr_s[:, 0:1], PS[5][:, 2:3]),
                   waits=[(CT, ctb2)], inc=CV)
            em.ins('vector', lambda e: e.tensor_scalar(
                thr_s[:, 1:2], PS[5][:, 2:3], -1.0, None, ALU.mult), inc=CV)
            em.ins('vector', lambda e: e.tensor_copy(thr_s[:, 2:3], PS[5][:, 3:4]), inc=CV)
            cv_thr = em.ins('vector', lambda e: e.tensor_scalar(
                thr_s[:, 3:4], PS[5][:, 3:4], -1.0, None, ALU.mult), inc=CV)


            for sg in range(SG):
                for _ in range(8):
                    emit_quant_chunk()
                half = sg % 2
                hb = half * 8192

                ls_str = 0
                for dc in range(16):
                    ls_str = em.ins('scalar', lambda e: e.dma_start(
                        out=stripes[:, hb + dc * 512:hb + (dc + 1) * 512],
                        in_=lnS[sg * 512:(sg + 1) * 512, dc * 128:(dc + 1) * 128],
                        transpose=True),
                        waits=[(LQ, lq_spill[sg]), (CT, ct_half[half])],
                        inc=LS, inc_n=16)

                def stripe(dc, lo, n):
                    return stripes[:, hb + dc * 512 + lo:hb + dc * 512 + lo + n]

                for dc in range(16):
                    em.ins('tensor', lambda e: e.matmul(
                        PS[0][:], wkh_s[:, dc * 128:(dc + 1) * 128], stripe(dc, 0, 512),
                        start=(dc == 0), stop=(dc == 15)),
                        waits=[(LS, ls_str), (CV, cv_evK), (LG, lg_p0w)], inc=CT)
                cv_evK = em.ins('vector', lambda e: e.tensor_copy(
                    kT_s[:, sg * 512:(sg + 1) * 512], PS[0][:]),
                    waits=[(CT, CT.v)], inc=CV)

                for tt in range(SGT):
                    for dc in range(16):
                        em.ins('tensor', lambda e: e.matmul(
                            PS[1][:, 0:128], stripe(dc, tt * 128, 128),
                            wvh_s[:, dc * 128:(dc + 1) * 128],
                            start=(dc == 0), stop=(dc == 15)),
                            waits=[(CV, cv_evV)], inc=CT)
                    cv_evV = em.ins('vector', lambda e: e.tensor_copy(
                        v_s[:, (sg * SGT + tt) * 128:(sg * SGT + tt + 1) * 128],
                        PS[1][:, 0:128]), waits=[(CT, CT.v)], inc=CV)

                for tt in range(SGT):
                    for dc in range(16):
                        em.ins('tensor', lambda e: e.matmul(
                            PS[3][:, 0:2], stripe(dc, tt * 128, 128),
                            abn_s[:, dc * 2:dc * 2 + 2],
                            start=(dc == 0), stop=(dc == 15)),
                            waits=[(CV, cv_evG)], inc=CT)
                    em.ins('vector', lambda e: e.tensor_copy(
                        gdc_s[:, tt:tt + 1], PS[3][:, 0:1]),
                        waits=[(CT, CT.v), (CA, ca_mb_prev)], inc=CV)
                    cv_evG = em.ins('vector', lambda e: e.tensor_copy(
                        gdc_s[:, 8 + tt:9 + tt], PS[3][:, 1:2]), inc=CV)

                ca_t = em.ins('scalar', lambda e: e.activation(
                    stat_s[:, 20:24], gdc_s[:, 0:4], AF.Tanh, scale=0.5),
                    waits=[(CV, cv_evG)], inc=CA)
                em.ins('vector', lambda e: e.tensor_scalar(
                    stat_s[:, 20:24], stat_s[:, 20:24], 0.5, 0.5, ALU.mult, ALU.add),
                    waits=[(CA, ca_t)], inc=CV)
                em.fence('vector', CV)
                cv6 = em.ins('vector', lambda e: e.tensor_tensor(
                    stat_s[:, 24:28], stat_s[:, 20:24], gdc_s[:, 8:12], ALU.mult),
                    waits=[(CA, ca_mb_prev)], inc=CV)
                ca_s = em.ins('scalar', lambda e: e.activation(
                    stat_s[:, 28:32], stat_s[:, 24:28], AF.Sign),
                    waits=[(CV, cv6)], inc=CA)
                ca_mb_prev = ca_s
                em.ins('vector', lambda e: e.tensor_scalar(
                    stat_s[:, 28:32], stat_s[:, 28:32], 1.0, 5e29, ALU.subtract, ALU.mult),
                    waits=[(CA, ca_s)], inc=CV)
                em.fence('vector', CV)
                em.ins('vector', lambda e: e.tensor_tensor(
                    mb_s[:, sg * 4:(sg + 1) * 4], stat_s[:, 24:28], stat_s[:, 28:32],
                    ALU.add), inc=CV)

                for cc in range(4):
                    for dc in range(16):
                        em.ins('tensor', lambda e: e.matmul(
                            PS[2][:], wqg_s[:, dc * 512 + cc * 128:dc * 512 + (cc + 1) * 128],
                            stripe(dc, 0, 512), start=(dc == 0), stop=(dc == 15)),
                            waits=[(CV, cv_evQ)], inc=CT)
                    cv_evQ = em.ins('vector', lambda e: e.tensor_copy(
                        qT_s[:, cc * S + sg * 512:cc * S + (sg + 1) * 512], PS[2][:]),
                        waits=[(CT, CT.v)], inc=CV)

                ct_half[half] = CT.v

            while abs_state['i'] < len(abs_chunks):
                emit_abs_chunk()

            lg_wo = em.ins('gpsimd', lambda e: e.dma_start(
                out=wo_s.rearrange("p (ch c) -> p ch c", ch=4), in_=wo_r.rearrange("(ch p) c -> p ch c", p=128)),
                waits=[(CT, CT.v)], inc=LG, inc_n=16)

            if debug:
                em.ins('sync', lambda e: e.dma_start(out=d_kT, in_=kT_s[:]),
                       waits=[(CV, CV.v)], inc=LQ, inc_n=16)
                em.ins('sync', lambda e: e.dma_start(out=d_v, in_=v_s[:]), inc=LQ, inc_n=16)
                em.ins('sync', lambda e: e.dma_start(out=d_qT, in_=qT_s[:]), inc=LQ, inc_n=16)
                em.ins('sync', lambda e: e.dma_start(out=d_mb, in_=mb_s[:]), inc=LQ, inc_n=16)
                em.ins('sync', lambda e: e.dma_start(out=d_thr, in_=thr_s[:]), inc=LQ, inc_n=16)
                em.ins('sync', lambda e: e.dma_start(out=d_abn, in_=abn_s[:]), inc=LQ, inc_n=16)

            p25_state = dict(lq_xm_prev=0, cv_xm=0)
            p25_done = set()

            def emit_p25_tile(l):
                nonlocal sumsq_free
                p25_done.add(l)
                g = l // 4
                lqa = em.ins('gpsimd', lambda e: e.dma_start(
                    out=xb_s[:, 0:2048], in_=attnR[l * 128:(l + 1) * 128, :]),
                    waits=[(CCg[g], 1), qs['free'][0],
                           (LG, p25_state['lq_xm_prev'])],
                    inc=LG, inc_n=16)
                lqb = em.ins('gpsimd', lambda e: e.dma_start(
                    out=xb_s[:, 2048:4096], in_=xown[l * 128:(l + 1) * 128, :]),
                    waits=[qs['free'][1]], inc=LG, inc_n=16)
                em.ins('vector', lambda e: e.reduce_sum(
                    stat_s[:, 3:4], xb_s[:, 0:2048], axis=AX.X),
                    waits=[(LG, lqa)], inc=CV)
                em.fence('vector', CV)
                em.ins('vector', lambda e: e.tensor_scalar(
                    stat_s[:, 0:1], stat_s[:, 3:4], 1.0 / 2048, None, ALU.mult), inc=CV)
                caq = em.ins('scalar', lambda e: e.activation(
                    sqd_s[:], xb_s[:, 0:2048], AF.Square, accum_out=stat_s[:, 4:5]),
                    waits=[(LG, lqa), (LG, qs['laststore']), (CV, sumsq_free)], inc=CA)
                cv1 = em.ins('vector', lambda e: e.tensor_scalar(
                    stat_s[:, 6:7], stat_s[:, 4:5], 1.0 / 2048, EPS, ALU.mult, ALU.add),
                    waits=[(CA, caq)], inc=CV)
                sumsq_free = cv1
                em.ins('vector', lambda e: e.tensor_tensor(
                    stat_s[:, 7:8], stat_s[:, 0:1], stat_s[:, 0:1], ALU.mult), inc=CV)
                em.fence('vector', CV)
                cv3 = em.ins('vector', lambda e: e.tensor_tensor(
                    stat_s[:, 6:7], stat_s[:, 6:7], stat_s[:, 7:8], ALU.subtract), inc=CV)
                em.ins('scalar', lambda e: e.activation(
                    stat_s[:, 8:9], stat_s[:, 6:7], AF.Ln), waits=[(CV, cv3)], inc=CA)
                ca2 = em.ins('scalar', lambda e: e.activation(
                    stat_s[:, 1:2], stat_s[:, 8:9], AF.Exp, scale=-0.5), inc=CA)
                cv4 = em.ins('vector', lambda e: e.tensor_scalar(
                    stat_s[:, 2:3], stat_s[:, 0:1], stat_s[:, 1:2], -1.0,
                    ALU.mult, ALU.mult), waits=[(CA, ca2)], inc=CV)
                em.fence('vector', CV)
                em.ins('vector', lambda e: e.tensor_scalar(
                    xb_s[:, 0:2048], xb_s[:, 0:2048], stat_s[:, 1:2], stat_s[:, 2:3],
                    ALU.mult, ALU.add), waits=[(CV, cv4)], inc=CV)
                cv_xm = em.ins('vector', lambda e: e.tensor_tensor(
                    xb_s[:, 0:2048], xb_s[:, 0:2048], xb_s[:, 2048:4096], ALU.add),
                    waits=[(LG, lqb)], inc=CV)
                p25_state['cv_xm'] = cv_xm
                p25_state['lq_xm_prev'] = em.ins('gpsimd', lambda e: e.dma_start(
                    out=xmS[l * 128:(l + 1) * 128, :], in_=xb_s[:, 0:2048]),
                    waits=[(CV, cv_xm)], inc=LG, inc_n=16)

            bounce_tiles = []
            for f0 in range(8):
                for d0 in range(16):
                    bounce_tiles.append((
                        w1qS[f0 * 1024:(f0 + 1) * 1024, d0 * 128:(d0 + 1) * 128],
                        w1qT[d0 * 128:(d0 + 1) * 128, f0 * 1024:(f0 + 1) * 1024], 0))
            for f0 in range(64):
                for m0 in range(2):
                    bounce_tiles.append((
                        w2qS[m0 * 1024:(m0 + 1) * 1024, f0 * 128:(f0 + 1) * 128],
                        w2qT[f0 * 128:(f0 + 1) * 128, m0 * 1024:(m0 + 1) * 1024], 1))
            bounce_state = dict(i=0, lq=0, plq=[0, 0])

            def emit_bounce_pair():
                bi = bounce_state['i']
                if bi >= len(bounce_tiles):
                    return
                bsrc, bdst, m = bounce_tiles[bi]
                par = bi % 2
                gate = qs['w1done'] if m == 0 else qs['w2done']
                lsb = em.ins('sync', lambda e: e.dma_start(
                    out=w2qT_s[:, par * 1024:(par + 1) * 1024], in_=bsrc,
                    transpose=True),
                    waits=[(LG, gate), (LQ, bounce_state['plq'][par])],
                    inc=LQ, inc_n=16)
                bounce_state['plq'][par] = em.ins('sync', lambda e: e.dma_start(
                    out=bdst, in_=w2qT_s[:, par * 1024:(par + 1) * 1024]),
                    waits=[(LQ, lsb)], inc=LQ, inc_n=16)
                bounce_state['lq'] = bounce_state['plq'][par]
                bounce_state['i'] = bi + 1


            cv_p1 = CV.v
            em.wait('tensor', CV, cv_p1)
            em.wait('scalar', CV, cv_p1)
            em.wait('scalar', CP, CP.v)

            PSA = [PS[0], PS[1], PS[5], PS[6]]
            psa_free = [0, 0, 0, 0]
            pt_free = [0] * 8
            cv_t4_par = [0, 0]
            ct_t4z_prev = 0
            ct_p1_stripes = CT.v
            cv_onorm_prev = 0
            cv_evac_prev = 0
            cv_evac_pair = [0, 0]
            ca_rb_prev = 0
            ct_B_prev = 0
            ct_wo_prev = 0
            cv_zp_prev = 0
            lq_evac_prev = 0
            ca_last_exp = 0
            for t in range(NT):
                ct_pv_last = 0
                nk = t + 1
                ngrp = (nk + 3) // 4
                exp_of = {}
                exp_last_grp = {-1: ca_last_exp}
                for g in range(ngrp + 1):
                    if g < ngrp:
                        kts = range(4 * g, min(4 * g + 4, nk))
                        for kt in kts:
                            bank = kt % 4
                            wS = [(CA, psa_free[bank])]
                            ctS = em.ins('tensor', lambda e: e.matmul(
                                PSA[bank][:], kT_s[:, kt * 128:(kt + 1) * 128],
                                qT_r[:, :, t * 128:(t + 1) * 128],
                                start=True, stop=True), waits=wS, inc=CT)
                            if kt == t:
                                cvtri = em.ins('vector', lambda e: e.tensor_tensor(
                                    PSA[bank][:], PSA[bank][:], tri_s[:], ALU.add),
                                    waits=[(CT, ctS)], inc=CV)
                                exp_of[kt] = ('cv', cvtri)
                            else:
                                exp_of[kt] = ('ct', ctS)
                    if g >= 1:
                        gp = (g - 1) % 2
                        hb2 = gp * 2048
                        kts = range(4 * (g - 1), min(4 * (g - 1) + 4, nk))
                        first = True
                        for kt in kts:
                            bank = kt % 4
                            wpv = [(CA, exp_last_grp[g - 1])] if first else []
                            first = False
                            if kt == 0:
                                wpv.append((CV, cv_onorm_prev))
                            ctP = em.ins('tensor', lambda e: e.matmul(
                                PS[2][:], v_s[:, kt * 128:(kt + 1) * 128],
                                pT_s[:, hb2 + bank * 512:hb2 + (bank + 1) * 512],
                                start=(kt == 0), stop=(kt == t)), waits=wpv, inc=CT)
                            ct_pv_last = ctP
                            pt_free[gp * 4 + bank] = ctP
                        g0 = 4 * (g - 1)
                        w = min(4, nk - g0)
                        if g == 1:
                            cv_t4 = em.ins('vector', lambda e: e.tensor_copy(
                                T4_s[:, 0:w * 512], pT_s[:, hb2:hb2 + w * 512]),
                                waits=[(CA, exp_last_grp[g - 1]), (CT, ct_t4z_prev),
                                       (CT, ct_p1_stripes)], inc=CV)
                            if w < 4:
                                cv_t4 = em.ins('vector', lambda e: e.memset(
                                    T4_s[:, w * 512:2048], 0.0), inc=CV)
                        else:
                            cv_t4 = em.ins('vector', lambda e: e.tensor_tensor(
                                T4_s[:, 0:w * 512], T4_s[:, 0:w * 512],
                                pT_s[:, hb2:hb2 + w * 512], ALU.add),
                                waits=[(CA, exp_last_grp[g - 1])], inc=CV)
                        cv_t4_par[gp] = cv_t4
                    if g < ngrp:
                        kts = range(4 * g, min(4 * g + 4, nk))
                        for kt in kts:
                            bank = kt % 4
                            kind, val = exp_of[kt]
                            wexp = [(CV, val)] if kind == 'cv' else [(CT, val)]
                            gp2 = g % 2
                            caE = em.ins('scalar', lambda e: e.activation(
                                pT_s[:, gp2 * 2048 + bank * 512:
                                     gp2 * 2048 + (bank + 1) * 512], PSA[bank][:],
                                AF.Exp, bias=mb_s[:, kt:kt + 1], scale=SCALE),
                                waits=wexp + [(CT, pt_free[gp2 * 4 + bank]),
                                              (CV, cv_t4_par[gp2])], inc=CA)
                            psa_free[bank] = caE
                            ca_last_exp = caE
                        exp_last_grp[g] = ca_last_exp

                cv_t4b = em.ins('vector', lambda e: e.tensor_copy(
                    T4b_s[:], T4_s[:]), waits=[(CT, ct_t4z_prev)], inc=CV)
                nzm = min(4, nk)
                for j in range(nzm):
                    wz = [(CV, cv_t4b)]
                    if j == 0:
                        wz.append((CV, cv_zp_prev))
                    em.ins('tensor', lambda e: e.matmul(
                        PS[3][0:1, :], ones_cb[:], T4b_s[:, j * 512:(j + 1) * 512],
                        start=(j == 0), stop=(j == nzm - 1)), waits=wz, inc=CT)
                ct_t4z_prev = CT.v
                cv_zp = em.ins('vector', lambda e: e.tensor_scalar(
                    rz_s[0:1, :], PS[3][0:1, :], 1e-30, None, ALU.add),
                    waits=[(CT, ct_t4z_prev), (CA, ca_rb_prev), (LQ, lq_evac_prev)],
                    inc=CV)
                cv_zp_prev = cv_zp
                em.fence('vector', CV)
                em.ins('vector', lambda e: e.reciprocal(rz_s[0:1, :], rz_s[0:1, :]),
                       inc=CV)
                em.fence('vector', CV)
                cv_rzb = em.ins('vector', lambda e: e.tensor_copy(
                    rzb_s[0:1, :], rz_s[0:1, :]),
                    waits=[(CT, ct_B_prev), (CT, ct_wo_prev)], inc=CV)
                ctB = em.ins('tensor', lambda e: e.matmul(
                    PS[4][:], ones_rb[0:1, :], rzb_s[0:1, :], start=True, stop=True),
                    waits=[(CV, cv_rzb), (CA, ca_rb_prev), (CV, cv_evac_prev)], inc=CT)
                ct_B_prev = ctB
                ca_rb = em.ins('scalar', lambda e: e.copy(rb_s[:], PS[4][:]),
                               waits=[(CT, ctB), (CV, cv_onorm_prev)], inc=CA)
                ca_rb_prev = ca_rb
                cv_on = em.ins('vector', lambda e: e.tensor_tensor(
                    onorm_s[:], PS[2][:], rb_s[:], ALU.mult),
                    waits=[(CA, ca_rb), (CT, ct_pv_last)], inc=CV)
                cv_onorm_prev = cv_on

                WOB = [PS7, PS[4]]
                for pair in range(2):
                    for hl in range(4):
                        for oc2 in range(2):
                            oc = pair * 2 + oc2
                            w = [(CV, cv_on), (LG, lg_wo)]
                            if hl == 0:
                                w.append((CV, cv_evac_pair[1] if pair == 0
                                          else cv_evac_pair[0]))
                            em.ins('tensor', lambda e: e.matmul(
                                WOB[oc2][:], onorm_s[:, hl * 128:(hl + 1) * 128],
                                wo_s[:, hl * 2048 + oc * 512:hl * 2048 + (oc + 1) * 512],
                                start=(hl == 0), stop=(hl == 3)), waits=w, inc=CT)
                    ct_wo = CT.v
                    ct_wo_prev = ct_wo
                    for oc2 in range(2):
                        oc = pair * 2 + oc2
                        cvs = em.ins('vector', lambda e: e.tensor_copy(
                            evac_s[:], WOB[oc2][:]),
                            waits=[(CT, ct_wo), (LQ, lq_evac_prev)], inc=CV)
                        lq_evac_prev = em.ins('sync', lambda e: e.dma_start(
                            out=attnP[t * 128:(t + 1) * 128,
                                      oc * 512:(oc + 1) * 512],
                            in_=evac_s[:]), waits=[(CV, cvs)], inc=LQ, inc_n=16)
                    cv_evac_pair[pair] = cvs
                cv_evac_prev = cvs


                for _ in range(4):
                    emit_bounce_pair()
                if t in (45, 49, 53):
                    for l in range(4 * ((t - 45) // 4), 4 * ((t - 45) // 4) + 4):
                        emit_p25_tile(l)

                if t % 16 == 15:
                    g = t // 16
                    em.ins('gpsimd', lambda e: e.collective_compute(
                        "ReduceScatter", ALU.add,
                        replica_groups=[[0, 1, 2, 3], [4, 5, 6, 7]],
                        ins=[attnP[g * 2048:(g + 1) * 2048, :].opt()],
                        outs=[attnR[g * 512:(g + 1) * 512, :].opt()]),
                        waits=[(LQ, lq_evac_prev)], inc=CCg[g])

            while qs['i'] < len(qchunks):
                emit_quant_chunk()
            while bounce_state['i'] < len(bounce_tiles):
                emit_bounce_pair()
            lq_bounce_all = bounce_state['lq']
            for l in range(16):
                if l not in p25_done:
                    emit_p25_tile(l)
            lg_xm_all = p25_state['lq_xm_prev']

            ln3_lq = [0, 0]
            for l in range(16):
                par = l % 2
                can, lqs = emit_ln_tile(
                    xmS[l * 128:(l + 1) * 128, :], ln3S[l * 128:(l + 1) * 128, :],
                    par, par,
                    load_waits=[(LG, lg_xm_all)],
                    norm_waits=[(LQ, ln3_lq[par]), qs['free'][par]])
                ln3_lq[par] = lqs
            lq_ln3_all = LQ.v

            lqb2 = em.ins('sync', lambda e: e.dma_start(out=b2r_s[0:1, :], in_=b2row),
                          waits=[(CT, CT.v)], inc=LQ, inc_n=16)
            cv_now = CV.v
            ca_now = CA.v
            for ocb in range(4):
                ctb = em.ins('tensor', lambda e: e.matmul(
                    PS[ocb][:], ones_rf[0:1, :], b2r_s[0:1, ocb * 512:(ocb + 1) * 512],
                    start=True, stop=True),
                    waits=[(LQ, lqb2), (CV, cv_now), (CA, ca_now)], inc=CT)
                em.ins('vector', lambda e: e.tensor_copy(
                    b2b_s[:, ocb * 512:(ocb + 1) * 512], PS[ocb][:]),
                    waits=[(CT, ctb)], inc=CV)
            cv_b2b = CV.v

            ct_f1_half = [CT.v, CT.v]
            ct_w2blk = [0, 0]
            ca_hs = [0, 0]
            cv_out2 = cv_b2b
            cv_final = 0
            lq_out_last = 0
            for sg2 in range(4):
                ls_ln3 = 0
                for dc in range(16):
                    ls_ln3 = em.ins('scalar', lambda e: e.dma_start(
                        out=ln3T_s[:, dc * 512:(dc + 1) * 512],
                        in_=ln3S[sg2 * 512:(sg2 + 1) * 512, dc * 128:(dc + 1) * 128],
                        transpose=True),
                        waits=[(LQ, lq_ln3_all), (CT, ct_f1_half[0])],
                        inc=LS, inc_n=16)

                w1qT_r = w1qT.rearrange("(dc p) f -> p dc f", p=128)
                for fbg in range(64):
                    wpar = fbg % 2
                    lss = em.ins('sync', lambda e: e.dma_start(
                        out=w1b_s[:, wpar * 2048:(wpar + 1) * 2048],
                        in_=w1qT_r[:, :, fbg * 128:(fbg + 1) * 128]),
                        waits=[(LQ, lq_bounce_all), (CT, ct_f1_half[wpar])],
                        inc=LQ, inc_n=16)
                    pb = fbg % 2
                    for dc in range(16):
                        w = [(LQ, lss), (LS, ls_ln3)]
                        if dc == 0:
                            w.append((CA, ca_hs[pb]))
                            if fbg < 2 and sg2 > 0:
                                w.append((CV, cv_out2))
                        em.ins('tensor', lambda e: e.matmul(
                            PS[pb][:],
                            w1b_s[:, wpar * 2048 + dc * 128:
                                  wpar * 2048 + (dc + 1) * 128],
                            ln3T_s[:, dc * 512:(dc + 1) * 512],
                            start=(dc == 0), stop=(dc == 15)), waits=w, inc=CT)
                    ct_f1_half[wpar] = CT.v
                    ca_hs[pb] = em.ins('scalar', lambda e: e.activation(
                        Hs_s[:, fbg * 512:(fbg + 1) * 512], PS[pb][:], AF.Gelu,
                        bias=b1c_s[:, fbg:fbg + 1]),
                        waits=[(CT, CT.v), (CV, cv_b1)], inc=CA)
                ca_f1_done = max(ca_hs)

                for dmh in range(2):
                    for fbp in range(32):
                        par = fbp % 2
                        wbuf = w2qT_s if par == 0 else w2p_s
                        lsw = em.ins('scalar', lambda e: e.dma_start(
                            out=wbuf.rearrange("p (two m) -> p two m", two=2),
                            in_=w2qT[fbp * 256:(fbp + 1) * 256,
                                     dmh * 1024:(dmh + 1) * 1024].rearrange(
                                         "(two p) m -> p two m", p=128)),
                            waits=[(LQ, lq_bounce_all), (CT, ct_w2blk[par])],
                            inc=LS, inc_n=16)
                        for half in range(2):
                            fb = fbp * 2 + half
                            for ts in range(4):
                                for oc2 in range(2):
                                    bi = ts * 2 + oc2
                                    bank = PS[bi] if bi < 7 else PS7
                                    w = [(LS, lsw), (CA, ca_f1_done)]
                                    if fb == 0:
                                        w.append((CV, cv_out2))
                                    em.ins('tensor', lambda e: e.matmul(
                                        bank[:],
                                        Hs_s[:, fb * 512 + ts * 128:
                                             fb * 512 + (ts + 1) * 128],
                                        wbuf[:, half * 1024 + oc2 * 512:
                                             half * 1024 + (oc2 + 1) * 512],
                                        start=(fb == 0), stop=(fb == 63)),
                                        waits=w, inc=CT)
                        ct_w2blk[par] = CT.v
                    ct_f2 = CT.v
                    for ts in range(4):
                        for oc2 in range(2):
                            bi = ts * 2 + oc2
                            bank = PS[bi] if bi < 7 else PS7
                            cv_out2 = em.ins('vector', lambda e: e.tensor_tensor(
                                out2_s[:, ts * 2048 + dmh * 1024 + oc2 * 512:
                                       ts * 2048 + dmh * 1024 + (oc2 + 1) * 512],
                                bank[:],
                                b2b_s[:, dmh * 1024 + oc2 * 512:
                                      dmh * 1024 + (oc2 + 1) * 512],
                                ALU.add), waits=[(CT, ct_f2), (CV, cv_b2b),
                                                 (CV, cv_final)], inc=CV)

                for ts in range(4):
                    l = sg2 * 4 + ts
                    base = ts * 2048
                    lqx = em.ins('sync', lambda e: e.dma_start(
                        out=xb_s[:, 0:2048], in_=xmS[l * 128:(l + 1) * 128, :]),
                        waits=[(LG, lg_xm_all), (CA, xb_free[0]),
                               (CV, cv_final)], inc=LQ, inc_n=16)
                    xb_free[0] = 0
                    em.ins('vector', lambda e: e.reduce_sum(
                        stat_s[:, 3:4], out2_s[:, base:base + 2048], axis=AX.X),
                        waits=[(CV, cv_out2)], inc=CV)
                    em.fence('vector', CV)
                    em.ins('vector', lambda e: e.tensor_scalar(
                        stat_s[:, 0:1], stat_s[:, 3:4], 1.0 / 2048, None, ALU.mult),
                        inc=CV)
                    caq = em.ins('scalar', lambda e: e.activation(
                        sqd_s[:], out2_s[:, base:base + 2048], AF.Square,
                        accum_out=stat_s[:, 4:5]),
                        waits=[(CV, cv_out2), (CV, sumsq_free)], inc=CA)
                    cv1 = em.ins('vector', lambda e: e.tensor_scalar(
                        stat_s[:, 6:7], stat_s[:, 4:5], 1.0 / 2048, EPS,
                        ALU.mult, ALU.add), waits=[(CA, caq)], inc=CV)
                    sumsq_free = cv1
                    em.ins('vector', lambda e: e.tensor_tensor(
                        stat_s[:, 7:8], stat_s[:, 0:1], stat_s[:, 0:1], ALU.mult),
                        inc=CV)
                    em.fence('vector', CV)
                    cv3 = em.ins('vector', lambda e: e.tensor_tensor(
                        stat_s[:, 6:7], stat_s[:, 6:7], stat_s[:, 7:8], ALU.subtract),
                        inc=CV)
                    em.ins('scalar', lambda e: e.activation(
                        stat_s[:, 8:9], stat_s[:, 6:7], AF.Ln), waits=[(CV, cv3)],
                        inc=CA)
                    ca2 = em.ins('scalar', lambda e: e.activation(
                        stat_s[:, 1:2], stat_s[:, 8:9], AF.Exp, scale=-0.5), inc=CA)
                    cv4 = em.ins('vector', lambda e: e.tensor_scalar(
                        stat_s[:, 2:3], stat_s[:, 0:1], stat_s[:, 1:2], -1.0,
                        ALU.mult, ALU.mult), waits=[(CA, ca2)], inc=CV)
                    em.fence('vector', CV)
                    em.ins('vector', lambda e: e.tensor_scalar(
                        out2_s[:, base:base + 2048], out2_s[:, base:base + 2048],
                        stat_s[:, 1:2], stat_s[:, 2:3], ALU.mult, ALU.add),
                        waits=[(CV, cv4)], inc=CV)
                    cv_final = em.ins('vector', lambda e: e.tensor_tensor(
                        out2_s[:, base:base + 2048], out2_s[:, base:base + 2048],
                        xb_s[:, 0:2048], ALU.add), waits=[(LQ, lqx)], inc=CV)
                    lq_out_last = em.ins('sync', lambda e: e.dma_start(
                        out=out[l * 128:(l + 1) * 128, :],
                        in_=out2_s[:, base:base + 2048]),
                        waits=[(CV, cv_final)], inc=LQ, inc_n=16)

            if debug:
                em.ins('gpsimd', lambda e: e.dma_start(out=d_attnP, in_=attnP),
                       waits=[(LQ, lq_out_last)], inc=LG, inc_n=16)
                em.ins('gpsimd', lambda e: e.dma_start(out=d_attnR, in_=attnR),
                       inc=LG, inc_n=16)
            em.wait('sync', LQ, lq_out_last)
            em.wait('gpsimd', LG, LG.v)
            em.wait('scalar', LS, LS.v)

        @blk.sync
        def _(eng):
            program(Em('sync', eng))

        @blk.gpsimd
        def _(eng):
            program(Em('gpsimd', eng))

        @blk.vector
        def _(eng):
            program(Em('vector', eng))

        @blk.scalar
        def _(eng):
            program(Em('scalar', eng))

        @blk.tensor
        def _(eng):
            program(Em('tensor', eng))

    return nc


def make_in_maps(X, Wq, Wk, Wv, Wg, Wd, Wo, w1, b1, w2, b2):
    in_maps = []
    for c in range(8):
        b, n = c // 4, c % 4
        own_rows = np.concatenate(
            [np.arange(g * 2048 + n * 512, g * 2048 + (n + 1) * 512)
             for g in range(4)])
        in_maps.append({
            "x": np.ascontiguousarray(X[b]),
            "xown": np.ascontiguousarray(X[b][own_rows]),
            "wq_g": np.ascontiguousarray(Wq[:, n * 512:(n + 1) * 512]),
            "wk_h": np.ascontiguousarray(Wk[:, n * 128:(n + 1) * 128]),
            "wv_h": np.ascontiguousarray(Wv[:, n * 128:(n + 1) * 128]),
            "wq_full": Wq,
            "wv_full": Wv,
            "wg_n": np.ascontiguousarray(Wg[:, n:n + 1]),
            "wd_n": np.ascontiguousarray(Wd[:, n:n + 1]),
            "wo_r": np.ascontiguousarray(Wo[n * 512:(n + 1) * 512, :]),
            "w1": w1,
            "w2": w2,
            "b1r": np.ascontiguousarray(b1.reshape(64, 128)),
            "b2row": np.ascontiguousarray(b2.reshape(1, 2048)),
        })
    return in_maps


def assemble(results):
    OUT = np.empty((B, S, D), dtype=np.float32)
    for c in range(8):
        b, n = c // 4, c % 4
        o = results[c]["out"]
        for g in range(4):
            OUT[b, g * 2048 + n * 512:g * 2048 + (n + 1) * 512, :] = \
                o[g * 512:(g + 1) * 512, :]
    return OUT


def kernel(L0_attn_hidden_states, L0_attn_Wq, L0_attn_Wk, L0_attn_Wv,
           L0_attn_Wg, L0_attn_Wd, L0_attn_Wo, L0_ffn_w1, L0_ffn_b1,
           L0_ffn_w2, L0_ffn_b2):
    f = lambda a: np.ascontiguousarray(np.asarray(a, dtype=np.float32))
    X = f(L0_attn_hidden_states)
    in_maps = make_in_maps(X, f(L0_attn_Wq), f(L0_attn_Wk), f(L0_attn_Wv),
                           f(L0_attn_Wg), f(L0_attn_Wd), f(L0_attn_Wo),
                           f(L0_ffn_w1), f(L0_ffn_b1), f(L0_ffn_w2),
                           f(L0_ffn_b2))
    nc = build_program(debug=False)
    res = run_bass_kernel_spmd(nc, in_maps, core_ids=list(range(8)))
    return assemble(res.results)
